# version 10
# baseline (speedup 1.0000x reference)
import numpy as np
from contextlib import ExitStack

import concourse.bass as bass
import concourse.mybir as mybir
from concourse.bass_utils import run_bass_kernel_spmd

F32 = mybir.dt.float32
BF16 = mybir.dt.bfloat16
I8 = mybir.dt.int8
I32 = mybir.dt.int32
U8 = mybir.dt.uint8
AF = mybir.ActivationFunctionType
ALU = mybir.AluOpType
AX = mybir.AxisListType

NCORES = 8
B = 2
S = 4096
D = 2048
SC = S // NCORES
T = B * SC
KT = D // 128
H0 = slice(0, SC)
H1 = slice(SC, T)
INTER = 8192
NHEAD = 16
HD = 128
CHUNK = 128

ENGINES = ("sync", "gpsimd", "vector", "scalar", "tensor")


class Prog:

    def __init__(self, nc, stack):
        self.nc = nc
        self.stack = stack
        self.q = {e: [] for e in ENGINES}
        self.cnt = {}
        self.sems = {}
        self.wm = {e: {} for e in ENGINES}
        self.gate = []

    def sem(self, name):
        if name not in self.sems:
            self.sems[name] = self.stack.enter_context(self.nc.semaphore(name))
            self.cnt[name] = 0
        return self.sems[name]

    def op(self, eng, fn, after=None, inc=None, inc_by=1, wm_self=True, force_wait=False):
        if inc:
            self.sem(inc)
        waits = []
        byname = {}
        for t in (after or []):
            if t is None:
                continue
            byname[t[0]] = max(byname.get(t[0], 0), t[1])
        for name, v in byname.items():
            self.sem(name)
            if not force_wait and self.wm[eng].get(name, 0) >= v:
                continue
            self.wm[eng][name] = max(self.wm[eng].get(name, 0), v)
            waits.append((name, v))
        for name, v in self.gate:
            if self.wm[eng].get(name, 0) >= v:
                continue
            self.wm[eng][name] = v
            waits.append((name, v))
        tok = None
        if inc is not None:
            self.cnt[inc] += inc_by
            tok = (inc, self.cnt[inc])
            if wm_self:
                self.wm[eng][inc] = max(self.wm[eng].get(inc, 0), self.cnt[inc])
        self.q[eng].append((waits, fn, (inc, inc_by) if inc else None))
        return tok

    def dma(self, eng, out, in_, after=None, sem="d_misc", **kw):
        def fn(e, out=out, in_=in_, kw=kw):
            return e.dma_start(out=out, in_=in_, **kw)
        return self.op(eng, fn, after=after, inc=sem, inc_by=16, wm_self=False)

    def barrier_tok(self, sem_name):
        self.sem(sem_name)
        return (sem_name, self.cnt[sem_name])

    def set_phase_gate(self):
        self.gate = [(name, self.cnt[name]) for name in list(self.cnt)]

    def replay_into(self, block):
        for name in ENGINES:
            dec = getattr(block, name)

            def make(qname):
                def body(engobj):
                    for waits, fn, inc in self.q[qname]:
                        for (s, v) in waits:
                            engobj.wait_ge(self.sems[s], v)
                        inst = fn(engobj) if fn is not None else None
                        if inc is not None:
                            if inst is None:
                                inst = engobj.nop()
                            inst.then_inc(self.sems[inc[0]], inc[1])
                return body
            dec(make(name))


class Consts:
    def __init__(self, nc, stack, maxn=64):
        self.nc = nc
        self.vals_f32 = {}
        self.vals_bf16 = {}
        self.tile_f32 = stack.enter_context(nc.sbuf_tensor("cpool_f32", [128, maxn], F32))
        self.tile_bf16 = stack.enter_context(nc.sbuf_tensor("cpool_bf16", [128, maxn], BF16))
        self.maxn = maxn

    def f32(self, v):
        v = float(v)
        if v not in self.vals_f32:
            i = len(self.vals_f32)
            assert i < self.maxn
            self.vals_f32[v] = i
            self.nc.const_aps.register(F32, v, self.tile_f32[:, i:i + 1])
        return v

    def bf16(self, v):
        v = float(v)
        if v not in self.vals_bf16:
            i = len(self.vals_bf16)
            assert i < self.maxn
            self.vals_bf16[v] = i
            self.nc.const_aps.register(BF16, v, self.tile_bf16[:, i:i + 1])
        return v

    def host_arrays(self):
        a32 = np.zeros((128, self.maxn), np.float32)
        for v, i in self.vals_f32.items():
            a32[:, i] = v
        a16 = np.zeros((128, self.maxn), dtype="bfloat16")
        for v, i in self.vals_bf16.items():
            a16[:, i] = v
        return a32, a16


class Arena:

    def __init__(self, nc, stack, size_bytes, name="arena"):
        self.nc = nc
        self.size = size_bytes
        self.handle = stack.enter_context(nc.sbuf_tensor(name, [128, size_bytes], U8))
        self.base = nc.lookup_mloc(self.handle).addr
        self._n = 0

    def at(self, off, shape, dtype, name=None):
        per_part = int(np.prod(shape[1:])) * mybir.dt.size(dtype)
        assert off % 32 == 0, (off, name)
        assert off + per_part <= self.size, (off, per_part, self.size, name)
        if name is None:
            name = f"ar{self._n}"
            self._n += 1
        h = self.nc.virtual_sbuf_tensor(name, shape, dtype)
        self.nc._tensor(h.name, shape, dtype, type="SB")
        mloc = self.nc.lookup_mloc(h)
        mloc.allocated = True
        mloc.addr = self.base + off
        return h



def emit_p1(nc, stack, prog, C, A, DR):
    xh_gate = DR.get("_xh_gate") or []
    x_slab = DR["x_slab"]
    w_in = DR["w_in"]
    wg = DR["wg"]
    w_out = DR["w_out"]
    bias_in = DR["bias_in"]
    bias_out = DR["bias_out"]
    bias_g = DR["bias_g"]
    arp = DR["arp"]
    aip = DR["aip"]
    identf = DR["identf"]
    cpool32 = DR["cpool32"]
    cpool16 = DR["cpool16"]
    mask8 = DR["mask8"]
    xT_out = DR["xT_out"]
    xn_out = DR["xn_out"]
    s0_out = DR["s0_out"]
    var_out = DR["var_out"]
    cc_in = DR["cc_in"]
    cc_out = DR["cc_out"]

    cur = [0]

    def al(shape, dtype, name=None):
        per = int(np.prod(shape[1:])) * mybir.dt.size(dtype)
        off = cur[0]
        cur[0] += (per + 31) // 32 * 32
        return A.at(off, shape, dtype, name)

    onesT = al([128, T], F32, "onesT")
    zeroT = al([128, T], F32, "zeroT")
    tscr = al([128, T], F32, "tscr")
    sq = al([128, T], F32, "sq")
    ident = al([128, 128], F32, "ident")
    binT = al([128, KT], F32, "binT")
    boutT = al([128, KT], F32, "boutT")
    bgT = al([128, 16], F32, "bgT")
    arpT = al([128, 8], F32, "arpT")
    aipT = al([128, 8], F32, "aipT")
    neg8sp = al([128, 8], F32, "neg8sp")
    neg16sp = al([128, 8], F32, "neg16sp")
    maskT = al([128, 8], F32, "maskT")
    sp_a = al([128, 8], F32, "sp_a")
    sp_b = al([128, 8], F32, "sp_b")
    ones_col = al([128, 1], F32, "ones_col")
    ones_row = al([1, 128], F32, "ones_row")
    sumb = al([128, 64], F32, "sumb")
    gat = al([128, 512], F32, "gat")
    Hst = al([128, 32], F32, "Hst")
    negHi = al([128, 16], F32, "negHi")
    w_a = al([128, 64], F32, "w_a")
    scr4 = al([128, 64], F32, "scr4")
    W0 = (cur[0] + 63) // 64 * 64

    XN = W0
    XH = W0 + 32 * 1024
    GA = W0 + 64 * 1024
    GX = GA + 16 * 1024
    BIG = W0 + 96 * 1024
    AUX = W0 + 160 * 1024
    WEND = W0 + 176 * 1024
    assert WEND <= 200 * 1024, WEND

    xn = [A.at(XN + k * 2048, [128, T], BF16, f"xn{k}") for k in range(KT)]
    xh = [A.at(XH + k * 2048, [128, T], BF16, f"xh{k}") for k in range(KT)]
    ga = [A.at(GA + i * 2048, [128, T], BF16, f"ga{i}") for i in range(8)]
    gx = [A.at(GX + i * 2048, [128, T], BF16, f"gx{i}") for i in range(8)]
    xT = [A.at(BIG + k * 4096, [128, T], F32, f"xT{k}") for k in range(KT)]
    x_tm = [A.at(AUX + i * 8192, [128, D], F32, f"x_tm{i}") for i in range(2)]
    stat_sb = A.at(GA, [1, 2 * T], F32, "stat_sb")
    mrow = A.at(GA + 8192, [1, T], F32, "mrow")
    varrow = A.at(GA + 12288, [1, T], F32, "varrow")
    rstdrow = A.at(GA + 16384, [1, T], F32, "rstdrow")
    nmr = A.at(GA + 20480, [1, T], F32, "nmr")
    Brstd = A.at(GA + 24576, [128, T], F32, "Brstd")
    Bneg = A.at(GA + 28672, [128, T], F32, "Bneg")
    win_sb = [A.at(BIG + k * 4096, [128, D], BF16, f"win{k}") for k in range(KT)]
    wg_sb = A.at(AUX, [128, 4096], BF16, "wg_sb")
    sc = [A.at(XN + i * 4096, [128, T], F32, f"sc{i}") for i in range(8)] + \
         [A.at(AUX + i * 4096, [128, T], F32, f"sc{8 + i}") for i in range(2)]
    sci = A.at(AUX + 8192, [128, T], I32, "sci")
    hr = [A.at(BIG + i * 2048, [128, T], BF16, f"hr{i}") for i in range(8)]
    hi = [A.at(BIG + 16384 + i * 2048, [128, T], BF16, f"hi{i}") for i in range(8)]
    Pr = [A.at(BIG + 32768 + i * 2048, [128, T], BF16, f"Pr{i}") for i in range(8)]
    Pi = [A.at(BIG + 49152 + i * 2048, [128, T], BF16, f"Pi{i}") for i in range(8)]
    wo_sb = [A.at((XN + k * 4096) if k < 8 else (GA + (k - 8) * 4096), [128, D], BF16, f"wo{k}")
             for k in range(KT)]
    s0 = [A.at(XH + k * 2048, [128, T], BF16, f"s0_{k}") for k in range(KT)]

    tok_cp = prog.dma("sync", C.tile_f32[:], cpool32, sem="d_pre")
    prog.dma("sync", C.tile_bf16[:], cpool16, sem="d_pre")
    prog.dma("sync", ident[:], identf, sem="d_pre")
    prog.dma("sync", binT[:], bias_in, sem="d_pre")
    prog.dma("sync", boutT[:], bias_out, sem="d_pre")
    prog.dma("sync", bgT[:], bias_g, sem="d_pre")
    prog.dma("sync", arpT[:], arp, sem="d_pre")
    prog.dma("sync", aipT[:], aip, sem="d_pre")
    tok_pre = prog.dma("sync", maskT[:], mask8, sem="d_pre")

    tok_ones = prog.op("vector", lambda e: e.memset(ones_col[:], 1.0), inc="v")
    prog.op("vector", lambda e: e.memset(ones_row[:], 1.0), inc="v")
    prog.op("vector", lambda e: e.memset(onesT[:], 1.0), inc="v")
    tok_zero = prog.op("vector", lambda e: e.memset(zeroT[:], 0.0), inc="v")

    u0 = prog.op("scalar", lambda e: e.activation(sp_a[:], arpT[:], AF.Abs), after=[tok_pre, tok_cp], inc="s")
    u1 = prog.op("scalar", lambda e: e.activation(sp_a[:], sp_a[:], AF.Exp, scale=C.f32(-1.0)),
                 after=[u0], inc="s", force_wait=True)
    t1 = prog.op("scalar", lambda e: e.activation(sp_a[:], sp_a[:], AF.Ln, bias=C.f32(1.0)),
                 after=[u1], inc="s", force_wait=True)
    t2 = prog.op("vector", lambda e: e.tensor_relu(sp_b[:], arpT[:]), after=[tok_pre], inc="v")
    t3 = prog.op("vector", lambda e: e.tensor_add(sp_a[:], sp_a[:], sp_b[:]), after=[t1, t2], inc="v",
                 force_wait=True)
    prog.op("vector", lambda e: e.tensor_scalar(neg8sp[:], sp_a[:], -8.0, None, ALU.mult), after=[t3], inc="v",
             force_wait=True)
    tok_sp = prog.op("vector", lambda e: e.tensor_scalar(neg16sp[:], sp_a[:], -16.0, None, ALU.mult),
                     after=[t3], inc="v", force_wait=True)

    es_pstr = ExitStack()
    ps_trs = [es_pstr.enter_context(nc.psum_tensor(f"ps_tr{i}", [128, 512], F32)) for i in range(2)]
    xtm_free = [None, None]
    ev_tok = [None, None]
    tok_tr_last = None
    for p in range(4):
        toks = [prog.dma("sync", x_tm[j][:], x_slab[128 * (2 * p + j):128 * (2 * p + j + 1), :],
                         after=[xtm_free[j]], sem=f"d_x{j}") for j in range(2)]
        for k in range(KT):
            par = (p * KT + k) % 2
            mm = None
            for j in range(2):
                def f(e, k=k, j=j, par=par):
                    return nc.tensor.transpose(ps_trs[par][:, 128 * j:128 * (j + 1)],
                                               x_tm[j][:, 128 * k:128 * (k + 1)], ident[:])
                aft = [toks[j], tok_pre, ev_tok[par]]
                mm = prog.op("tensor", f, after=aft, inc="t" if j == 1 else None)
            if k == KT - 1:
                xtm_free[0] = xtm_free[1] = mm
            tok_tr_last = mm
            def fe(e, k=k, p=p, par=par):
                return e.tensor_copy(xT[k][:, 256 * p:256 * (p + 1)], ps_trs[par][:, 0:256])
            ev_tok[par] = prog.op("vector", fe, after=[mm], inc="v")
    tok_xT_done = [ev_tok[0], ev_tok[1]]

    es_psst = ExitStack()
    ps_st = [es_psst.enter_context(nc.psum_tensor(f"ps_st{i}", [128, 512], F32)) for i in range(4)]
    last_sum = {}
    for c in range(2):
        mm = None
        for k in range(KT):
            def f(e, k=k, c=c):
                return nc.tensor.matmul(ps_st[c][0:1, :], ones_col[:], xT[k][:, 512 * c:512 * (c + 1)],
                                        start=(k == 0), stop=(k == KT - 1))
            mm = prog.op("tensor", f, after=tok_xT_done + [tok_ones] if k == 0 else None,
                         inc="t" if k == KT - 1 else None)
        last_sum[c] = mm
    sq_last_mm = None
    sq_final = {}
    for k in range(KT):
        def f(e, k=k):
            return e.activation(sq[:], xT[k][:], AF.Square)
        tk = prog.op("scalar", f, after=tok_xT_done + ([sq_last_mm] if sq_last_mm else []), inc="s")
        for c in range(2):
            def f2(e, k=k, c=c):
                return nc.tensor.matmul(ps_st[2 + c][0:1, :], ones_col[:], sq[:, 512 * c:512 * (c + 1)],
                                        start=(k == 0), stop=(k == KT - 1))
            sq_last_mm = prog.op("tensor", f2, after=[tk], inc="t")
            if k == KT - 1:
                sq_final[c] = sq_last_mm

    ev = []
    ev.append(prog.op("vector", lambda e: e.tensor_copy(stat_sb[0:1, 0:512], ps_st[0][0:1, :]),
                      after=[last_sum[0]], inc="v"))
    ev.append(prog.op("vector", lambda e: e.tensor_copy(stat_sb[0:1, 512:1024], ps_st[1][0:1, :]),
                      after=[last_sum[1]], inc="v"))
    ev.append(prog.op("vector", lambda e: e.tensor_copy(stat_sb[0:1, T:T + 512], ps_st[2][0:1, :]),
                      after=[sq_final[0]], inc="v"))
    ev.append(prog.op("vector", lambda e: e.tensor_copy(stat_sb[0:1, T + 512:2 * T], ps_st[3][0:1, :]),
                      after=[sq_final[1]], inc="v"))

    t1 = prog.op("vector", lambda e: e.tensor_scalar(mrow[:], stat_sb[0:1, 0:T], 1.0 / D, None, ALU.mult),
                 after=ev[0:2], inc="v")
    t2 = prog.op("vector", lambda e: e.tensor_mul(varrow[:], mrow[:], mrow[:]), after=[t1], inc="v")
    t3 = prog.op("vector", lambda e: e.scalar_tensor_tensor(varrow[:], stat_sb[0:1, T:2 * T], 1.0 / D,
                                                            varrow[:], ALU.mult, ALU.subtract),
                 after=[t2] + ev[2:4], inc="v")
    t4 = prog.op("scalar", lambda e: e.activation(rstdrow[:], varrow[:], AF.Ln, bias=C.f32(1e-5)),
                 after=[t3], inc="s")
    t5 = prog.op("scalar", lambda e: e.activation(rstdrow[:], rstdrow[:], AF.Exp, scale=C.f32(-0.5)),
                 after=[t4], inc="s")
    t6 = prog.op("vector", lambda e: e.tensor_mul(nmr[:], mrow[:], rstdrow[:]), after=[t5], inc="v")
    t7 = prog.op("vector", lambda e: e.tensor_scalar(nmr[:], nmr[:], -1.0, None, ALU.mult), after=[t6], inc="v")
    tok_var = prog.dma("sync", var_out, varrow[:], after=[t3], sem="d_st")

    bc_toks = []
    for i, (row, dst, tok) in enumerate(((rstdrow, Brstd, t5), (nmr, Bneg, t7))):
        for c in range(2):
            ps = ps_st[2 * i + c]
            def f(e, row=row, c=c, ps=ps):
                return nc.tensor.matmul(ps[:], ones_row[:], row[0:1, 512 * c:512 * (c + 1)],
                                        start=True, stop=True)
            mmtok = prog.op("tensor", f, after=[tok] + ev, inc="t")
            def fe(e, dst=dst, c=c, ps=ps):
                return e.tensor_copy(dst[:, 512 * c:512 * (c + 1)], ps[:])
            bc_toks.append(prog.op("vector", fe, after=[mmtok], inc="v"))

    tok_xn = [None] * KT
    for k in range(KT):
        a1 = prog.op("vector", lambda e, k=k: e.tensor_mul(tscr[:], xT[k][:], Brstd[:]),
                     after=bc_toks, inc="v")
        tok_xn[k] = prog.op("vector", lambda e, k=k: e.tensor_add(xn[k][:], tscr[:], Bneg[:]),
                            after=[a1], inc="v")
        prog.dma("sync", xT_out[:, T * k:T * (k + 1)], xT[k][:], after=[tok_xn[k]], sem="d_st")
        prog.dma("sync", xn_out[:, T * k:T * (k + 1)], xn[k][:], after=[tok_xn[k]], sem="d_st")

    es_psst.close()
    es_pstr.close()
    tok_xfree = prog.barrier_tok("d_st")

    tok_win = [prog.dma("gpsimd", win_sb[k][:], w_in[128 * k:128 * (k + 1), :],
                        after=[tok_xfree, tok_tr_last], sem="d_w") for k in range(KT)]
    es_psmm = ExitStack()
    ps_mm = [es_psmm.enter_context(nc.psum_tensor(f"ps_mm{i}", [128, 512], F32)) for i in range(8)]
    ps_free = [None] * 8

    for c in range(2):
        for grp in range(2):
            mm_tok = [None] * 8
            for k in range(KT):
                for j in range(8):
                    n = grp * 8 + j
                    def f(e, k=k, j=j, n=n, c=c):
                        return nc.tensor.matmul(ps_mm[j][:], win_sb[k][:, 128 * n:128 * (n + 1)],
                                                xn[k][:, 512 * c:512 * (c + 1)],
                                                start=(k == 0), stop=(k == KT - 1))
                    aft = [tok_win[KT - 1], tok_xn[KT - 1], ps_free[j]] if k == 0 else None
                    mm_tok[j] = prog.op("tensor", f, after=aft, inc="t" if k == KT - 1 else None)
            for j in range(8):
                n = grp * 8 + j
                def fe(e, j=j, n=n, c=c):
                    return e.activation(xh[n][:, 512 * c:512 * (c + 1)], ps_mm[j][:], AF.Identity,
                                        bias=binT[:, n:n + 1])
                ps_free[j] = prog.op("scalar", fe, after=[mm_tok[j]] + xh_gate, inc="s")
    tok_xh = dict((n, ps_free[n % 8]) for n in range(KT))

    tok_wg = None
    for g in range(2):
        for b8 in range(8):
            for sub in range(2):
                col = (g * 16 + b8 * 2 + sub) * 128
                tok_wg = prog.dma("gpsimd", wg_sb[:, col:col + 128],
                                  wg[g, b8, 128 * sub:128 * (sub + 1), :],
                                  after=[tok_xfree], sem="d_w")
    tok_ga = [None] * 8
    tok_gx = [None] * 8
    for c in range(2):
        for g in range(2):
            for b8 in range(8):
                pj = b8
                mm = None
                for sub in range(2):
                    col = (g * 16 + b8 * 2 + sub) * 128
                    def f(e, b8=b8, sub=sub, col=col, c=c, pj=pj):
                        return nc.tensor.matmul(ps_mm[pj][:], wg_sb[:, col:col + 128],
                                                xh[2 * b8 + sub][:, 512 * c:512 * (c + 1)],
                                                start=(sub == 0), stop=(sub == 1))
                    aft = [tok_wg, tok_xh[2 * b8 + 1], ps_free[pj]] if sub == 0 else None
                    mm = prog.op("tensor", f, after=aft, inc="t" if sub == 1 else None)
                dst, dtok = (gx, tok_gx) if g == 0 else (ga, tok_ga)
                def fe(e, b8=b8, g=g, c=c, dst=dst, pj=pj):
                    return e.activation(dst[b8][:, 512 * c:512 * (c + 1)], ps_mm[pj][:], AF.Sigmoid,
                                        bias=bgT[:, g * 8 + b8:g * 8 + b8 + 1])
                ps_free[pj] = prog.op("scalar", fe, after=[mm], inc="s")
                dtok[b8] = ps_free[pj]
    tok_lastwinmm = prog.barrier_tok("t")

    INV2PI = 1.0 / (2 * np.pi)
    scan_out_toks = []
    prev_tile_last_v = prog.barrier_tok("v")
    first_gate = [tok_lastwinmm]
    for i in range(8):
        mag, msq, phi, Phi, w1, w2, snt, cst, ur, ui = sc
        ta = prog.op("scalar", lambda e, i=i: e.activation(mag[:], ga[i][:], AF.Exp, scale=neg8sp[:, i:i + 1]),
                     after=[tok_ga[i], tok_sp, prev_tile_last_v] + first_gate, inc="s")
        tb = prog.op("scalar", lambda e, i=i: e.activation(msq[:], ga[i][:], AF.Exp, scale=neg16sp[:, i:i + 1]),
                     inc="s")
        tc = prog.op("scalar", lambda e: e.activation(w1[:], msq[:], AF.Ln, scale=C.f32(-1.0), bias=C.f32(1.0)),
                     after=[tb], inc="s")
        tnrm = prog.op("scalar", lambda e: e.activation(w1[:], w1[:], AF.Exp, scale=C.f32(0.5)), after=[tc], inc="s")
        tphi = prog.op("vector", lambda e, i=i: e.tensor_scalar(phi[:], ga[i][:], aipT[:, i:i + 1], None, ALU.mult),
                       after=[tok_ga[i], tok_pre, prev_tile_last_v] + first_gate, inc="v")
        tPhi0 = prog.op("vector", lambda e: e.tensor_tensor_scan(Phi[:, H0], onesT[:, H0], phi[:, H0], 0.0,
                                                                 ALU.mult, ALU.add), after=[tphi, tok_ones], inc="v")
        tPhi = prog.op("vector", lambda e: e.tensor_tensor_scan(Phi[:, H1], onesT[:, H1], phi[:, H1], 0.0,
                                                                ALU.mult, ALU.add), after=[tPhi0], inc="v")
        r1 = prog.op("vector", lambda e: e.tensor_scalar(sci[:], Phi[:], INV2PI, None, ALU.mult),
                     after=[tPhi], inc="v")
        r2 = prog.op("vector", lambda e: e.tensor_copy(w2[:], sci[:]), after=[r1], inc="v")
        r3 = prog.op("vector", lambda e: e.scalar_tensor_tensor(w2[:], w2[:], -2 * np.pi, Phi[:],
                                                                ALU.mult, ALU.add), after=[r2], inc="v")
        tsin = prog.op("scalar", lambda e: e.activation(snt[:], w2[:], AF.Sin), after=[r3], inc="s")
        r4 = prog.op("vector", lambda e: e.tensor_scalar(sci[:], Phi[:], INV2PI, 0.25, ALU.mult, ALU.add),
                     after=[r3], inc="v")
        r5 = prog.op("vector", lambda e: e.tensor_copy(w2[:], sci[:]), after=[r4, tsin], inc="v")
        r6 = prog.op("vector", lambda e: e.scalar_tensor_tensor(w2[:], w2[:], -2 * np.pi, Phi[:],
                                                                ALU.mult, ALU.add), after=[r5], inc="v")
        tcos = prog.op("scalar", lambda e: e.activation(cst[:], w2[:], AF.Sin, bias=C.f32(np.pi / 2)),
                       after=[r6], inc="s")
        tgxn = prog.op("vector", lambda e, i=i: e.tensor_mul(phi[:], gx[i][:], w1[:]),
                       after=[tnrm, tPhi, tok_gx[i]], inc="v")
        txr = prog.op("vector", lambda e, i=i: e.tensor_mul(w2[:], phi[:], xh[i][:]),
                      after=[tgxn, r6, tcos], inc="v")
        txi = prog.op("vector", lambda e, i=i: e.tensor_mul(w1[:], phi[:], xh[8 + i][:]),
                      after=[tgxn], inc="v")
        q1 = prog.op("vector", lambda e: e.tensor_mul(ur[:], cst[:], w2[:]), after=[txr, tcos], inc="v")
        q2 = prog.op("vector", lambda e: e.tensor_mul(phi[:], snt[:], w1[:]), after=[txi, tsin], inc="v")
        q3 = prog.op("vector", lambda e: e.tensor_add(ur[:], ur[:], phi[:]), after=[q1, q2], inc="v")
        q4 = prog.op("vector", lambda e: e.tensor_mul(ui[:], cst[:], w1[:]), after=[q2], inc="v")
        q5 = prog.op("vector", lambda e: e.tensor_mul(phi[:], snt[:], w2[:]), after=[q4, q1], inc="v")
        q6 = prog.op("vector", lambda e: e.tensor_sub(ui[:], ui[:], phi[:]), after=[q5], inc="v")
        s1 = prog.op("vector", lambda e: e.tensor_tensor_scan(w2[:, H0], mag[:, H0], ur[:, H0], 0.0,
                                                              ALU.mult, ALU.add), after=[q3, q6, ta], inc="v")
        s2 = prog.op("vector", lambda e: e.tensor_tensor_scan(w2[:, H1], mag[:, H1], ur[:, H1], 0.0,
                                                              ALU.mult, ALU.add), after=[s1], inc="v")
        s3 = prog.op("vector", lambda e: e.tensor_tensor_scan(w1[:, H0], mag[:, H0], ui[:, H0], 0.0,
                                                              ALU.mult, ALU.add), after=[s2], inc="v")
        s4 = prog.op("vector", lambda e: e.tensor_tensor_scan(w1[:, H1], mag[:, H1], ui[:, H1], 0.0,
                                                              ALU.mult, ALU.add), after=[s3], inc="v")
        s5 = prog.op("vector", lambda e: e.tensor_tensor_scan(phi[:, H0], mag[:, H0], zeroT[:, H0], 1.0,
                                                              ALU.mult, ALU.add), after=[s4, tok_zero], inc="v")
        s6 = prog.op("vector", lambda e: e.tensor_tensor_scan(phi[:, H1], mag[:, H1], zeroT[:, H1], 1.0,
                                                              ALU.mult, ALU.add), after=[s5], inc="v")
        z1 = prog.op("vector", lambda e: e.tensor_mul(ur[:], cst[:], w2[:]), after=[s4], inc="v")
        z2 = prog.op("vector", lambda e: e.tensor_mul(ui[:], snt[:], w1[:]), after=[s4], inc="v")
        z3 = prog.op("vector", lambda e, i=i: e.tensor_sub(hr[i][:], ur[:], ui[:]), after=[z1, z2], inc="v")
        z4 = prog.op("vector", lambda e: e.tensor_mul(ur[:], cst[:], w1[:]), after=[z2, z3], inc="v")
        z5 = prog.op("vector", lambda e: e.tensor_mul(ui[:], snt[:], w2[:]), after=[z1, z4], inc="v")
        z6 = prog.op("vector", lambda e, i=i: e.tensor_add(hi[i][:], ur[:], ui[:]), after=[z4, z5], inc="v")
        z7 = prog.op("vector", lambda e, i=i: e.tensor_mul(Pr[i][:], cst[:], phi[:]), after=[s6], inc="v")
        z8 = prog.op("vector", lambda e, i=i: e.tensor_mul(Pi[i][:], snt[:], phi[:]), after=[s6], inc="v")
        z9 = prog.op("vector", lambda e, i=i: e.tensor_copy(sumb[:, 2 * i:2 * i + 2],
                                                            hr[i][:, SC - 1::SC]), after=[z3], inc="v")
        z10 = prog.op("vector", lambda e, i=i: e.tensor_copy(sumb[:, 16 + 2 * i:16 + 2 * i + 2],
                                                             hi[i][:, SC - 1::SC]), after=[z6], inc="v")
        z11 = prog.op("vector", lambda e, i=i: e.tensor_copy(sumb[:, 32 + 2 * i:32 + 2 * i + 2],
                                                             Pr[i][:, SC - 1::SC]), after=[z7], inc="v")
        z12 = prog.op("vector", lambda e, i=i: e.tensor_copy(sumb[:, 48 + 2 * i:48 + 2 * i + 2],
                                                             Pi[i][:, SC - 1::SC]), after=[z8], inc="v")
        scan_out_toks += [z9, z10, z11, z12]
        prev_tile_last_v = z12
        first_gate = []

    tok_sum = prog.dma("sync", cc_in, sumb[:], after=scan_out_toks, sem="d_cc")
    tok_cc = prog.op("gpsimd", lambda e: e.collective_compute(
        "AllGather", ALU.bypass, replica_groups=[list(range(NCORES))],
        ins=[cc_in.opt()], outs=[cc_out.opt()]), after=[tok_sum], inc="cc")
    tok_gat = [prog.dma("sync", gat[:, 64 * r:64 * (r + 1)], cc_out[128 * r:128 * (r + 1), :],
                        after=[tok_cc], sem="d_cc") for r in range(NCORES)]

    tprev = prog.op("vector", lambda e: e.memset(Hst[:], 0.0), after=tok_gat + [tok_sum], inc="v")
    for r in range(NCORES):
        g0 = 64 * r
        m_ap = maskT[:, r:r + 1]
        ops = [
            lambda e, g0=g0, m_ap=m_ap: e.tensor_scalar(
                w_a[:, 0:16], gat[:, g0:g0 + 16], m_ap, None, ALU.mult),
            lambda e, g0=g0, m_ap=m_ap: e.tensor_scalar(
                w_a[:, 16:32], gat[:, g0 + 16:g0 + 32], m_ap, None, ALU.mult),
            lambda e, g0=g0, m_ap=m_ap: e.tensor_scalar(
                w_a[:, 32:48], gat[:, g0 + 32:g0 + 48], m_ap, 1.0, ALU.mult, ALU.add),
            lambda e, m_ap=m_ap: e.tensor_scalar(
                w_a[:, 32:48], w_a[:, 32:48], m_ap, None, ALU.subtract),
            lambda e, g0=g0, m_ap=m_ap: e.tensor_scalar(
                w_a[:, 48:64], gat[:, g0 + 48:g0 + 64], m_ap, None, ALU.mult),
            lambda e: e.tensor_mul(scr4[:, 0:16], w_a[:, 32:48], Hst[:, 0:16]),
            lambda e: e.tensor_mul(scr4[:, 16:32], w_a[:, 48:64], Hst[:, 16:32]),
            lambda e: e.tensor_mul(scr4[:, 32:48], w_a[:, 32:48], Hst[:, 16:32]),
            lambda e: e.tensor_mul(scr4[:, 48:64], w_a[:, 48:64], Hst[:, 0:16]),
            lambda e: e.tensor_sub(Hst[:, 0:16], scr4[:, 0:16], scr4[:, 16:32]),
            lambda e: e.tensor_add(Hst[:, 0:16], Hst[:, 0:16], w_a[:, 0:16]),
            lambda e: e.tensor_add(Hst[:, 16:32], scr4[:, 32:48], scr4[:, 48:64]),
            lambda e: e.tensor_add(Hst[:, 16:32], Hst[:, 16:32], w_a[:, 16:32]),
        ]
        for fn in ops:
            tprev = prog.op("vector", fn, after=[tprev], inc="v", force_wait=True)
    tok_H = tprev
    tok_nH = prog.op("vector", lambda e: e.tensor_scalar(negHi[:], Hst[:, 16:32], -1.0, None, ALU.mult),
                     after=[tok_H], inc="v", force_wait=True)

    tok_fix = None
    for i in range(8):
        for b in range(2):
            hb = [H0, H1][b]
            col = 2 * i + b
            prog.op("vector", lambda e, i=i, hb=hb, col=col: e.scalar_tensor_tensor(
                hr[i][:, hb], Pr[i][:, hb], Hst[:, col:col + 1], hr[i][:, hb], ALU.mult, ALU.add),
                after=[tok_nH], inc="v", force_wait=True)
            prog.op("vector", lambda e, i=i, hb=hb, col=col: e.scalar_tensor_tensor(
                hr[i][:, hb], Pi[i][:, hb], negHi[:, col:col + 1], hr[i][:, hb], ALU.mult, ALU.add), inc="v")
            prog.op("vector", lambda e, i=i, hb=hb, col=col: e.scalar_tensor_tensor(
                hi[i][:, hb], Pr[i][:, hb], Hst[:, 16 + col:16 + col + 1], hi[i][:, hb], ALU.mult, ALU.add),
                inc="v")
            tok_fix = prog.op("vector", lambda e, i=i, hb=hb, col=col: e.scalar_tensor_tensor(
                hi[i][:, hb], Pi[i][:, hb], Hst[:, col:col + 1], hi[i][:, hb], ALU.mult, ALU.add), inc="v")

    gate_wo = prog.barrier_tok("v")
    tok_wo = [prog.dma("gpsimd", wo_sb[k][:], w_out[128 * k:128 * (k + 1), :],
                       after=[tok_lastwinmm, gate_wo], sem="d_w") for k in range(KT)]
    hcat = hr + hi
    evac_tok = {}
    for c in range(2):
        for grp in range(2):
            mm_tok = [None] * 8
            for k in range(KT):
                for j in range(8):
                    n = grp * 8 + j
                    def f(e, k=k, j=j, n=n, c=c):
                        return nc.tensor.matmul(ps_mm[j][:], wo_sb[k][:, 128 * n:128 * (n + 1)],
                                                hcat[k][:, 512 * c:512 * (c + 1)],
                                                start=(k == 0), stop=(k == KT - 1))
                    aft = [tok_wo[KT - 1], tok_fix, ps_free[j]] if k == 0 else None
                    mm_tok[j] = prog.op("tensor", f, after=aft, inc="t" if k == KT - 1 else None)
            for j in range(8):
                n = grp * 8 + j
                def fe(e, j=j, n=n, c=c):
                    return e.activation(s0[n][:, 512 * c:512 * (c + 1)], ps_mm[j][:], AF.Identity,
                                        bias=boutT[:, n:n + 1])
                ps_free[j] = prog.op("scalar", fe, after=[mm_tok[j]], inc="s")
                evac_tok[(c, n)] = ps_free[j]
    for k in range(KT):
        prog.dma("sync", s0_out[:, T * k:T * (k + 1)], s0[k][:], after=[evac_tok[(1, k)]], sem="d_st")

    prog.op("sync", None, after=[prog.barrier_tok("d_st")])

    es_psmm.close()


def build_p1():
    nc = bass.Bass("TRN2", target_bir_lowering=False)
    stack = ExitStack()

    x_slab = nc.dram_tensor("x_slab", [T, D], F32, kind="ExternalInput").ap()
    w_in = nc.dram_tensor("w_in", [D, D], F32, kind="ExternalInput").ap()
    wg = nc.dram_tensor("wg", [2, 8, 256, 128], F32, kind="ExternalInput").ap()
    w_out = nc.dram_tensor("w_out", [D, D], F32, kind="ExternalInput").ap()
    bias_in = nc.dram_tensor("bias_in", [128, KT], F32, kind="ExternalInput").ap()
    bias_out = nc.dram_tensor("bias_out", [128, KT], F32, kind="ExternalInput").ap()
    bias_g = nc.dram_tensor("bias_g", [128, 16], F32, kind="ExternalInput").ap()
    arp = nc.dram_tensor("arp", [128, 8], F32, kind="ExternalInput").ap()
    aip = nc.dram_tensor("aip", [128, 8], F32, kind="ExternalInput").ap()
    identf = nc.dram_tensor("identf", [128, 128], F32, kind="ExternalInput").ap()
    cpool32 = nc.dram_tensor("cpool32", [128, 64], F32, kind="ExternalInput").ap()
    cpool16 = nc.dram_tensor("cpool16", [128, 64], BF16, kind="ExternalInput").ap()
    mask8 = nc.dram_tensor("mask8", [128, 8], F32, kind="ExternalInput").ap()

    xT_out = nc.dram_tensor("xT_out", [128, KT * T], F32, kind="ExternalOutput").ap()
    xn_out = nc.dram_tensor("xn_out", [128, KT * T], BF16, kind="ExternalOutput").ap()
    s0_out = nc.dram_tensor("s0_out", [128, KT * T], BF16, kind="ExternalOutput").ap()
    var_out = nc.dram_tensor("var_out", [1, T], F32, kind="ExternalOutput").ap()

    cc_in = nc.dram_tensor("cc_in", [128, 64], F32, kind="Internal").ap()
    cc_out = nc.dram_tensor("cc_out", [NCORES * 128, 64], F32, kind="Internal",
                            addr_space="Shared").ap()

    DR = {
        "x_slab": x_slab,
        "w_in": w_in,
        "wg": wg,
        "w_out": w_out,
        "bias_in": bias_in,
        "bias_out": bias_out,
        "bias_g": bias_g,
        "arp": arp,
        "aip": aip,
        "identf": identf,
        "cpool32": cpool32,
        "cpool16": cpool16,
        "mask8": mask8,
        "xT_out": xT_out,
        "xn_out": xn_out,
        "s0_out": s0_out,
        "var_out": var_out,
        "cc_in": cc_in,
        "cc_out": cc_out,
    }
    with stack:
        prog = Prog(nc, stack)
        C = Consts(nc, stack)
        A = Arena(nc, stack, 206 * 1024)
        emit_p1(nc, stack, prog, C, A, DR)
        block = stack.enter_context(nc.Block())
        prog.replay_into(block)
        consts = C.host_arrays()
    return nc, consts

def _slab(x_full, c):
    return np.ascontiguousarray(
        np.concatenate([x_full[0, c * SC:(c + 1) * SC], x_full[1, c * SC:(c + 1) * SC]], axis=0))


def _tilevec(v):
    v = np.asarray(v, np.float32)
    n = v.shape[0] // 128
    return np.ascontiguousarray(v.reshape(n, 128).T)


def p1_inputs(inputs, consts):
    x = np.asarray(inputs["L0_attn_x"], np.float32)
    ident = np.eye(128, dtype=np.float32)
    wgarr = np.ascontiguousarray(np.stack([np.asarray(inputs["L0_attn_Wg_x"], np.float32),
                                           np.asarray(inputs["L0_attn_Wg_a"], np.float32)], axis=0))
    bias_g = np.ascontiguousarray(np.concatenate(
        [_tilevec(inputs["L0_attn_bg_x"]), _tilevec(inputs["L0_attn_bg_a"])], axis=1))
    a32, a16 = consts
    base = {
        "w_in": np.ascontiguousarray(np.asarray(inputs["L0_attn_W_in"], np.float32)),
        "wg": wgarr,
        "w_out": np.ascontiguousarray(np.asarray(inputs["L0_attn_W_out"], np.float32)),
        "bias_in": _tilevec(inputs["L0_attn_b_in"]),
        "bias_out": _tilevec(inputs["L0_attn_b_out"]),
        "bias_g": bias_g,
        "arp": _tilevec(inputs["L0_attn_a_real_param"]),
        "aip": _tilevec(inputs["L0_attn_a_imag_param"]),
        "identf": ident,
        "cpool32": a32,
        "cpool16": a16,
    }
    per_core = []
    for c in range(NCORES):
        d = dict(base)
        d["x_slab"] = _slab(x, c)
        mask = np.zeros((128, 8), np.float32)
        mask[:, :c] = 1.0
        d["mask8"] = mask
        per_core.append(d)
    return per_core


def run_p1(inputs):
    nc, consts = build_p1()
    per_core = p1_inputs(inputs, consts)
    return run_bass_kernel_spmd(nc, per_core, core_ids=list(range(NCORES)))





def emit_p2(nc, stack, prog, C, A, DR):
    NT = INTER // 128
    NCH = 16
    KCH = 8
    xn_in = DR["xn_in"]
    xT_in = DR["xT_in"]
    s0_in = DR["s0_in"]
    var_in = DR["var_in"]
    w1t = DR["w1t"]
    w2t = DR["w2t"]
    ccw_out = DR["ccw_out"]
    ho_sw = DR["ho_sw"]
    tok_ccw = DR.get("_tok_ccw")
    identf = DR["identf"]
    cpool32 = DR["cpool32"]
    cpool16 = DR["cpool16"]
    x1_out = DR["x1_out"]
    hmid = DR["hmid"]
    cc2_in = DR["cc2_in"]
    cc2_out = DR["cc2_out"]
    cur = [0]

    def al(shape, dtype, name=None):
        per = int(np.prod(shape[1:])) * mybir.dt.size(dtype)
        off = cur[0]
        cur[0] += (per + 31) // 32 * 32
        return A.at(off, shape, dtype, name)

    ident = al([128, 128], F32, "ident")
    ones_col = al([128, 1], F32, "ones_col")
    ones_row = al([1, 128], F32, "ones_row")
    swsum = al([1, 16], F32, "swsum")
    swacc = al([128, 2], F32, "swacc")
    swrow = al([128, 4], F32, "swrow")
    redcol = al([128, 1], F32, "redcol")
    OFF = {}

    def mark(name):
        OFF[name] = cur[0]
    mark("varr")
    varr = al([1, T], F32, "varr")
    rms1 = al([1, T], F32, "rms1")
    mark("rowa")
    rowa = al([1, T], F32, "rowa")
    mark("rowb")
    rowb = al([1, T], F32, "rowb")
    mark("c1r")
    c1r = al([1, T], F32, "c1r")
    r1r = al([1, T], F32, "r1r")
    c2r = al([1, T], F32, "c2r")
    r2r = al([1, T], F32, "r2r")
    mabs_r = al([1, T], F32, "mabs_r")
    s2row = al([1, T], F32, "s2row")
    C1B = al([128, T], F32, "C1B")
    R1B = al([128, T], F32, "R1B")
    C2B = al([128, T], F32, "C2B")
    R2B = al([128, T], F32, "R2B")
    mark("macc")
    macc = al([128, T], F32, "macc")
    tsc1 = al([128, T], F32, "tsc1")
    tsc2 = al([128, T], F32, "tsc2")
    tsc3 = al([128, 512], F32, "tsc3")
    i8s = al([128, T], I8, "i8s")
    wclip = al([128, T], F32, "wclip")
    sldA = A.at(OFF["macc"], [128, T], F32, "sldA")
    sldB = A.at(OFF["rowa"], [128, T], BF16, "sldB")
    hst = [A.at(OFF["rowb"] + i * 1024, [128, 512], BF16, f"hst{i}") for i in range(2)]
    hld = [A.at(OFF["c1r"] + i * 2048, [128, T], BF16, f"hld{i}") for i in range(2)]
    W0 = (cur[0] + 63) // 64 * 64

    XN = W0
    XQ = W0 + 32 * 1024
    WB = W0 + 64 * 1024
    HB = W0 + 112 * 1024
    WEND = W0 + 128 * 1024
    assert WEND <= 206 * 1024, (WEND, W0)

    xn = [A.at(XN + k * 2048, [128, T], BF16, f"xn{k}") for k in range(KT)]
    xq2 = [A.at(XN + k * 2048, [128, T], BF16, f"xq2_{k}") for k in range(16)]
    gt = [A.at(WB + k * 2048, [128, T], BF16, f"gt{k}") for k in range(KT)]
    xq = [A.at(XQ + k * 2048, [128, T], BF16, f"xq{k}") for k in range(KT)]
    y2a = [A.at(XQ + k * 2048, [128, T], BF16, f"y2a{k}") for k in range(KT)]
    w1qb = [A.at(WB + b * 16 * 1024, [128, 8192], BF16, f"w1qb{b}") for b in range(2)]
    w2qb = [A.at(WB + b * 32 * 1024, [128, 16384], BF16, f"w2qb{b}") for b in range(2)]

    tok_cp = prog.dma("sync", C.tile_f32[:], cpool32, sem="d_pre")
    prog.dma("sync", C.tile_bf16[:], cpool16, sem="d_pre")
    prog.dma("sync", ident[:], identf, sem="d_pre")
    tok_var = prog.dma("sync", varr[:], var_in, sem="d_pre")
    tok_ones = prog.op("vector", lambda e: e.memset(ones_col[:], 1.0), inc="v")
    prog.op("vector", lambda e: e.memset(ones_row[:], 1.0), inc="v")

    es_ps = ExitStack()
    ps = [es_ps.enter_context(nc.psum_tensor(f"q2ps{i}", [128, 512], F32)) for i in range(8)]
    psfree = [None] * 8

    tok_swl = prog.dma("sync", swsum[:], ho_sw, sem="d_cc")
    tok_swb = tok_swl

    tok_xn = [prog.dma("sync", xn[k][:], xn_in[:, T * k:T * (k + 1)], sem="d_x") for k in range(KT)]
    pm = prog.op("vector", lambda e: e.tensor_tensor(macc[:], xn[0][:], xn[1][:], ALU.max),
                 after=[tok_xn[0], tok_xn[1]], inc="v")
    nm = prog.op("vector", lambda e: e.tensor_tensor(tsc2[:], xn[0][:], xn[1][:], ALU.min),
                 after=[tok_xn[0], tok_xn[1]], inc="v")
    for k in range(2, KT):
        pm = prog.op("vector", lambda e, k=k: e.tensor_tensor(macc[:], macc[:], xn[k][:], ALU.max),
                     after=[tok_xn[k], pm], inc="v", force_wait=True)
        nm = prog.op("vector", lambda e, k=k: e.tensor_tensor(tsc2[:], tsc2[:], xn[k][:], ALU.min),
                     after=[tok_xn[k], nm], inc="v", force_wait=True)
    nm2 = prog.op("vector", lambda e: e.tensor_scalar(tsc2[:], tsc2[:], -1.0, None, ALU.mult),
                  after=[nm], inc="v", force_wait=True)
    pm = prog.op("vector", lambda e: e.tensor_tensor(macc[:], macc[:], tsc2[:], ALU.max),
                 after=[pm, nm2], inc="v", force_wait=True)
    prevT = pm
    for ch in range(8):
        mmt = prog.op("tensor", lambda e, ch=ch: nc.tensor.transpose(
            ps[7][:, 0:128], macc[:, 128 * ch:128 * (ch + 1)], ident[:]),
            after=[prevT, tok_cp, psfree[7]], inc="t")
        rd = prog.op("vector", lambda e: e.reduce_max(redcol[:], ps[7][:, 0:128], axis=AX.X),
                     after=[mmt], inc="v")
        mmt2 = prog.op("tensor", lambda e: nc.tensor.transpose(ps[7][0:1, 128:256], redcol[:], ident[:]),
                       after=[rd], inc="t")
        prevT = prog.op("vector", lambda e, ch=ch: e.tensor_copy(mabs_r[0:1, 128 * ch:128 * (ch + 1)],
                                                                 ps[7][0:1, 128:256]), after=[mmt2], inc="v")
        psfree[7] = prevT
    tok_mabs = prevT
    v1 = prog.op("scalar", lambda e: e.activation(rowa[:], varr[:], AF.Ln, bias=C.f32(1e-5)),
                 after=[tok_var, tok_cp], inc="s")
    v2 = prog.op("scalar", lambda e: e.activation(rowb[:], varr[:], AF.Ln), after=[v1], inc="s",
                 force_wait=True)
    v3 = prog.op("vector", lambda e: e.tensor_sub(rms1[:], rowa[:], rowb[:]), after=[v2], inc="v")
    v4 = prog.op("scalar", lambda e: e.activation(rms1[:], rms1[:], AF.Exp, scale=C.f32(0.5)),
                 after=[v3], inc="s")
    q1 = prog.op("vector", lambda e: e.tensor_mul(rowa[:], rms1[:], mabs_r[:]), after=[v4, tok_mabs],
                 inc="v", force_wait=True)
    q2 = prog.op("vector", lambda e: e.tensor_scalar(rowa[:], rowa[:], 1e-5, None, ALU.max),
                 after=[q1], inc="v", force_wait=True)
    q3 = prog.op("vector", lambda e: e.reciprocal(rowb[:], rowa[:]), after=[q2], inc="v", force_wait=True)
    q4 = prog.op("vector", lambda e: e.tensor_mul(c1r[:], rms1[:], rowb[:]), after=[q3], inc="v",
                 force_wait=True)
    q5 = prog.op("vector", lambda e: e.tensor_scalar(c1r[:], c1r[:], 127.0, None, ALU.mult),
                 after=[q4], inc="v", force_wait=True)
    q6 = prog.op("vector", lambda e: e.tensor_scalar(r1r[:], rowa[:], 1.0 / 127.0, None, ALU.mult),
                 after=[q5], inc="v", force_wait=True)
    q7 = prog.op("vector", lambda e: e.tensor_scalar(r1r[:], r1r[:], swsum[0:1, 2:3], None, ALU.mult),
                 after=[q6, tok_swb], inc="v", force_wait=True)
    bc_toks = []
    for (row, dst, tok) in ((c1r, C1B, q5), (r1r, R1B, q7)):
        for cc_ in range(2):
            mmb = prog.op("tensor", lambda e, row=row, cc_=cc_: nc.tensor.matmul(
                ps[6][:], ones_row[:], row[0:1, 512 * cc_:512 * (cc_ + 1)], start=True, stop=True),
                after=[tok, psfree[6]], inc="t")
            tkb = prog.op("vector", lambda e, dst=dst, cc_=cc_: e.tensor_copy(
                dst[:, 512 * cc_:512 * (cc_ + 1)], ps[6][:]), after=[mmb], inc="v")
            psfree[6] = tkb
            bc_toks.append(tkb)
    tok_xq = [None] * KT
    for k in range(KT):
        qa = prog.op("vector", lambda e, k=k: e.tensor_tensor(i8s[:], xn[k][:], C1B[:], ALU.mult),
                     after=bc_toks + [tok_xn[k]], inc="v")
        tok_xq[k] = prog.op("vector", lambda e, k=k: e.tensor_copy(xq[k][:], i8s[:]),
                            after=[qa], inc="v", force_wait=True)

    tok_hacc0 = prog.op("vector", lambda e: e.memset(macc[:], 0.0), after=[tok_mabs], inc="v")
    hacc_tok = tok_hacc0
    sq_started = [False, False]
    sq_tok = {0: None, 1: None}
    w1buf_tok = [None, None]
    w1buf_mm = [None, None]
    last_mm_of_chunk = None
    tsc1_free = None
    tsc2_free = None
    hst_free = [None, None]
    hstore_all = []
    for ch in range(NCH):
        wb = ch % 2
        rnk = ch // 2
        half = ch % 2
        src_ap = ccw_out[128 * rnk:128 * (rnk + 1), 0:16384].rearrange(
            "p (kt x) -> p kt x", kt=KT)[:, :, 512 * half:512 * (half + 1)]
        w1buf_tok[wb] = prog.dma("gpsimd", w1qb[wb][:], src_ap,
                                 after=[w1buf_mm[wb], tok_ccw], sem=f"d_wf{wb}")
        for tc in range(2):
            mmt = [None] * 4
            for k in range(KT):
                for j in range(4):
                    def f(e, k=k, j=j, tc=tc, wb=wb):
                        return nc.tensor.matmul(ps[j][:], w1qb[wb][:, 512 * k + 128 * j:512 * k + 128 * (j + 1)],
                                                xq[k][:, 512 * tc:512 * (tc + 1)],
                                                start=(k == 0), stop=(k == KT - 1))
                    aft = [w1buf_tok[wb]]
                    if k == 0:
                        aft += [tok_xq[KT - 1], psfree[j]]
                    mmt[j] = prog.op("tensor", f, after=aft, inc="t" if k == KT - 1 else None)
            last_mm_of_chunk = mmt[3]
            w1buf_mm[wb] = mmt[3]
            for j in range(4):
                it = ch * 4 + j
                e1 = prog.op("vector", lambda e, j=j, tc=tc: e.tensor_tensor(
                    tsc1[:, 0:512], ps[j][:], R1B[:, 512 * tc:512 * (tc + 1)], ALU.mult),
                    after=[mmt[j], tsc1_free], inc="v")
                psfree[j] = e1
                e2 = prog.op("scalar", lambda e: e.activation(tsc2[:, 0:512], tsc1[:, 0:512], AF.Gelu),
                             after=[e1, tsc2_free], inc="s")
                tsc1_free = e2
                e3 = prog.op("scalar", lambda e: e.activation(tsc2[:, 0:512], tsc2[:, 0:512], AF.Gelu),
                             after=[e2], inc="s", force_wait=True)
                hb = it % 2
                e5 = prog.op("vector", lambda e, hb=hb: e.tensor_copy(hst[hb][:], tsc2[:, 0:512]),
                             after=[e3, hst_free[hb]], inc="v")
                tkst = prog.dma("sync", hmid[:, T * it + 512 * tc:T * it + 512 * (tc + 1)],
                                hst[hb][:], after=[e5], sem="d_h")
                hst_free[hb] = tkst
                hstore_all.append(tkst)
                e6 = prog.op("gpsimd", lambda e: e.tensor_mul(tsc3[:, 0:512], tsc2[:, 0:512],
                                                              tsc2[:, 0:512]), after=[e3], inc="p")
                mmq = prog.op("tensor", lambda e, tc=tc, last=(ch == NCH - 1 and j == 3): nc.tensor.matmul(
                    ps[4 + tc][0:1, :], ones_col[:], tsc3[:, 0:512],
                    start=(not sq_started[tc]), stop=last), after=[e6], inc="t")
                sq_started[tc] = True
                sq_tok[tc] = mmq
                hacc_tok = prog.op("vector", lambda e, tc=tc: e.tensor_tensor(
                    macc[:, 512 * tc:512 * (tc + 1)], macc[:, 512 * tc:512 * (tc + 1)],
                    tsc3[:, 0:512], ALU.max), after=[e3, hacc_tok, e6], inc="v", force_wait=True)
                tsc2_free = hacc_tok

    ev_s2 = []
    ev_s2.append(prog.op("vector", lambda e: e.tensor_copy(s2row[0:1, 0:512], ps[4][0:1, :]),
                         after=[sq_tok[0]], inc="v"))
    ev_s2.append(prog.op("vector", lambda e: e.tensor_copy(s2row[0:1, 512:1024], ps[5][0:1, :]),
                         after=[sq_tok[1]], inc="v"))
    psfree[4] = ev_s2[0]
    psfree[5] = ev_s2[1]
    prevT = hacc_tok
    for ch in range(8):
        mmt = prog.op("tensor", lambda e, ch=ch: nc.tensor.transpose(
            ps[7][:, 0:128], macc[:, 128 * ch:128 * (ch + 1)], ident[:]),
            after=[prevT, psfree[7]], inc="t")
        rd = prog.op("vector", lambda e: e.reduce_max(redcol[:], ps[7][:, 0:128], axis=AX.X),
                     after=[mmt], inc="v")
        mmt2 = prog.op("tensor", lambda e: nc.tensor.transpose(ps[7][0:1, 128:256], redcol[:], ident[:]),
                       after=[rd], inc="t")
        prevT = prog.op("vector", lambda e, ch=ch: e.tensor_copy(mabs_r[0:1, 128 * ch:128 * (ch + 1)],
                                                                 ps[7][0:1, 128:256]), after=[mmt2], inc="v")
        psfree[7] = prevT
    tok_m2a = prevT
    tok_m2b = prog.op("scalar", lambda e: e.activation(mabs_r[:], mabs_r[:], AF.Ln),
                      after=[tok_m2a], inc="s")
    tok_m2 = prog.op("scalar", lambda e: e.activation(mabs_r[:], mabs_r[:], AF.Exp, scale=C.f32(0.5)),
                     after=[tok_m2b], inc="s", force_wait=True)
    g1 = prog.op("scalar", lambda e: e.activation(rowa[:], s2row[:], AF.Ln), after=ev_s2 + [hstore_all[-1]], inc="s")
    g2 = prog.op("scalar", lambda e: e.activation(rowa[:], rowa[:], AF.Exp, scale=C.f32(0.5)),
                 after=[g1], inc="s", force_wait=True)
    g3 = prog.op("vector", lambda e: e.reciprocal(rowb[:], rowa[:]), after=[g2], inc="v")
    g4 = prog.op("vector", lambda e: e.tensor_scalar(rms1[:], rowb[:], float(np.sqrt(INTER)), None,
                                                     ALU.mult), after=[g3], inc="v", force_wait=True)
    q1 = prog.op("vector", lambda e: e.tensor_mul(rowa[:], rms1[:], mabs_r[:]), after=[g4, tok_m2],
                 inc="v", force_wait=True)
    q2 = prog.op("vector", lambda e: e.tensor_scalar(rowa[:], rowa[:], 1e-5, None, ALU.max),
                 after=[q1], inc="v", force_wait=True)
    q3 = prog.op("vector", lambda e: e.reciprocal(rowb[:], rowa[:]), after=[q2], inc="v", force_wait=True)
    q4 = prog.op("vector", lambda e: e.tensor_mul(c2r[:], rms1[:], rowb[:]), after=[q3], inc="v",
                 force_wait=True)
    q5 = prog.op("vector", lambda e: e.tensor_scalar(c2r[:], c2r[:], 127.0, None, ALU.mult),
                 after=[q4], inc="v", force_wait=True)
    q6 = prog.op("vector", lambda e: e.tensor_scalar(r2r[:], rowa[:], 1.0 / 127.0, None, ALU.mult),
                 after=[q5], inc="v", force_wait=True)
    q7 = prog.op("vector", lambda e: e.tensor_scalar(r2r[:], r2r[:], swsum[0:1, 3:4], None, ALU.mult),
                 after=[q6], inc="v", force_wait=True)
    bc2_toks = []
    for (row, dst, tok) in ((c2r, C2B, q5), (r2r, R2B, q7)):
        for cc_ in range(2):
            mmb = prog.op("tensor", lambda e, row=row, cc_=cc_: nc.tensor.matmul(
                ps[6][:], ones_row[:], row[0:1, 512 * cc_:512 * (cc_ + 1)], start=True, stop=True),
                after=[tok, psfree[6]], inc="t")
            tkb = prog.op("vector", lambda e, dst=dst, cc_=cc_: e.tensor_copy(
                dst[:, 512 * cc_:512 * (cc_ + 1)], ps[6][:]), after=[mmb], inc="v")
            psfree[6] = tkb
            bc2_toks.append(tkb)

    w2buf_tok = [None, None]
    w2buf_mm = [None, None]
    h_load = [None, None]
    hld_cons = [None, None]
    xq2_tok = [None] * 16
    xq2_mm = [None, None]
    y2a_tok = [None] * KT
    for kc in range(KCH):
        wb = kc % 2
        src_ap = ccw_out[128 * kc:128 * (kc + 1), 16384:].rearrange("p (kt x) -> p kt x", kt=8)
        w2buf_tok[wb] = prog.dma("gpsimd", w2qb[wb][:], src_ap,
                                 after=[w2buf_mm[wb], tok_ccw] +
                                       ([last_mm_of_chunk] if kc < 2 else []), sem=f"d_w2{wb}")
        for kk in range(8):
            kglob = kc * 8 + kk
            hb = kk % 2
            h_load[hb] = prog.dma("sync", hld[hb][:], hmid[:, T * kglob:T * (kglob + 1)],
                                  after=[h_load[hb], hld_cons[hb]] +
                                        ([hstore_all[-1]] if kc == 0 and kk == 0 else []) +
                                        ([xq2_mm[wb]] if kk < 2 else []),
                                  sem=f"d_h{hb}")
            qh = prog.op("vector", lambda e, hb=hb: e.tensor_tensor(i8s[:], hld[hb][:], C2B[:], ALU.mult),
                         after=[h_load[hb], xq2_mm[wb] if kk == 0 else None] + bc2_toks, inc="v")
            xq2_tok[wb * 8 + kk] = prog.op("vector", lambda e, wb=wb, kk=kk: e.tensor_copy(
                xq2[wb * 8 + kk][:], i8s[:]), after=[qh], inc="v", force_wait=True)
            hld_cons[hb] = xq2_tok[wb * 8 + kk]
        for tc in range(2):
            for ng in range(2):
                mmt = [None] * 8
                for kk in range(8):
                    for j in range(8):
                        n = ng * 8 + j
                        def f(e, kk=kk, j=j, n=n, tc=tc, wb=wb):
                            return nc.tensor.matmul(ps[j][:], w2qb[wb][:, 2048 * kk + 128 * n:2048 * kk + 128 * (n + 1)],
                                                    xq2[wb * 8 + kk][:, 512 * tc:512 * (tc + 1)],
                                                    start=(kk == 0), stop=(kk == 7))
                        aft = [w2buf_tok[wb], xq2_tok[wb * 8 + kk]]
                        if kk == 0:
                            aft += [psfree[j]]
                        mmt[j] = prog.op("tensor", f, after=aft, inc="t" if kk == 7 else None)
                for j in range(8):
                    n = ng * 8 + j
                    if kc == 0:
                        tk = prog.op("vector", lambda e, j=j, n=n, tc=tc: e.tensor_copy(
                            y2a[n][:, 512 * tc:512 * (tc + 1)], ps[j][:]),
                            after=[mmt[j], tok_xq[n]], inc="v")
                    else:
                        tk = prog.op("vector", lambda e, j=j, n=n, tc=tc: e.tensor_tensor(
                            y2a[n][:, 512 * tc:512 * (tc + 1)], y2a[n][:, 512 * tc:512 * (tc + 1)],
                            ps[j][:], ALU.add), after=[mmt[j], y2a_tok[n]], inc="v", force_wait=True)
                    psfree[j] = tk
                    y2a_tok[n] = tk
        w2buf_mm[wb] = mmt[7]
        xq2_mm[wb] = mmt[7]

    sA_tok = None
    sB_tok = None
    for k in range(KT):
        l1 = prog.dma("sync", sldA[:], xT_in[:, T * k:T * (k + 1)], after=[sA_tok, tok_m2], sem="d_rA")
        l2 = prog.dma("sync", sldB[:], s0_in[:, T * k:T * (k + 1)], after=[sB_tok] + bc2_toks, sem="d_rB")
        m1 = prog.op("vector", lambda e, k=k: e.tensor_tensor(tsc1[:], y2a[k][:], R2B[:], ALU.mult),
                     after=[y2a_tok[k], tsc1_free, w2buf_mm[0], w2buf_mm[1]] + bc2_toks, inc="v")
        tsc1_free = m1
        m2 = prog.op("scalar", lambda e, k=k: e.activation(gt[k][:], tsc1[:], AF.Silu),
                     after=[m1, tok_xq[k]], inc="s")
        m3 = prog.op("vector", lambda e, k=k: e.tensor_tensor(sldB[:], sldB[:], gt[k][:], ALU.mult),
                     after=[m2, l2], inc="v")
        sB_tok = m3
        m4 = prog.op("vector", lambda e: e.tensor_tensor(sldA[:], sldA[:], sldB[:], ALU.add),
                     after=[m3, l1], inc="v", force_wait=True)
        sA_tok = prog.dma("sync", x1_out[:, T * k:T * (k + 1)], sldA[:], after=[m4], sem="d_st")

    prog.op("sync", None, after=[prog.barrier_tok("d_st")])

    es_ps.close()


def build_p2():
    nc = bass.Bass("TRN2", target_bir_lowering=False)
    stack = ExitStack()
    NT = INTER // 128
    NCH = 16
    KCH = 8

    xn_in = nc.dram_tensor("xn_in", [128, KT * T], BF16, kind="ExternalInput").ap()
    xT_in = nc.dram_tensor("xT_in", [128, KT * T], F32, kind="ExternalInput").ap()
    s0_in = nc.dram_tensor("s0_in", [128, KT * T], BF16, kind="ExternalInput").ap()
    var_in = nc.dram_tensor("var_in", [1, T], F32, kind="ExternalInput").ap()
    w1t = nc.dram_tensor("w1t", [D, INTER], F32, kind="ExternalInput").ap()
    w2t = nc.dram_tensor("w2t", [INTER, D], F32, kind="ExternalInput").ap()
    wslice1 = nc.dram_tensor("wslice1", [D, 1024], F32, kind="ExternalInput").ap()
    wslice2 = nc.dram_tensor("wslice2", [1024, D], F32, kind="ExternalInput").ap()
    identf = nc.dram_tensor("identf", [128, 128], F32, kind="ExternalInput").ap()
    cpool32 = nc.dram_tensor("cpool32", [128, 64], F32, kind="ExternalInput").ap()
    cpool16 = nc.dram_tensor("cpool16", [128, 64], BF16, kind="ExternalInput").ap()

    x1_out = nc.dram_tensor("x1_out", [128, KT * T], F32, kind="ExternalOutput").ap()

    hmid = nc.dram_tensor("hmid", [128, NT * T], BF16, kind="Internal").ap()
    cc2_in = nc.dram_tensor("cc2_in", [1, 16], F32, kind="Internal").ap()
    cc2_out = nc.dram_tensor("cc2_out", [1, 16], F32, kind="Internal").ap()

    DR = {
        "xn_in": xn_in,
        "xT_in": xT_in,
        "s0_in": s0_in,
        "var_in": var_in,
        "w1t": w1t,
        "w2t": w2t,
        "wslice1": wslice1,
        "wslice2": wslice2,
        "identf": identf,
        "cpool32": cpool32,
        "cpool16": cpool16,
        "x1_out": x1_out,
        "hmid": hmid,
        "cc2_in": cc2_in,
        "cc2_out": cc2_out,
    }
    with stack:
        prog = Prog(nc, stack)
        C = Consts(nc, stack)
        A = Arena(nc, stack, 206 * 1024)
        emit_p2(nc, stack, prog, C, A, DR)
        block = stack.enter_context(nc.Block())
        prog.replay_into(block)
        consts = C.host_arrays()
    return nc, consts

def p2_inputs(inputs, consts, p1_results):
    a32, a16 = consts
    w1 = np.asarray(inputs["L0_ffn_w1"], np.float32)
    w2 = np.asarray(inputs["L0_ffn_w2"], np.float32)
    w1t_h = np.ascontiguousarray(w1.T)
    w2t_h = np.ascontiguousarray(w2.T)
    ident = np.eye(128, dtype=np.float32)
    per_core = []
    for c in range(NCORES):
        r = p1_results[c]
        per_core.append({
            "xn_in": np.asarray(r["xn_out"]),
            "xT_in": np.asarray(r["xT_out"]),
            "s0_in": np.asarray(r["s0_out"]),
            "var_in": np.asarray(r["var_out"]),
            "w1t": w1t_h,
            "w2t": w2t_h,
            "wslice1": np.ascontiguousarray(w1t_h[:, 1024 * c:1024 * (c + 1)]),
            "wslice2": np.ascontiguousarray(w2t_h[1024 * c:1024 * (c + 1), :]),
            "identf": ident,
            "cpool32": a32,
            "cpool16": a16,
        })
    return per_core


def run_p2(inputs, p1_results):
    nc, consts = build_p2()
    per_core = p2_inputs(inputs, consts, p1_results)
    return run_bass_kernel_spmd(nc, per_core, core_ids=list(range(NCORES)))


LAMBDA = np.exp(-np.array([(2.0 ** (-0.5)) ** (i + 1) for i in range(NHEAD)], dtype=np.float64))


def emit_p3(nc, stack, prog, C, A, DR):
    NCHK = 4
    x1_in = DR["x1_in"]
    wqkvt = DR["wqkvt"]
    woutt = DR["woutt"]
    bqk = DR["bqk"]
    bvrow_d = DR["bvrow_d"]
    bo = DR["bo"]
    identf = DR["identf"]
    identb_d = DR["identb_d"]
    dmaskt_d = DR["dmaskt_d"]
    dq1_d = DR["dq1_d"]
    lamc_d = DR["lamc_d"]
    dkcol_d = DR["dkcol_d"]
    dqfix_d = DR["dqfix_d"]
    wlam_d = DR["wlam_d"]
    cpool32 = DR["cpool32"]
    cpool16 = DR["cpool16"]
    xn2_out = DR["xn2_out"]
    s1_out = DR["s1_out"]
    cc3_in = DR["cc3_in"]
    cc3_out = DR["cc3_out"]
    cur = [0]

    def al(shape, dtype, name=None):
        per = int(np.prod(shape[1:])) * mybir.dt.size(dtype)
        off = cur[0]
        cur[0] += (per + 31) // 32 * 32
        return A.at(off, shape, dtype, name)

    ident = al([128, 128], F32, "ident")
    identb = al([128, 128], BF16, "identb")
    dmaskt = al([128, NHEAD * 128], BF16, "dmaskt")
    dq1 = al([128, NHEAD * 128], BF16, "dq1")
    lamc = al([128, NHEAD * 128], BF16, "lamc")
    dkcol = al([128, NHEAD], F32, "dkcol")
    bqkT = al([128, 32], F32, "bqkT")
    boT = al([128, KT], F32, "boT")
    Bv = al([128, D], BF16, "Bv")
    ones_col = al([128, 1], F32, "ones_col")
    ones_row = al([1, 128], F32, "ones_row")
    qsb = al([128, 128], BF16, "qsb")
    qsfb = al([128, 512], BF16, "qsfb")
    P_END = cur[0]
    assert P_END <= 22 * 1024, P_END

    X1R = 22 * 1024
    XN2R = X1R + 64 * 1024
    VBR = XN2R + 32 * 1024
    ATR = VBR + 32 * 1024
    SSR = ATR + 32 * 1024
    GKR = SSR + 16 * 1024
    assert GKR + 8 * 1024 <= 206 * 1024

    x1 = [A.at(X1R + k * 4096, [128, T], F32, f"x1_{k}") for k in range(KT)]
    qk = [A.at(X1R + n * 2048, [128, T], BF16, f"qk{n}") for n in range(32)]
    wout_sb = [A.at(X1R + k * 4096, [128, D], BF16, f"wout{k}") for k in range(KT)]
    xn2 = [A.at(XN2R + k * 2048, [128, T], BF16, f"xn2_{k}") for k in range(KT)]
    stmp = A.at(XN2R + 0, [128, 1024], F32, "stmp")
    ssb = A.at(XN2R + 4096, [128, NHEAD * 128], BF16, "ssb")
    wlam = A.at(XN2R, [128, NCORES * D], BF16, "wlam")
    s1 = [A.at(XN2R + k * 2048, [128, T], BF16, f"s1_{k}") for k in range(KT)]
    stat_sb = A.at(VBR, [1, 2 * T], F32, "stat_sb")
    mrow = A.at(VBR + 8192, [1, T], F32, "mrow")
    rstdrow = A.at(VBR + 12288, [1, T], F32, "rstdrow")
    nmr = A.at(VBR + 16384, [1, T], F32, "nmr")
    sq = A.at(VBR + 20480, [128, T], F32, "sq")
    bvrow = A.at(VBR + 24576, [1, D], F32, "bvrow")
    vb = [A.at(VBR + i * 4096, [128, D], BF16, f"vb{i}") for i in range(8)]
    dqfix = A.at(VBR, [128, NHEAD * SC], BF16, "dqfix")
    pacc = A.at(VBR + 16 * 1024, [128, D], F32, "pacc")
    ptmp = A.at(VBR + 24 * 1024, [128, 1024], F32, "ptmp")
    Brstd = A.at(ATR, [128, T], F32, "Brstd")
    Bneg = A.at(ATR + 4096, [128, T], F32, "Bneg")
    tscp = A.at(ATR + 8192, [128, T], F32, "tscp")
    attn = [A.at(ATR + h * 2048, [128, T], BF16, f"attn{h}") for h in range(NHEAD)]
    S_sb = [A.at(SSR + b * 4096, [128, D], BF16, f"S{b}") for b in range(2)]
    sin_sb = [A.at(SSR + 8192 + b * 4096, [128, D], BF16, f"sin{b}") for b in range(2)]
    wbuf = [A.at(SSR, [128, 8192], BF16, "wbuf0"), A.at(ATR, [128, 8192], BF16, "wbuf1")]
    kdb = A.at(GKR, [128, NHEAD * 128], BF16, "kdb")
    gat3 = A.at(GKR, [128, 2 * D], BF16, "gat3")

    tok_cp = prog.dma("sync", C.tile_f32[:], cpool32, sem="d_pre")
    prog.dma("sync", C.tile_bf16[:], cpool16, sem="d_pre")
    prog.dma("sync", ident[:], identf, sem="d_pre")
    prog.dma("sync", identb[:], identb_d, sem="d_pre")
    prog.dma("sync", dmaskt[:], dmaskt_d, sem="d_pre")
    prog.dma("sync", dq1[:], dq1_d, sem="d_pre")
    prog.dma("sync", lamc[:], lamc_d, sem="d_pre")
    prog.dma("sync", dkcol[:], dkcol_d, sem="d_pre")
    prog.dma("sync", bqkT[:], bqk, sem="d_pre")
    prog.dma("sync", boT[:], bo, sem="d_pre")
    tok_bv = prog.dma("sync", bvrow[:], bvrow_d, sem="d_pre")
    tok_pre = tok_bv
    tok_ones = prog.op("vector", lambda e: e.memset(ones_col[:], 1.0), inc="v")
    prog.op("vector", lambda e: e.memset(ones_row[:], 1.0), inc="v")

    es_ps = ExitStack()
    psA = [es_ps.enter_context(nc.psum_tensor(f"q3psA{i}", [128, 512], F32)) for i in range(4)]
    psB = es_ps.enter_context(nc.psum_tensor("q3psB", [128, 1024], F32))
    psC = es_ps.enter_context(nc.psum_tensor("q3psC", [128, 1024], F32))
    psfree = {"A0": None, "A1": None, "A2": None, "A3": None, "B": None, "C": None}

    tok_x1 = [prog.dma("sync", x1[k][:], x1_in[:, T * k:T * (k + 1)], sem="d_x") for k in range(KT)]
    last_sum = {}
    for c in range(2):
        mm = None
        for k in range(KT):
            def f(e, k=k, c=c):
                return nc.tensor.matmul(psA[c][0:1, :], ones_col[:], x1[k][:, 512 * c:512 * (c + 1)],
                                        start=(k == 0), stop=(k == KT - 1))
            mm = prog.op("tensor", f, after=[tok_x1[k], tok_ones], inc="t" if k == KT - 1 else None)
        last_sum[c] = mm
    sq_last = None
    sq_fin = {}
    for k in range(KT):
        tk = prog.op("scalar", lambda e, k=k: e.activation(sq[:], x1[k][:], AF.Square),
                     after=[tok_x1[k], sq_last], inc="s")
        for c in range(2):
            sq_last = prog.op("tensor", lambda e, k=k, c=c: nc.tensor.matmul(
                psA[2 + c][0:1, :], ones_col[:], sq[:, 512 * c:512 * (c + 1)],
                start=(k == 0), stop=(k == KT - 1)), after=[tk], inc="t")
            if k == KT - 1:
                sq_fin[c] = sq_last
    ev = []
    ev.append(prog.op("vector", lambda e: e.tensor_copy(stat_sb[0:1, 0:512], psA[0][0:1, :]),
                      after=[last_sum[0]], inc="v"))
    ev.append(prog.op("vector", lambda e: e.tensor_copy(stat_sb[0:1, 512:1024], psA[1][0:1, :]),
                      after=[last_sum[1]], inc="v"))
    ev.append(prog.op("vector", lambda e: e.tensor_copy(stat_sb[0:1, T:T + 512], psA[2][0:1, :]),
                      after=[sq_fin[0]], inc="v"))
    ev.append(prog.op("vector", lambda e: e.tensor_copy(stat_sb[0:1, T + 512:2 * T], psA[3][0:1, :]),
                      after=[sq_fin[1]], inc="v"))
    for i in range(4):
        psfree[f"A{i}"] = ev[i]
    t1 = prog.op("vector", lambda e: e.tensor_scalar(mrow[:], stat_sb[0:1, 0:T], 1.0 / D, None, ALU.mult),
                 after=ev[0:2], inc="v")
    t2 = prog.op("vector", lambda e: e.tensor_mul(rstdrow[:], mrow[:], mrow[:]), after=[t1], inc="v",
                 force_wait=True)
    t3 = prog.op("vector", lambda e: e.scalar_tensor_tensor(rstdrow[:], stat_sb[0:1, T:2 * T], 1.0 / D,
                                                            rstdrow[:], ALU.mult, ALU.subtract),
                 after=[t2] + ev[2:4], inc="v", force_wait=True)
    t4 = prog.op("scalar", lambda e: e.activation(rstdrow[:], rstdrow[:], AF.Ln, bias=C.f32(1e-5)),
                 after=[t3, tok_cp], inc="s")
    t5 = prog.op("scalar", lambda e: e.activation(rstdrow[:], rstdrow[:], AF.Exp, scale=C.f32(-0.5)),
                 after=[t4], inc="s", force_wait=True)
    t6 = prog.op("vector", lambda e: e.tensor_mul(nmr[:], mrow[:], rstdrow[:]), after=[t5], inc="v")
    t7 = prog.op("vector", lambda e: e.tensor_scalar(nmr[:], nmr[:], -1.0, None, ALU.mult), after=[t6],
                 inc="v", force_wait=True)
    bc_toks = []
    for i, (row, dst, tok) in enumerate(((rstdrow, Brstd, t5), (nmr, Bneg, t7))):
        for c in range(2):
            key = f"A{2 * i + c}"
            mmb = prog.op("tensor", lambda e, row=row, c=c, i=i: nc.tensor.matmul(
                psA[2 * i + c][:], ones_row[:], row[0:1, 512 * c:512 * (c + 1)], start=True, stop=True),
                after=[tok, psfree[key]], inc="t")
            tkb = prog.op("vector", lambda e, dst=dst, c=c, i=i: e.tensor_copy(
                dst[:, 512 * c:512 * (c + 1)], psA[2 * i + c][:]), after=[mmb], inc="v")
            psfree[key] = tkb
            bc_toks.append(tkb)
    tok_xn2 = [None] * KT
    for k in range(KT):
        a1 = prog.op("vector", lambda e, k=k: e.tensor_mul(tscp[:], x1[k][:], Brstd[:]),
                     after=bc_toks, inc="v")
        tok_xn2[k] = prog.op("vector", lambda e, k=k: e.tensor_add(xn2[k][:], tscp[:], Bneg[:]),
                             after=[a1], inc="v", force_wait=True)
        prog.dma("sync", xn2_out[:, T * k:T * (k + 1)], xn2[k][:], after=[tok_xn2[k]], sem="d_st")
    bv_toks = []
    for c4 in range(4):
        key = f"A{c4}"
        mmb = prog.op("tensor", lambda e, c4=c4: nc.tensor.matmul(
            psA[c4][:], ones_row[:], bvrow[0:1, 512 * c4:512 * (c4 + 1)], start=True, stop=True),
            after=[tok_bv, psfree[key]], inc="t")
        tkb = prog.op("vector", lambda e, c4=c4: e.tensor_copy(Bv[:, 512 * c4:512 * (c4 + 1)],
                                                               psA[c4][:]), after=[mmb], inc="v")
        psfree[key] = tkb
        bv_toks.append(tkb)
    tok_ln_done_v = prog.barrier_tok("v")
    tok_ln_done_s = prog.barrier_tok("s")
    tok_xn2_st = prog.barrier_tok("d_st")

    qk_tok = [None] * 32
    wb_tok = [None, None]
    wb_cons = [None, None]
    for grp in range(8):
        par = grp % 2
        n0 = grp * 4
        wb_tok[par] = prog.dma("gpsimd", wbuf[par][:],
                               wqkvt[:, 512 * grp:512 * (grp + 1)].rearrange("(kt p) n -> p kt n", p=128),
                               after=[wb_tok[par], wb_cons[par], tok_ln_done_v, tok_ln_done_s],
                               sem=f"d_wb{par}")
        for c in range(2):
            mmg = [None] * 4
            for k in range(KT):
                for j in range(4):
                    key = f"A{j}"
                    def f(e, k=k, j=j, c=c, par=par):
                        return nc.tensor.matmul(psA[j][:], wbuf[par][:, 512 * k + 128 * j:512 * k + 128 * (j + 1)],
                                                xn2[k][:, 512 * c:512 * (c + 1)],
                                                start=(k == 0), stop=(k == KT - 1))
                    aft = [wb_tok[par], tok_xn2[KT - 1], psfree[key]] if k == 0 else None
                    mmg[j] = prog.op("tensor", f, after=aft, inc="t" if k == KT - 1 else None)
            wb_cons[par] = mmg[3]
            for j in range(4):
                n = n0 + j
                key = f"A{j}"
                def fe(e, j=j, n=n, c=c):
                    return e.activation(qk[n][:, 512 * c:512 * (c + 1)], psA[j][:], AF.Identity,
                                        bias=bqkT[:, n:n + 1])
                psfree[key] = prog.op("scalar", fe, after=[mmg[j]], inc="s")
                qk_tok[n] = psfree[key]

    vb_tok = [None] * 8
    for nch in range(4):
        par = nch % 2
        wb_tok[par] = prog.dma("gpsimd", wbuf[par][:],
                               wqkvt[:, 2 * D + 512 * nch:2 * D + 512 * (nch + 1)].rearrange(
                                   "(kt p) n -> p kt n", p=128),
                               after=[wb_tok[par], wb_cons[par]], sem=f"d_wb{par}")
        for tg in range(2):
            mmg = [None] * 4
            for k in range(KT):
                for j in range(4):
                    ti = tg * 4 + j
                    key = f"A{j}"
                    def f(e, k=k, j=j, ti=ti, par=par):
                        return nc.tensor.matmul(psA[j][:], xn2[k][:, 128 * ti:128 * (ti + 1)],
                                                wbuf[par][:, 512 * k:512 * (k + 1)],
                                                start=(k == 0), stop=(k == KT - 1))
                    aft = [wb_tok[par], psfree[key]] if k == 0 else None
                    mmg[j] = prog.op("tensor", f, after=aft, inc="t" if k == KT - 1 else None)
            wb_cons[par] = mmg[3]
            for j in range(4):
                ti = tg * 4 + j
                key = f"A{j}"
                def fe(e, j=j, ti=ti, nch=nch):
                    return e.tensor_tensor(vb[ti][:, 512 * nch:512 * (nch + 1)], psA[j][:],
                                           Bv[:, 512 * nch:512 * (nch + 1)], ALU.add)
                psfree[key] = prog.op("vector", fe, after=[mmg[j]] + (bv_toks if nch == 0 else []), inc="v")
                vb_tok[ti] = psfree[key]

    zS = [prog.op("vector", lambda e, b=b: e.memset(S_sb[b][:], 0.0),
                  after=[wb_tok[0], wb_tok[1], vb_tok[7]], inc="v") for b in range(2)]
    S_tok = {0: zS[0], 1: zS[1]}
    kd_tok = [None] * NHEAD
    ev_sc = None
    attn_last = None
    for b in range(2):
        for u in range(NCHK):
            cols = slice(512 * b + 128 * u, 512 * b + 128 * (u + 1))
            for sg in range(2):
                sh0 = sg * 8
                sc_mm = None
                for h in range(sh0, sh0 + 8):
                    def f(e, h=h, cols=cols, sh0=sh0):
                        return nc.tensor.matmul(psC[:, 128 * (h - sh0):128 * (h - sh0 + 1)],
                                                qk[16 + h][:, cols], qk[h][:, cols],
                                                start=True, stop=True)
                    aft = [qk_tok[16 + h], qk_tok[h]]
                    if h == sh0:
                        aft += [psfree["C"]]
                    sc_mm = prog.op("tensor", f, after=aft, inc="t" if h == sh0 + 7 else None)
                ev_sc = prog.op("vector", lambda e, sg=sg: e.tensor_tensor(
                    ssb[:, 1024 * sg:1024 * (sg + 1)], psC[:], dmaskt[:, 1024 * sg:1024 * (sg + 1)],
                    ALU.mult), after=[sc_mm, tok_pre, ev_sc], inc="v")
                psfree["C"] = ev_sc
            kt_mm = None
            for h in range(NHEAD):
                def ft(e, h=h, cols=cols):
                    return nc.tensor.transpose(psC[:].bitcast(BF16)[:, 128 * h:128 * (h + 1)],
                                               qk[16 + h][:, cols], identb[:])
                aft = [ev_sc] + ([kd_tok[15]] if h == 0 else [])
                kt_mm = prog.op("tensor", ft, after=aft, inc="t" if h == NHEAD - 1 else None)
            for h in range(NHEAD):
                kd_tok[h] = prog.op("vector", lambda e, h=h: e.tensor_scalar(
                    kdb[:, 128 * h:128 * (h + 1)], psC[:].bitcast(BF16)[:, 128 * h:128 * (h + 1)],
                    dkcol[:, h:h + 1], None, ALU.mult), after=[kt_mm], inc="v")
            psfree["C"] = kd_tok[NHEAD - 1]
            for hg in range(2):
                h0 = hg * 8
                mm_o = None
                q_toks = []
                for h in range(h0, h0 + 8):
                    def fi(e, h=h, b=b, u=u, h0=h0):
                        return nc.tensor.matmul(psB[:, 128 * (h - h0):128 * (h - h0 + 1)],
                                                vb[b * 4 + u][:, 128 * h:128 * (h + 1)],
                                                ssb[:, 128 * h:128 * (h + 1)], start=True, stop=False)
                    aft = [ev_sc, vb_tok[b * 4 + u]]
                    if h == h0:
                        aft += [psfree["B"]]
                    prog.op("tensor", fi, after=aft)
                    tq = prog.op("vector", lambda e, h=h, cols=cols: e.tensor_mul(
                        qsb[:], qk[h][:, cols], dq1[:, 128 * h:128 * (h + 1)]),
                        after=[S_tok[b], q_toks[-1] if q_toks else None], inc="v")
                    def f2(e, h=h, b=b, h0=h0):
                        return nc.tensor.matmul(psB[:, 128 * (h - h0):128 * (h - h0 + 1)],
                                                S_sb[b][:, 128 * h:128 * (h + 1)], qsb[:],
                                                start=False, stop=True)
                    mm_o = prog.op("tensor", f2, after=[tq, S_tok[b]], inc="t")
                    q_toks.append(mm_o)
                ev_at = None
                for h in range(h0, h0 + 8):
                    def fe(e, h=h, cols=cols, h0=h0):
                        return e.tensor_copy(attn[h][:, cols], psB[:, 128 * (h - h0):128 * (h - h0 + 1)])
                    ev_at = prog.op("vector", fe, after=[mm_o], inc="v")
                attn_last = ev_at
                psfree["B"] = ev_at
                mm_kv = None
                for h in range(h0, h0 + 8):
                    def fk(e, h=h, b=b, u=u, h0=h0):
                        return nc.tensor.matmul(psB[:, 128 * (h - h0):128 * (h - h0 + 1)],
                                                kdb[:, 128 * h:128 * (h + 1)],
                                                vb[b * 4 + u][:, 128 * h:128 * (h + 1)],
                                                start=True, stop=True)
                    aft = [kd_tok[h]]
                    if h == h0:
                        aft += [psfree["B"]]
                    mm_kv = prog.op("tensor", fk, after=aft, inc="t" if h == h0 + 7 else None)
                su1 = prog.op("vector", lambda e, b=b, h0=h0: e.tensor_tensor(
                    stmp[:], S_sb[b][:, 128 * h0:128 * h0 + 1024],
                    lamc[:, 128 * h0:128 * h0 + 1024], ALU.mult),
                    after=[mm_kv, S_tok[b], tok_xn2_st], inc="v")
                S_tok[b] = prog.op("vector", lambda e, b=b, h0=h0: e.tensor_tensor(
                    S_sb[b][:, 128 * h0:128 * h0 + 1024], stmp[:], psB[:], ALU.add),
                    after=[su1], inc="v", force_wait=True)
                psfree["B"] = S_tok[b]

    tok_pub = [prog.dma("sync", cc3_in[:, D * b:D * (b + 1)], S_sb[b][:], after=[S_tok[b]], sem="d_cc")
               for b in range(2)]
    tok_cc = prog.op("gpsimd", lambda e: e.collective_compute(
        "AllGather", ALU.bypass, replica_groups=[list(range(NCORES))],
        ins=[cc3_in.opt()], outs=[cc3_out.opt()]), after=tok_pub, inc="cc")
    tok_wlam = prog.dma("sync", wlam[:], wlam_d,
                        after=[prog.barrier_tok("t"), prog.barrier_tok("v"),
                               prog.barrier_tok("d_st")], sem="d_w2")
    tok_dqfix = prog.dma("sync", dqfix[:], dqfix_d,
                         after=[prog.barrier_tok("v"), prog.barrier_tok("t")], sem="d_w2")
    sin_tok = {}
    gat_tok = None
    for b in range(2):
        acc_tok = prog.op("vector", lambda e: e.memset(pacc[:], 0.0),
                          after=[S_tok[0], S_tok[1], prog.barrier_tok("t"),
                                 prog.barrier_tok("v") if b == 0 else None], inc="v")
        for j in range(NCORES):
            gat_tok = prog.dma("sync", gat3[:], cc3_out[128 * j:128 * (j + 1), :],
                               after=[tok_cc, gat_tok, acc_tok], sem=f"d_g{b}")
            for half in range(2):
                m1 = prog.op("vector", lambda e, j=j, b=b, half=half: e.tensor_tensor(
                    ptmp[:], gat3[:, D * b + 1024 * half:D * b + 1024 * (half + 1)],
                    wlam[:, D * j + 1024 * half:D * j + 1024 * (half + 1)], ALU.mult),
                    after=[gat_tok, tok_wlam], inc="v")
                acc_tok = prog.op("vector", lambda e, half=half: e.tensor_tensor(
                    pacc[:, 1024 * half:1024 * (half + 1)], pacc[:, 1024 * half:1024 * (half + 1)],
                    ptmp[:], ALU.add), after=[m1], inc="v", force_wait=True)
        sin_tok[b] = prog.op("vector", lambda e, b=b: e.tensor_copy(sin_sb[b][:], pacc[:]),
                             after=[acc_tok], inc="v", force_wait=True)

    fix_last = None
    qs_tok = None
    mmf = None
    for b in range(2):
        for h in range(NHEAD):
            brange = slice(512 * b, 512 * (b + 1))
            qs_tok = prog.op("vector", lambda e, h=h, brange=brange: e.tensor_mul(
                qsfb[:], qk[h][:, brange], dqfix[:, SC * h:SC * (h + 1)]),
                after=[tok_dqfix, sin_tok[b], qs_tok, mmf], inc="v")
            mmf = prog.op("tensor", lambda e, h=h, b=b: nc.tensor.matmul(
                psA[0][:], sin_sb[b][:, 128 * h:128 * (h + 1)], qsfb[:], start=True, stop=True),
                after=[qs_tok, sin_tok[b], psfree["A0"]], inc="t")
            fix_last = prog.op("vector", lambda e, h=h, brange=brange: e.tensor_tensor(
                attn[h][:, brange], attn[h][:, brange], psA[0][:], ALU.add),
                after=[mmf, attn_last], inc="v")
            psfree["A0"] = fix_last

    wo_tok = [prog.dma("gpsimd", wout_sb[k][:], woutt[128 * k:128 * (k + 1), :],
                       after=[prog.barrier_tok("t"), fix_last], sem="d_w") for k in range(KT)]
    ev_s1 = {}
    for c in range(2):
        for grp in range(4):
            mmg = [None] * 4
            for k in range(KT):
                for j in range(4):
                    n = grp * 4 + j
                    key = f"A{j}"
                    def f(e, k=k, j=j, n=n, c=c):
                        return nc.tensor.matmul(psA[j][:], wout_sb[k][:, 128 * n:128 * (n + 1)],
                                                attn[k][:, 512 * c:512 * (c + 1)],
                                                start=(k == 0), stop=(k == KT - 1))
                    aft = [wo_tok[KT - 1], fix_last, psfree[key]] if k == 0 else None
                    mmg[j] = prog.op("tensor", f, after=aft, inc="t" if k == KT - 1 else None)
            for j in range(4):
                n = grp * 4 + j
                key = f"A{j}"
                def fe(e, j=j, n=n, c=c):
                    return e.activation(s1[n][:, 512 * c:512 * (c + 1)], psA[j][:], AF.Identity,
                                        bias=boT[:, n:n + 1])
                psfree[key] = prog.op("scalar", fe, after=[mmg[j], tok_wlam], inc="s")
                ev_s1[(c, n)] = psfree[key]
    for k in range(KT):
        prog.dma("sync", s1_out[:, T * k:T * (k + 1)], s1[k][:], after=[ev_s1[(1, k)]], sem="d_st")
    prog.op("sync", None, after=[prog.barrier_tok("d_st")])

    es_ps.close()


def build_p3():
    nc = bass.Bass("TRN2", target_bir_lowering=False)
    stack = ExitStack()
    NCHK = 4

    x1_in = nc.dram_tensor("x1_in", [128, KT * T], F32, kind="ExternalInput").ap()
    wqkvt = nc.dram_tensor("wqkvt", [D, 3 * D], F32, kind="ExternalInput").ap()
    woutt = nc.dram_tensor("woutt", [D, D], F32, kind="ExternalInput").ap()
    bqk = nc.dram_tensor("bqk", [128, 32], F32, kind="ExternalInput").ap()
    bvrow_d = nc.dram_tensor("bvrow_d", [1, D], F32, kind="ExternalInput").ap()
    bo = nc.dram_tensor("bo", [128, KT], F32, kind="ExternalInput").ap()
    identf = nc.dram_tensor("identf", [128, 128], F32, kind="ExternalInput").ap()
    identb_d = nc.dram_tensor("identb_d", [128, 128], BF16, kind="ExternalInput").ap()
    dmaskt_d = nc.dram_tensor("dmaskt_d", [128, NHEAD * 128], BF16, kind="ExternalInput").ap()
    dq1_d = nc.dram_tensor("dq1_d", [128, NHEAD * 128], BF16, kind="ExternalInput").ap()
    lamc_d = nc.dram_tensor("lamc_d", [128, NHEAD * 128], BF16, kind="ExternalInput").ap()
    dkcol_d = nc.dram_tensor("dkcol_d", [128, NHEAD], F32, kind="ExternalInput").ap()
    dqfix_d = nc.dram_tensor("dqfix_d", [128, NHEAD * SC], BF16, kind="ExternalInput").ap()
    wlam_d = nc.dram_tensor("wlam_d", [128, NCORES * D], BF16, kind="ExternalInput").ap()
    cpool32 = nc.dram_tensor("cpool32", [128, 64], F32, kind="ExternalInput").ap()
    cpool16 = nc.dram_tensor("cpool16", [128, 64], BF16, kind="ExternalInput").ap()

    xn2_out = nc.dram_tensor("xn2_out", [128, KT * T], BF16, kind="ExternalOutput").ap()
    s1_out = nc.dram_tensor("s1_out", [128, KT * T], BF16, kind="ExternalOutput").ap()

    cc3_in = nc.dram_tensor("cc3_in", [128, 2 * D], BF16, kind="Internal").ap()
    cc3_out = nc.dram_tensor("cc3_out", [NCORES * 128, 2 * D], BF16, kind="Internal",
                             addr_space="Shared").ap()

    DR = {
        "x1_in": x1_in,
        "wqkvt": wqkvt,
        "woutt": woutt,
        "bqk": bqk,
        "bvrow_d": bvrow_d,
        "bo": bo,
        "identf": identf,
        "identb_d": identb_d,
        "dmaskt_d": dmaskt_d,
        "dq1_d": dq1_d,
        "lamc_d": lamc_d,
        "dkcol_d": dkcol_d,
        "dqfix_d": dqfix_d,
        "wlam_d": wlam_d,
        "cpool32": cpool32,
        "cpool16": cpool16,
        "xn2_out": xn2_out,
        "s1_out": s1_out,
        "cc3_in": cc3_in,
        "cc3_out": cc3_out,
    }
    with stack:
        prog = Prog(nc, stack)
        C = Consts(nc, stack)
        A = Arena(nc, stack, 206 * 1024)
        emit_p3(nc, stack, prog, C, A, DR)
        block = stack.enter_context(nc.Block())
        prog.replay_into(block)
        consts = C.host_arrays()
    return nc, consts

def p3_consts(core):
    lam = LAMBDA
    dmaskt = np.zeros((128, NHEAD * 128), np.float32)
    dq1 = np.zeros((128, NHEAD * 128), np.float32)
    lamc = np.zeros((128, NHEAD * 128), np.float32)
    dkcol = np.zeros((128, NHEAD), np.float32)
    dqfix = np.zeros((128, NHEAD * SC), np.float32)
    pos = np.arange(128)
    for h in range(NHEAD):
        l_ = lam[h]
        diff = pos[None, :] - pos[:, None]
        dmaskt[:, 128 * h:128 * (h + 1)] = np.where(diff >= 0, l_ ** np.maximum(diff, 0), 0.0)
        dq1[:, 128 * h:128 * (h + 1)] = (l_ ** (pos + 1.0))[None, :]
        lamc[:, 128 * h:128 * (h + 1)] = l_ ** 128
        dkcol[:, h] = l_ ** (127.0 - pos)
        tpos = np.arange(SC)
        dqfix[:, SC * h:SC * (h + 1)] = (l_ ** (tpos + 1.0))[None, :]
    wlam = np.zeros((128, NCORES * D), np.float32)
    for j in range(NCORES):
        if j < core:
            for h in range(NHEAD):
                wlam[:, D * j + 128 * h:D * j + 128 * (h + 1)] = lam[h] ** (512.0 * (core - 1 - j))
    bf = lambda a: np.ascontiguousarray(a.astype("bfloat16"))
    return {"dmaskt_d": bf(dmaskt), "dq1_d": bf(dq1), "lamc_d": bf(lamc),
            "dkcol_d": np.ascontiguousarray(dkcol), "dqfix_d": bf(dqfix), "wlam_d": bf(wlam)}


def p3_inputs(inputs, consts, p2_results):
    a32, a16 = consts
    wqkv = np.asarray(inputs["L1_attn_w_qkv"], np.float32)
    wout = np.asarray(inputs["L1_attn_w_out"], np.float32)
    bqkv = np.asarray(inputs["L1_attn_b_qkv"], np.float32)
    bout = np.asarray(inputs["L1_attn_b_out"], np.float32)
    ident = np.eye(128, dtype=np.float32)
    base = {
        "wqkvt": np.ascontiguousarray(wqkv.T),
        "woutt": np.ascontiguousarray(wout.T),
        "bqk": np.ascontiguousarray(bqkv[:2 * D].reshape(32, 128).T),
        "bvrow_d": np.ascontiguousarray(bqkv[2 * D:].reshape(1, D)),
        "bo": np.ascontiguousarray(bout.reshape(KT, 128).T),
        "identf": ident,
        "identb_d": np.ascontiguousarray(ident.astype("bfloat16")),
        "cpool32": a32,
        "cpool16": a16,
    }
    per_core = []
    for c in range(NCORES):
        d = dict(base)
        d.update(p3_consts(c))
        d["x1_in"] = np.asarray(p2_results[c]["x1_out"])
        per_core.append(d)
    return per_core


def run_p3(inputs, p2_results):
    nc, consts = build_p3()
    per_core = p3_inputs(inputs, consts, p2_results)
    return run_bass_kernel_spmd(nc, per_core, core_ids=list(range(NCORES)))


def _hadamard(n):
    h = np.array([[1.0]])
    while h.shape[0] < n:
        h = np.block([[h, h], [h, -h]])
    return h


def fwht_perm1():
    p = np.empty(D, np.int64)
    for j in range(16):
        for lo3 in range(8):
            for t4 in range(16):
                p[j * 128 + lo3 * 16 + t4] = t4 * 128 + j * 8 + lo3
    return p


def fwht_perm2():
    p = np.empty(INTER, np.int64)
    for j2 in range(64):
        for lo1 in range(2):
            for t6 in range(64):
                p[j2 * 128 + lo1 * 64 + t6] = t6 * 128 + j2 * 2 + lo1
    return p


def emit_p4(nc, stack, prog, C, A, DR):
    NT = INTER // 128
    x1_in = DR["x1_in"]
    xn2_in = DR["xn2_in"]
    s1_in = DR["s1_in"]
    if "wupt16" not in DR:
        w16 = {}
        w16_toks = {}
        for nm, src_nm, a_ in (("wupt16", "wupt", 16), ("wgt16", "wgt", 16),
                               ("wdnt16", "wdnt", 64)):
            t16 = nc.dram_tensor(nm, [128, 131072], BF16, kind="Internal").ap()
            w16[nm] = t16
            w16_toks[nm] = prog.dma(
                "gpsimd", t16.rearrange("p (a n) -> p a n", a=a_),
                DR[src_nm].rearrange("(a p) n -> p a n", p=128), sem="d_cast")
        DR = dict(DR)
        DR.update(w16)
        DR["_w16_toks"] = w16_toks
    wupt16 = DR["wupt16"].rearrange("p (a n) -> p a n", a=16)
    wgt16 = DR["wgt16"].rearrange("p (a n) -> p a n", a=16)
    wdnt16 = DR["wdnt16"].rearrange("p (a n) -> p a n", a=64)
    w16_toks = DR["_w16_toks"]
    suug_d = DR["suug_d"]
    sudp_d = DR["sudp_d"]
    wscup_d = DR["wscup_d"]
    wscg_d = DR["wscg_d"]
    wscdp_d = DR["wscdp_d"]
    h128_d = DR["h128_d"]
    sb1_d = DR["sb1_d"]
    sb2_d = DR["sb2_d"]
    identf = DR["identf"]
    cpool32 = DR["cpool32"]
    cpool16 = DR["cpool16"]
    out_slab = DR["out_slab"]
    za_dram = DR["za_dram"]
    za2_dram = DR["za2_dram"]
    cur = [0]

    def al(shape, dtype, name=None):
        per = int(np.prod(shape[1:])) * mybir.dt.size(dtype)
        off = cur[0]
        cur[0] += (per + 31) // 32 * 32
        return A.at(off, shape, dtype, name)

    ident = al([128, 128], F32, "ident")
    h128b = al([128, 128], BF16, "h128b")
    sb1 = al([128, 128], BF16, "sb1")
    sb2 = al([128, 128], BF16, "sb2")
    suug = al([128, KT], F32, "suug")
    sudp = al([128, NT], F32, "sudp")
    v_up = al([128, NT], F32, "v_up")
    v_g = al([128, NT], F32, "v_g")
    v_dn = al([128, KT], F32, "v_dn")
    zst = [al([128, 512], BF16, f"zst{i}") for i in range(2)]
    mtst = [al([128, T], BF16, f"mtst{i}") for i in range(2)]
    gss = [al([128, 512], BF16, f"gss{i}") for i in range(4)]
    mst = [al([128, 512], BF16, f"mst{i}") for i in range(4)]
    s1st = [al([128, T], BF16, f"s1st{i}") for i in range(2)]
    gst = al([128, T], F32, "gst")
    gbf = al([128, T], BF16, "gbf")
    otm = [al([128, D], F32, f"otm{i}") for i in range(2)]
    P_END = cur[0]

    W0 = (P_END + 63) // 64 * 64
    ZC = W0
    XN2R = W0 + 32 * 1024
    WBR = W0 + 64 * 1024
    MHR = W0 + 96 * 1024
    WBR2 = W0 + 112 * 1024
    WEND = W0 + 144 * 1024
    assert WEND <= 206 * 1024, (WEND, W0)

    zc = [A.at(ZC + k * 2048, [128, T], BF16, f"zc{k}") for k in range(KT)]
    xn2 = [A.at(XN2R + k * 2048, [128, T], BF16, f"xn2_{k}") for k in range(KT)]
    y3a = [A.at(XN2R + k * 2048, [128, T], BF16, f"y3a{k}") for k in range(KT)]
    wbU = [A.at(r, [128, 8192], BF16, f"wbU{i}") for i, r in enumerate((WBR, WBR2))]
    wbG = [A.at(r + 16 * 1024, [128, 8192], BF16, f"wbG{i}") for i, r in enumerate((WBR, WBR2))]
    wbD = [A.at(r, [128, 8 * 2048], BF16, f"wbD{i}") for i, r in enumerate((WBR, WBR2))]
    mh = [A.at(MHR + i * 2048, [128, T], BF16, f"mh{i}") for i in range(8)]
    xo = [A.at((ZC + k * 4096) if k < 8 else (WBR + (k - 8) * 4096), [128, T], F32, f"xo{k}")
          for k in range(KT)]

    tok_cp = prog.dma("sync", C.tile_f32[:], cpool32, sem="d_pre")
    prog.dma("sync", C.tile_bf16[:], cpool16, sem="d_pre")
    prog.dma("sync", ident[:], identf, sem="d_pre")
    prog.dma("sync", h128b[:], h128_d, sem="d_pre")
    prog.dma("sync", sb1[:], sb1_d, sem="d_pre")
    prog.dma("sync", sb2[:], sb2_d, sem="d_pre")
    prog.dma("sync", suug[:], suug_d, sem="d_pre")
    prog.dma("sync", sudp[:], sudp_d, sem="d_pre")
    prog.dma("sync", v_up[:], wscup_d, sem="d_pre")
    prog.dma("sync", v_g[:], wscg_d, sem="d_pre")
    tok_pre = prog.dma("sync", v_dn[:], wscdp_d, sem="d_pre")
    f1 = prog.op("vector", lambda e: e.tensor_scalar(v_g[:], v_g[:], float(1 / np.sqrt(D)), None, ALU.mult),
                 after=[tok_pre], inc="v", force_wait=True)
    f2 = prog.op("vector", lambda e: e.tensor_mul(v_up[:], v_up[:], sudp[:]), after=[f1], inc="v",
                 force_wait=True)
    f3 = prog.op("vector", lambda e: e.tensor_scalar(v_up[:], v_up[:], float(1 / np.sqrt(D)), None, ALU.mult),
                 after=[f2], inc="v", force_wait=True)
    tok_fold = prog.op("vector", lambda e: e.tensor_scalar(v_dn[:], v_dn[:], float(1 / np.sqrt(INTER)),
                                                           None, ALU.mult), after=[f3], inc="v",
                       force_wait=True)

    es_ps = ExitStack()
    ps = [es_ps.enter_context(nc.psum_tensor(f"q4ps{i}", [128, 512], F32)) for i in range(8)]
    psfree = [None] * 8

    tok_xn2 = [prog.dma("sync", xn2[k][:], xn2_in[:, T * k:T * (k + 1)], sem="d_x") for k in range(KT)]
    z_cons = [None, None]
    zst_cons = [None, None]
    it = 0
    for k in range(KT):
        par = k % 2
        zb = mtst[par]
        tz = prog.op("vector", lambda e, k=k, zb=zb: e.tensor_scalar(zb[:], xn2[k][:], suug[:, k:k + 1],
                                                                     None, ALU.mult),
                     after=[tok_xn2[k], tok_pre, z_cons[par]], inc="v")
        for c in range(2):
            pj = it % 4
            sp = it % 2
            mm = prog.op("tensor", lambda e, zb=zb, c=c, pj=pj: nc.tensor.matmul(
                ps[pj][:], h128b[:], zb[:, 512 * c:512 * (c + 1)], start=True, stop=True),
                after=[tz, psfree[pj]], inc="t")
            if c == 1:
                z_cons[par] = mm
            ev = prog.op("vector", lambda e, pj=pj, sp=sp: e.tensor_copy(zst[sp][:], ps[pj][:]),
                         after=[mm, zst_cons[sp]], inc="v")
            psfree[pj] = ev
            zst_cons[sp] = prog.dma("sync", za_dram[:, T * k + 512 * c:T * k + 512 * (c + 1)],
                                    zst[sp][:], after=[ev], sem="d_za")
            it += 1
    tok_zaw = prog.barrier_tok("d_za")

    za_flat = za_dram.rearrange("p (s t) -> (p s) t", s=KT)
    mt_cons = [None, None]
    zc_tok = [None] * KT
    for j in range(KT):
        par = j % 2
        ld = prog.dma("sync", mtst[par][:], za_flat[128 * j:128 * (j + 1), :],
                      after=[tok_zaw, mt_cons[par], z_cons[par]], sem=f"d_mt{par}")
        mm_last = None
        for c in range(2):
            pj = 4 + (2 * j + c) % 4
            mm = prog.op("tensor", lambda e, par=par, c=c, pj=pj: nc.tensor.matmul(
                ps[pj][:], sb1[:], mtst[par][:, 512 * c:512 * (c + 1)], start=True, stop=True),
                after=[ld, psfree[pj]], inc="t")
            mm_last = mm
            psfree[pj] = prog.op("vector", lambda e, j=j, c=c, pj=pj: e.tensor_copy(
                zc[j][:, 512 * c:512 * (c + 1)], ps[pj][:]), after=[mm], inc="v")
        mt_cons[par] = mm_last
        zc_tok[j] = psfree[4 + (2 * j + 1) % 4]

    wu_tok = [None, None]
    wg_tok2 = [None, None]
    wU_cons = [None, None]
    wG_cons = [None, None]
    za2_st = [None, None]
    for ch in range(KT):
        par = ch % 2
        gate_w = [prog.barrier_tok("t")]
        wu_tok[par] = prog.dma("gpsimd", wbU[par][:],
                               wupt16[:, :, 512 * ch:512 * (ch + 1)],
                               after=[wu_tok[par], wU_cons[par], w16_toks["wupt16"]] + gate_w,
                               sem=f"d_wu{par}")
        wg_tok2[par] = prog.dma("gpsimd", wbG[par][:],
                                wgt16[:, :, 512 * ch:512 * (ch + 1)],
                                after=[wg_tok2[par], wG_cons[par], w16_toks["wgt16"]] + gate_w,
                                sem=f"d_wg{par}")
        for tc in range(2):
            mmU = [None] * 4
            mmG = [None] * 4
            for k in range(KT):
                for j in range(4):
                    def fU(e, k=k, j=j, tc=tc, par=par):
                        return nc.tensor.matmul(ps[j][:], wbU[par][:, 512 * k + 128 * j:512 * k + 128 * (j + 1)],
                                                zc[k][:, 512 * tc:512 * (tc + 1)],
                                                start=(k == 0), stop=(k == KT - 1))
                    aft = [wu_tok[par], zc_tok[k], psfree[j]] if k == 0 else None
                    mmU[j] = prog.op("tensor", fU, after=aft, inc="t" if k == KT - 1 else None)
                for j in range(4):
                    def fG(e, k=k, j=j, tc=tc, par=par):
                        return nc.tensor.matmul(ps[4 + j][:], wbG[par][:, 512 * k + 128 * j:512 * k + 128 * (j + 1)],
                                                zc[k][:, 512 * tc:512 * (tc + 1)],
                                                start=(k == 0), stop=(k == KT - 1))
                    aft = [wg_tok2[par], zc_tok[k], psfree[4 + j]] if k == 0 else None
                    mmG[j] = prog.op("tensor", fG, after=aft, inc="t" if k == KT - 1 else None)
            wU_cons[par] = mmU[3]
            wG_cons[par] = mmG[3]
            for j in range(4):
                n = 4 * ch + j
                e_up = prog.op("vector", lambda e, j=j, n=n: e.tensor_scalar(
                    mst[j][:], ps[j][:], v_up[:, n:n + 1], None, ALU.mult),
                    after=[mmU[j], tok_fold], inc="v")
                e_g = prog.op("scalar", lambda e, j=j, n=n: e.activation(
                    gss[j][:], ps[4 + j][:], AF.Silu, scale=v_g[:, n:n + 1]),
                    after=[mmG[j], tok_fold, tok_cp], inc="s")
                e_m = prog.op("vector", lambda e, j=j: e.tensor_mul(mst[j][:], mst[j][:], gss[j][:]),
                              after=[e_up, e_g], inc="v", force_wait=True)
                psfree[j] = e_m
                mm2 = prog.op("tensor", lambda e, j=j: nc.tensor.matmul(
                    ps[j][:], h128b[:], mst[j][:], start=True, stop=True),
                    after=[e_m], inc="t")
                psfree[4 + j] = mm2
                sp = (2 * (4 * ch + j) + tc) % 2
                ev2 = prog.op("vector", lambda e, j=j, sp=sp: e.tensor_copy(zst[sp][:], ps[j][:]),
                              after=[mm2, za2_st[sp]], inc="v")
                psfree[j] = ev2
                za2_st[sp] = prog.dma("sync", za2_dram[:, T * n + 512 * tc:T * n + 512 * (tc + 1)],
                                      zst[sp][:], after=[ev2], sem="d_za")
    tok_za2 = prog.barrier_tok("d_za")

    za2_flat = za2_dram.rearrange("p (s t) -> (p s) t", s=NT)
    wd_tok = [None, None]
    wD_cons = [wU_cons[0], wU_cons[1]]
    y3_tok = [None] * KT
    mh_tok = [None] * 8
    mt_cons = [None, None]
    for kc in range(8):
        wpar = kc % 2
        wd_tok[wpar] = prog.dma("gpsimd", wbD[wpar][:],
                                wdnt16[:, 8 * kc:8 * (kc + 1), :],
                                after=[wd_tok[wpar], wD_cons[wpar], w16_toks["wdnt16"],
                                       wG_cons[wpar] if kc < 2 else None], sem=f"d_wd{wpar}")
        for kk in range(8):
            j2 = 8 * kc + kk
            par = kk % 2
            ld = prog.dma("sync", mtst[par][:], za2_flat[128 * j2:128 * (j2 + 1), :],
                          after=[tok_za2, mt_cons[par]], sem=f"d_mt{par}")
            mm_last = None
            for c in range(2):
                pj = (2 * kk + c) % 4
                mm = prog.op("tensor", lambda e, par=par, c=c, pj=pj: nc.tensor.matmul(
                    ps[pj][:], sb2[:], mtst[par][:, 512 * c:512 * (c + 1)], start=True, stop=True),
                    after=[ld, psfree[pj]], inc="t")
                mm_last = mm
                psfree[pj] = prog.op("vector", lambda e, kk=kk, c=c, pj=pj: e.tensor_copy(
                    mh[kk][:, 512 * c:512 * (c + 1)], ps[pj][:]), after=[mm, mh_tok[kk]], inc="v")
            mt_cons[par] = mm_last
            mh_tok[kk] = psfree[(2 * kk + 1) % 4]
        for tc in range(2):
            for ng in range(2):
                mmt = [None] * 8
                for kk in range(8):
                    for j in range(8):
                        n = ng * 8 + j
                        def f(e, kk=kk, j=j, n=n, tc=tc, wpar=wpar):
                            return nc.tensor.matmul(ps[j][:], wbD[wpar][:, 2048 * kk + 128 * n:2048 * kk + 128 * (n + 1)],
                                                    mh[kk][:, 512 * tc:512 * (tc + 1)],
                                                    start=(kk == 0), stop=(kk == 7))
                        aft = [wd_tok[wpar], mh_tok[kk]]
                        if kk == 0:
                            aft += [psfree[j]]
                        mmt[j] = prog.op("tensor", f, after=aft, inc="t" if kk == 7 else None)
                wD_cons[wpar] = mmt[7]
                for j in range(8):
                    n = ng * 8 + j
                    if kc == 0:
                        tk = prog.op("vector", lambda e, j=j, n=n, tc=tc: e.tensor_copy(
                            y3a[n][:, 512 * tc:512 * (tc + 1)], ps[j][:]),
                            after=[mmt[j], tok_xn2[n]], inc="v")
                    else:
                        tk = prog.op("vector", lambda e, j=j, n=n, tc=tc: e.tensor_tensor(
                            y3a[n][:, 512 * tc:512 * (tc + 1)], y3a[n][:, 512 * tc:512 * (tc + 1)],
                            ps[j][:], ALU.add), after=[mmt[j], y3_tok[n]], inc="v", force_wait=True)
                    psfree[j] = tk
                    y3_tok[n] = tk
        for kk in range(8):
            mh_tok[kk] = prog.barrier_tok("t")

    tok_wdn_done = prog.barrier_tok("t")
    x1_tok = [None] * KT
    s1_tok2 = [None] * KT
    xo_tok = [None] * KT
    sA = [None, None]
    for n in range(KT):
        par = n % 2
        l1 = prog.dma("sync", xo[n][:], x1_in[:, T * n:T * (n + 1)],
                      after=[tok_wdn_done, y3_tok[n]], sem="d_r1")
        l2 = prog.dma("sync", s1st[par][:], s1_in[:, T * n:T * (n + 1)],
                      after=[s1_tok2[n - 2] if n >= 2 else None], sem=f"d_r2{par}")
        m1 = prog.op("vector", lambda e, n=n: e.tensor_scalar(gst[:], y3a[n][:], v_dn[:, n:n + 1],
                                                              None, ALU.mult),
                     after=[y3_tok[n], xo_tok[n - 1] if n >= 1 else None], inc="v")
        m2 = prog.op("scalar", lambda e: e.activation(gbf[:], gst[:], AF.Silu), after=[m1], inc="s")
        m3 = prog.op("vector", lambda e, par=par: e.tensor_mul(s1st[par][:], s1st[par][:], gbf[:]),
                     after=[m2, l2], inc="v")
        s1_tok2[n] = m3
        xo_tok[n] = prog.op("vector", lambda e, n=n, par=par: e.tensor_tensor(
            xo[n][:], xo[n][:], s1st[par][:], ALU.add), after=[m3, l1], inc="v", force_wait=True)
    ot_cons = [None, None]
    for ti in range(8):
        par = ti % 2
        ev_last = None
        for n in range(KT):
            pj = n % 4
            mm = prog.op("tensor", lambda e, n=n, ti=ti, pj=pj: nc.tensor.transpose(
                ps[pj][:, 0:128], xo[n][:, 128 * ti:128 * (ti + 1)], ident[:]),
                after=[xo_tok[n], psfree[pj], tok_cp], inc="t")
            ev_last = prog.op("vector", lambda e, n=n, par=par, pj=pj: e.tensor_copy(
                otm[par][:, 128 * n:128 * (n + 1)], ps[pj][:, 0:128]),
                after=[mm, ot_cons[par] if n == 0 else None], inc="v")
            psfree[pj] = ev_last
        ot_cons[par] = prog.dma("sync", out_slab[128 * ti:128 * (ti + 1), :], otm[par][:],
                                after=[ev_last], sem="d_st")
    prog.op("sync", None, after=[prog.barrier_tok("d_st")])

    es_ps.close()


def build_p4():
    nc = bass.Bass("TRN2", target_bir_lowering=False)
    stack = ExitStack()
    NT = INTER // 128

    x1_in = nc.dram_tensor("x1_in", [128, KT * T], F32, kind="ExternalInput").ap()
    xn2_in = nc.dram_tensor("xn2_in", [128, KT * T], BF16, kind="ExternalInput").ap()
    s1_in = nc.dram_tensor("s1_in", [128, KT * T], BF16, kind="ExternalInput").ap()
    wupt = nc.dram_tensor("wupt", [D, INTER], F32, kind="ExternalInput").ap()
    wgt = nc.dram_tensor("wgt", [D, INTER], F32, kind="ExternalInput").ap()
    wdnt = nc.dram_tensor("wdnt", [INTER, D], F32, kind="ExternalInput").ap()
    suug_d = nc.dram_tensor("suug_d", [128, KT], F32, kind="ExternalInput").ap()
    sudp_d = nc.dram_tensor("sudp_d", [128, NT], F32, kind="ExternalInput").ap()
    wscup_d = nc.dram_tensor("wscup_d", [128, NT], F32, kind="ExternalInput").ap()
    wscg_d = nc.dram_tensor("wscg_d", [128, NT], F32, kind="ExternalInput").ap()
    wscdp_d = nc.dram_tensor("wscdp_d", [128, KT], F32, kind="ExternalInput").ap()
    h128_d = nc.dram_tensor("h128_d", [128, 128], BF16, kind="ExternalInput").ap()
    sb1_d = nc.dram_tensor("sb1_d", [128, 128], BF16, kind="ExternalInput").ap()
    sb2_d = nc.dram_tensor("sb2_d", [128, 128], BF16, kind="ExternalInput").ap()
    identf = nc.dram_tensor("identf", [128, 128], F32, kind="ExternalInput").ap()
    cpool32 = nc.dram_tensor("cpool32", [128, 64], F32, kind="ExternalInput").ap()
    cpool16 = nc.dram_tensor("cpool16", [128, 64], BF16, kind="ExternalInput").ap()

    out_slab = nc.dram_tensor("out_slab", [T, D], F32, kind="ExternalOutput").ap()

    za_dram = nc.dram_tensor("za_dram", [128, KT * T], BF16, kind="Internal").ap()
    za2_dram = nc.dram_tensor("za2_dram", [128, NT * T], BF16, kind="Internal").ap()

    DR = {
        "x1_in": x1_in,
        "xn2_in": xn2_in,
        "s1_in": s1_in,
        "wupt": wupt,
        "wgt": wgt,
        "wdnt": wdnt,
        "suug_d": suug_d,
        "sudp_d": sudp_d,
        "wscup_d": wscup_d,
        "wscg_d": wscg_d,
        "wscdp_d": wscdp_d,
        "h128_d": h128_d,
        "sb1_d": sb1_d,
        "sb2_d": sb2_d,
        "identf": identf,
        "cpool32": cpool32,
        "cpool16": cpool16,
        "out_slab": out_slab,
        "za_dram": za_dram,
        "za2_dram": za2_dram,
    }
    with stack:
        prog = Prog(nc, stack)
        C = Consts(nc, stack)
        A = Arena(nc, stack, 206 * 1024)
        emit_p4(nc, stack, prog, C, A, DR)
        block = stack.enter_context(nc.Block())
        prog.replay_into(block)
        consts = C.host_arrays()
    return nc, consts

def p4_inputs(inputs, consts, p2_results, p3_results):
    a32, a16 = consts
    p1_ = fwht_perm1()
    p2_ = fwht_perm2()
    wup = np.asarray(inputs["L1_ffn_W_up"], np.float32)
    wgate = np.asarray(inputs["L1_ffn_W_gate"], np.float32)
    wdn = np.asarray(inputs["L1_ffn_W_down"], np.float32)
    wsc = np.asarray(inputs["L1_ffn_Wscale_ug"], np.float32)
    base = {
        "wupt": np.ascontiguousarray(wup.T[p1_, :]),
        "wgt": np.ascontiguousarray(wgate.T[p1_, :]),
        "wdnt": np.ascontiguousarray(wdn.T[p2_, :]),
        "suug_d": _tilevec(inputs["L1_ffn_SU_ug"]),
        "sudp_d": _tilevec(inputs["L1_ffn_SU_dp"]),
        "wscup_d": _tilevec(wsc[:INTER]),
        "wscg_d": _tilevec(wsc[INTER:]),
        "wscdp_d": _tilevec(inputs["L1_ffn_Wscale_dp"]),
        "h128_d": np.ascontiguousarray(_hadamard(128).astype("bfloat16")),
        "sb1_d": np.ascontiguousarray(np.kron(np.eye(8), _hadamard(16)).astype("bfloat16")),
        "sb2_d": np.ascontiguousarray(np.kron(np.eye(2), _hadamard(64)).astype("bfloat16")),
        "identf": np.eye(128, dtype=np.float32),
        "cpool32": a32,
        "cpool16": a16,
    }
    per_core = []
    for c in range(NCORES):
        d = dict(base)
        d["x1_in"] = np.asarray(p2_results[c]["x1_out"])
        d["xn2_in"] = np.asarray(p3_results[c]["xn2_out"])
        d["s1_in"] = np.asarray(p3_results[c]["s1_out"])
        per_core.append(d)
    return per_core


def run_p4(inputs, p2_results, p3_results):
    nc, consts = build_p4()
    per_core = p4_inputs(inputs, consts, p2_results, p3_results)
    return run_bass_kernel_spmd(nc, per_core, core_ids=list(range(NCORES)))





def build_fused(dbg=False):
    nc = bass.Bass("TRN2", target_bir_lowering=False)
    stack = ExitStack()
    NT = INTER // 128

    def ein(name, shape, dtype):
        return nc.dram_tensor(name, shape, dtype, kind="ExternalInput").ap()

    ins = {}
    for nm, shp, dt_ in (
        ("x_slab", [T, D], F32), ("w_in", [D, D], F32), ("wg", [2, 8, 256, 128], F32),
        ("w_out", [D, D], F32), ("bias_in", [128, KT], F32), ("bias_out", [128, KT], F32),
        ("bias_g", [128, 16], F32), ("arp", [128, 8], F32), ("aip", [128, 8], F32),
        ("identf", [128, 128], F32), ("cpool32", [128, 64], F32), ("cpool16", [128, 64], BF16),
        ("mask8", [128, 8], F32),
        ("w1t", [D, INTER], F32), ("w2t", [INTER, D], F32),
        ("wslice1", [D, 1024], F32), ("wslice2", [1024, D], F32),
        ("wqkvt", [D, 3 * D], F32), ("woutt", [D, D], F32), ("bqk", [128, 32], F32),
        ("bvrow_d", [1, D], F32), ("bo", [128, KT], F32), ("identb_d", [128, 128], BF16),
        ("dmaskt_d", [128, NHEAD * 128], BF16), ("dq1_d", [128, NHEAD * 128], BF16),
        ("lamc_d", [128, NHEAD * 128], BF16), ("dkcol_d", [128, NHEAD], F32),
        ("dqfix_d", [128, NHEAD * SC], BF16), ("wlam_d", [128, NCORES * D], BF16),
        ("wupt", [D, INTER], F32), ("wgt", [D, INTER], F32), ("wdnt", [INTER, D], F32),
        ("suug_d", [128, KT], F32), ("sudp_d", [128, NT], F32), ("wscup_d", [128, NT], F32),
        ("wscg_d", [128, NT], F32), ("wscdp_d", [128, KT], F32),
        ("h128_d", [128, 128], BF16), ("sb1_d", [128, 128], BF16), ("sb2_d", [128, 128], BF16),
    ):
        ins[nm] = ein(nm, shp, dt_)

    out_slab = nc.dram_tensor("out_slab", [T, D], F32, kind="ExternalOutput").ap()

    ho = {}
    for nm, shp, dt_ in (
        ("ho_xT", [128, KT * T], F32), ("ho_xn", [128, KT * T], BF16),
        ("ho_s0", [128, KT * T], BF16), ("ho_var", [1, T], F32),
        ("ho_x1", [128, KT * T], F32), ("ho_xn2", [128, KT * T], BF16),
        ("ho_s1", [128, KT * T], BF16),
        ("hmid", [128, NT * T], BF16), ("za_dram", [128, KT * T], BF16),
        ("za2_dram", [128, NT * T], BF16),
        ("cc_in", [128, 64], F32), ("cc2_in", [1, 16], F32), ("cc2_out", [1, 16], F32),
        ("cc3_in", [128, 2 * D], BF16),
    ):
        ho[nm] = nc.dram_tensor(nm, shp, dt_, kind="Internal").ap()
    ho["cc_out"] = nc.dram_tensor("cc_out", [NCORES * 128, 64], F32, kind="Internal",
                                  addr_space="Shared").ap()
    ho["cc3_out"] = nc.dram_tensor("cc3_out", [NCORES * 128, 2 * D], BF16, kind="Internal",
                                   addr_space="Shared").ap()
    ho["ccw_in"] = nc.dram_tensor("ccw_in", [128, 32768], BF16, kind="Internal").ap()
    ho["ccw_out"] = nc.dram_tensor("ccw_out", [NCORES * 128, 32768], BF16, kind="Internal",
                                   addr_space="Shared").ap()
    ho["ho_sw"] = nc.dram_tensor("ho_sw", [1, 16], F32, kind="Internal").ap()
    ho["p0scr"] = nc.dram_tensor("p0scr", [1, 256], F32, kind="Internal").ap()

    with stack:
        prog = Prog(nc, stack)
        C = Consts(nc, stack)
        A = Arena(nc, stack, 206 * 1024)

        dr0 = dict(ins)
        dr0.update({"ccw_in": ho["ccw_in"], "ccw_out": ho["ccw_out"], "ho_sw": ho["ho_sw"],
                    "p0scr": ho["p0scr"], "cc2_in": ho["cc2_in"], "cc2_out": ho["cc2_out"]})
        p0_toks = emit_p0(nc, stack, prog, C, A, dr0)

        dr1 = dict(ins)
        dr1.update({"xT_out": ho["ho_xT"], "xn_out": ho["ho_xn"], "s0_out": ho["ho_s0"],
                    "var_out": ho["ho_var"], "cc_in": ho["cc_in"], "cc_out": ho["cc_out"],
                    "_xh_gate": p0_toks["xh_gate"]})
        emit_p1(nc, stack, prog, C, A, dr1)
        w16 = {}
        w16_toks = {}
        for nm, src_nm, a_ in (("wupt16", "wupt", 16), ("wgt16", "wgt", 16),
                               ("wdnt16", "wdnt", 64)):
            t16 = nc.dram_tensor(nm, [128, 131072], BF16, kind="Internal").ap()
            w16[nm] = t16
            w16_toks[nm] = prog.dma(
                "gpsimd", t16.rearrange("p (a n) -> p a n", a=a_),
                ins[src_nm].rearrange("(a p) n -> p a n", p=128), sem="d_cast")
        prog.set_phase_gate()

        dr2 = dict(ins)
        dr2.update({"xn_in": ho["ho_xn"], "xT_in": ho["ho_xT"], "s0_in": ho["ho_s0"],
                    "var_in": ho["ho_var"], "x1_out": ho["ho_x1"], "hmid": ho["hmid"],
                    "ccw_out": ho["ccw_out"], "ho_sw": ho["ho_sw"],
                    "cc2_in": ho["cc2_in"], "cc2_out": ho["cc2_out"],
                    "_tok_ccw": p0_toks["ccw"]})
        emit_p2(nc, stack, prog, C, A, dr2)
        prog.set_phase_gate()

        if dbg:
            o_x1 = nc.dram_tensor("o_x1", [128, KT * T], F32, kind="ExternalOutput").ap()
            o_s0 = nc.dram_tensor("o_s0", [128, KT * T], BF16, kind="ExternalOutput").ap()
            prog.dma("sync", o_x1, ho["ho_x1"], after=prog.gate, sem="d_dbg")
            prog.dma("sync", o_s0, ho["ho_s0"], after=prog.gate, sem="d_dbg")
        dr3 = dict(ins)
        dr3.update({"x1_in": ho["ho_x1"], "xn2_out": ho["ho_xn2"], "s1_out": ho["ho_s1"],
                    "cc3_in": ho["cc3_in"], "cc3_out": ho["cc3_out"]})
        emit_p3(nc, stack, prog, C, A, dr3)
        prog.set_phase_gate()

        dr4 = dict(ins)
        dr4.update({"x1_in": ho["ho_x1"], "xn2_in": ho["ho_xn2"], "s1_in": ho["ho_s1"],
                    "out_slab": out_slab, "za_dram": ho["za_dram"], "za2_dram": ho["za2_dram"]})
        dr4.update(w16)
        dr4["_w16_toks"] = w16_toks
        if dbg:
            o_s1 = nc.dram_tensor("o_s1", [128, KT * T], BF16, kind="ExternalOutput").ap()
            prog.dma("sync", o_s1, ho["ho_s1"], after=prog.gate, sem="d_dbg")
        emit_p4(nc, stack, prog, C, A, dr4)
        if dbg:
            prog.op("sync", None, after=[prog.barrier_tok("d_dbg")])

        block = stack.enter_context(nc.Block())
        prog.replay_into(block)
        consts = C.host_arrays()
    return nc, consts


def fused_inputs(inputs, consts):
    pc1 = p1_inputs(inputs, consts)
    dummy1 = [{"xn_out": None, "xT_out": None, "s0_out": None, "var_out": None}] * NCORES
    per_core = []
    w1 = np.asarray(inputs["L0_ffn_w1"], np.float32)
    w2 = np.asarray(inputs["L0_ffn_w2"], np.float32)
    w1t_h = np.ascontiguousarray(w1.T)
    w2t_h = np.ascontiguousarray(w2.T)
    wqkv = np.asarray(inputs["L1_attn_w_qkv"], np.float32)
    wout = np.asarray(inputs["L1_attn_w_out"], np.float32)
    bqkv = np.asarray(inputs["L1_attn_b_qkv"], np.float32)
    bout = np.asarray(inputs["L1_attn_b_out"], np.float32)
    p1_ = fwht_perm1()
    p2_ = fwht_perm2()
    wup = np.asarray(inputs["L1_ffn_W_up"], np.float32)
    wgate = np.asarray(inputs["L1_ffn_W_gate"], np.float32)
    wdn = np.asarray(inputs["L1_ffn_W_down"], np.float32)
    wsc = np.asarray(inputs["L1_ffn_Wscale_ug"], np.float32)
    ident = np.eye(128, dtype=np.float32)
    shared = {
        "w1t": w1t_h, "w2t": w2t_h,
        "wqkvt": np.ascontiguousarray(wqkv.T),
        "woutt": np.ascontiguousarray(wout.T),
        "bqk": np.ascontiguousarray(bqkv[:2 * D].reshape(32, 128).T),
        "bvrow_d": np.ascontiguousarray(bqkv[2 * D:].reshape(1, D)),
        "bo": np.ascontiguousarray(bout.reshape(KT, 128).T),
        "identb_d": np.ascontiguousarray(ident.astype("bfloat16")),
        "wupt": np.ascontiguousarray(wup.T[p1_, :]),
        "wgt": np.ascontiguousarray(wgate.T[p1_, :]),
        "wdnt": np.ascontiguousarray(wdn.T[p2_, :]),
        "suug_d": _tilevec(inputs["L1_ffn_SU_ug"]),
        "sudp_d": _tilevec(inputs["L1_ffn_SU_dp"]),
        "wscup_d": _tilevec(wsc[:INTER]),
        "wscg_d": _tilevec(wsc[INTER:]),
        "wscdp_d": _tilevec(inputs["L1_ffn_Wscale_dp"]),
        "h128_d": np.ascontiguousarray(_hadamard(128).astype("bfloat16")),
        "sb1_d": np.ascontiguousarray(np.kron(np.eye(8), _hadamard(16)).astype("bfloat16")),
        "sb2_d": np.ascontiguousarray(np.kron(np.eye(2), _hadamard(64)).astype("bfloat16")),
    }
    for c in range(NCORES):
        d = dict(pc1[c])
        d.update(shared)
        d.update(p3_consts(c))
        d["wslice1"] = np.ascontiguousarray(w1t_h[:, 1024 * c:1024 * (c + 1)])
        d["wslice2"] = np.ascontiguousarray(w2t_h[1024 * c:1024 * (c + 1), :])
        per_core.append(d)
    return per_core


def kernel(**inputs):
    nc, consts = build_fused()
    per_core = fused_inputs(inputs, consts)
    res = run_bass_kernel_spmd(nc, per_core, core_ids=list(range(NCORES)))
    out = np.zeros((B, S, D), np.float32)
    for c in range(NCORES):
        slab = np.asarray(res.results[c]["out_slab"], np.float32)
        out[0, c * SC:(c + 1) * SC] = slab[:SC]
        out[1, c * SC:(c + 1) * SC] = slab[SC:]
    return out


def emit_p0(nc, stack, prog, C, A, DR):
    wslice1 = DR["wslice1"]
    wslice2 = DR["wslice2"]
    ccw_in = DR["ccw_in"]
    ccw_out = DR["ccw_out"]
    cc2_in = DR["cc2_in"]
    cc2_out = DR["cc2_out"]
    ho_sw = DR["ho_sw"]
    scr_d = DR["p0scr"]

    B0 = 54 * 1024
    wf = [A.at(B0 + i * 8192, [128, D], F32, f"p0wf{i}") for i in range(2)]
    wq8 = A.at(B0 + 16 * 1024, [128, D], I8, "p0wq8")
    wcl = A.at(B0 + 20 * 1024, [128, 1024], F32, "p0wcl")
    swacc = A.at(B0 + 24 * 1024, [128, 2], F32, "p0swacc")
    redc = A.at(B0 + 24 * 1024 + 64, [128, 1], F32, "p0redc")
    swrow = A.at(B0 + 24 * 1024 + 128, [1, 256], F32, "p0swrow")
    swb = A.at(B0 + 25 * 1024, [128, 2], F32, "p0swb")

    tok_z = prog.op("vector", lambda e: e.memset(swacc[:], 0.0), inc="v")
    ld = [None, None]
    prevred = tok_z
    for k in range(KT):
        b = k % 2
        ld[b] = prog.dma("sync", wf[b][:, 0:1024], wslice1[128 * k:128 * (k + 1), :],
                         after=[ld[b]], sem=f"d_p0{b}")
        rk = prog.op("vector", lambda e, b=b: e.reduce_sum(redc[:], wf[b][:, 0:1024], axis=AX.X,
                                                           apply_absolute_value=True),
                     after=[ld[b], prevred], inc="v", force_wait=True)
        prevred = prog.op("vector", lambda e: e.tensor_add(swacc[:, 0:1], swacc[:, 0:1], redc[:]),
                          after=[rk], inc="v", force_wait=True)
        ld[b] = prevred
    for k in range(8):
        b = k % 2
        ld[b] = prog.dma("sync", wf[b][:], wslice2[128 * k:128 * (k + 1), :],
                         after=[ld[b]], sem=f"d_p0{b}")
        rk = prog.op("vector", lambda e, b=b: e.reduce_sum(redc[:], wf[b][:], axis=AX.X,
                                                           apply_absolute_value=True),
                     after=[ld[b], prevred], inc="v", force_wait=True)
        prevred = prog.op("vector", lambda e: e.tensor_add(swacc[:, 1:2], swacc[:, 1:2], redc[:]),
                          after=[rk], inc="v", force_wait=True)
        ld[b] = prevred
    t1 = prog.dma("sync", scr_d[0:1, 0:256], swacc[:], after=[prevred], sem="d_p0s")
    t2 = prog.dma("sync", swrow[:], scr_d[0:1, 0:256], after=[t1], sem="d_p0s")
    r1 = prog.op("vector", lambda e: e.reduce_sum(swrow[0:1, 0:1], swrow[0:1, 0:256:2], axis=AX.X),
                 after=[t2], inc="v", force_wait=True)
    r2 = prog.op("vector", lambda e: e.reduce_sum(swrow[0:1, 1:2], swrow[0:1, 1:256:2], axis=AX.X),
                 after=[r1], inc="v", force_wait=True)
    t3 = prog.dma("sync", cc2_in[0:1, 0:16], swrow[0:1, 0:16], after=[r2], sem="d_p0s")
    tok_cc = prog.op("gpsimd", lambda e: e.collective_compute(
        "AllReduce", ALU.add, replica_groups=[list(range(NCORES))],
        ins=[cc2_in.opt()], outs=[cc2_out.opt()]), after=[t3], inc="cc")
    t4 = prog.dma("sync", swrow[0:1, 0:16], cc2_out, after=[tok_cc], sem="d_p0s")
    u1 = prog.op("vector", lambda e: e.tensor_scalar(swrow[0:1, 2:4], swrow[0:1, 0:2],
                                                     1.0 / (D * INTER), None, ALU.mult),
                 after=[t4], inc="v", force_wait=True)
    u2 = prog.op("vector", lambda e: e.tensor_scalar(swrow[0:1, 2:4], swrow[0:1, 2:4], 1e-5, None,
                                                     ALU.max), after=[u1], inc="v", force_wait=True)
    u3 = prog.op("vector", lambda e: e.reciprocal(swrow[0:1, 4:6], swrow[0:1, 2:4]),
                 after=[u2], inc="v", force_wait=True)
    tok_swst = prog.dma("sync", ho_sw, swrow[0:1, 0:16], after=[u3], sem="d_p0s")
    tok_swb = prog.dma("sync", swb[:], ho_sw[0:1, 4:6].broadcast_to((128, 2)),
                       after=[tok_swst], sem="d_p0s")
    ld = [None, None]
    st_tok = None
    for k in range(KT):
        b = k % 2
        ld[b] = prog.dma("sync", wf[b][:, 0:1024], wslice1[128 * k:128 * (k + 1), :],
                         after=[ld[b]], sem=f"d_p0{b}")
        qa = prog.op("vector", lambda e, b=b: e.tensor_scalar(wcl[:], wf[b][:, 0:1024], swb[:, 0:1],
                                                              1.49, ALU.mult, ALU.min),
                     after=[ld[b], tok_swb, st_tok], inc="v")
        qb = prog.op("vector", lambda e: e.tensor_scalar(wq8[:, 0:1024], wcl[:], -1.49, None, ALU.max),
                     after=[qa], inc="v", force_wait=True)
        st_tok = prog.dma("gpsimd", ccw_in[:, 1024 * k:1024 * (k + 1)], wq8[:, 0:1024],
                          after=[qb], sem="d_p0q")
        ld[b] = st_tok
    for k in range(8):
        b = k % 2
        ld[b] = prog.dma("sync", wf[b][:], wslice2[128 * k:128 * (k + 1), :],
                         after=[ld[b]], sem=f"d_p0{b}")
        qa = prog.op("vector", lambda e, b=b: e.tensor_scalar(wcl[:], wf[b][:, 0:1024], swb[:, 1:2],
                                                              1.49, ALU.mult, ALU.min),
                     after=[ld[b], tok_swb, st_tok], inc="v")
        qb = prog.op("vector", lambda e: e.tensor_scalar(wq8[:, 0:1024], wcl[:], -1.49, None, ALU.max),
                     after=[qa], inc="v", force_wait=True)
        qa2 = prog.op("vector", lambda e, b=b: e.tensor_scalar(wcl[:], wf[b][:, 1024:2048], swb[:, 1:2],
                                                               1.49, ALU.mult, ALU.min),
                      after=[qb], inc="v", force_wait=True)
        qb2 = prog.op("vector", lambda e: e.tensor_scalar(wq8[:, 1024:2048], wcl[:], -1.49, None, ALU.max),
                      after=[qa2], inc="v", force_wait=True)
        st_tok = prog.dma("gpsimd", ccw_in[:, 16384 + 2048 * k:16384 + 2048 * (k + 1)], wq8[:],
                          after=[qb2], sem="d_p0q")
        ld[b] = st_tok
    tok_ccw = prog.op("gpsimd", lambda e: e.collective_compute(
        "AllGather", ALU.bypass, replica_groups=[list(range(NCORES))],
        ins=[ccw_in.opt()], outs=[ccw_out.opt()]),
        after=[st_tok, prog.barrier_tok("d_p0q")], inc="cc")
    return {"ccw": tok_ccw,
            "xh_gate": [prog.barrier_tok("d_p0q"), prog.barrier_tok("v")]}


# version 11
# speedup vs baseline: 1.1066x; 1.1066x over previous
import numpy as np
from contextlib import ExitStack

import concourse.bass as bass
import concourse.mybir as mybir
from concourse.bass_utils import run_bass_kernel_spmd

F32 = mybir.dt.float32
BF16 = mybir.dt.bfloat16
I8 = mybir.dt.int8
I32 = mybir.dt.int32
U8 = mybir.dt.uint8
AF = mybir.ActivationFunctionType
ALU = mybir.AluOpType
AX = mybir.AxisListType

NCORES = 8
B = 2
S = 4096
D = 2048
SC = S // NCORES
T = B * SC
KT = D // 128
H0 = slice(0, SC)
H1 = slice(SC, T)
INTER = 8192
NHEAD = 16
HD = 128
CHUNK = 128

ENGINES = ("sync", "gpsimd", "vector", "scalar", "tensor")


class Prog:

    def __init__(self, nc, stack):
        self.nc = nc
        self.stack = stack
        self.q = {e: [] for e in ENGINES}
        self.cnt = {}
        self.sems = {}
        self.wm = {e: {} for e in ENGINES}
        self.gate = []

    def sem(self, name):
        if name not in self.sems:
            self.sems[name] = self.stack.enter_context(self.nc.semaphore(name))
            self.cnt[name] = 0
        return self.sems[name]

    def op(self, eng, fn, after=None, inc=None, inc_by=1, wm_self=True, force_wait=False):
        if inc:
            self.sem(inc)
        waits = []
        byname = {}
        for t in (after or []):
            if t is None:
                continue
            byname[t[0]] = max(byname.get(t[0], 0), t[1])
        for name, v in byname.items():
            self.sem(name)
            if not force_wait and self.wm[eng].get(name, 0) >= v:
                continue
            self.wm[eng][name] = max(self.wm[eng].get(name, 0), v)
            waits.append((name, v))
        for name, v in self.gate:
            if self.wm[eng].get(name, 0) >= v:
                continue
            self.wm[eng][name] = v
            waits.append((name, v))
        tok = None
        if inc is not None:
            self.cnt[inc] += inc_by
            tok = (inc, self.cnt[inc])
            if wm_self:
                self.wm[eng][inc] = max(self.wm[eng].get(inc, 0), self.cnt[inc])
        self.q[eng].append((waits, fn, (inc, inc_by) if inc else None))
        return tok

    def dma(self, eng, out, in_, after=None, sem="d_misc", **kw):
        def fn(e, out=out, in_=in_, kw=kw):
            return e.dma_start(out=out, in_=in_, **kw)
        return self.op(eng, fn, after=after, inc=sem, inc_by=16, wm_self=False)

    def barrier_tok(self, sem_name):
        self.sem(sem_name)
        return (sem_name, self.cnt[sem_name])

    def set_phase_gate(self):
        self.gate = [(name, self.cnt[name]) for name in list(self.cnt)]

    def replay_into(self, block):
        for name in ENGINES:
            dec = getattr(block, name)

            def make(qname):
                def body(engobj):
                    for waits, fn, inc in self.q[qname]:
                        for (s, v) in waits:
                            engobj.wait_ge(self.sems[s], v)
                        inst = fn(engobj) if fn is not None else None
                        if inc is not None:
                            if inst is None:
                                inst = engobj.nop()
                            inst.then_inc(self.sems[inc[0]], inc[1])
                return body
            dec(make(name))


class Consts:
    def __init__(self, nc, stack, maxn=64):
        self.nc = nc
        self.vals_f32 = {}
        self.vals_bf16 = {}
        self.tile_f32 = stack.enter_context(nc.sbuf_tensor("cpool_f32", [128, maxn], F32))
        self.tile_bf16 = stack.enter_context(nc.sbuf_tensor("cpool_bf16", [128, maxn], BF16))
        self.maxn = maxn

    def f32(self, v):
        v = float(v)
        if v not in self.vals_f32:
            i = len(self.vals_f32)
            assert i < self.maxn
            self.vals_f32[v] = i
            self.nc.const_aps.register(F32, v, self.tile_f32[:, i:i + 1])
        return v

    def bf16(self, v):
        v = float(v)
        if v not in self.vals_bf16:
            i = len(self.vals_bf16)
            assert i < self.maxn
            self.vals_bf16[v] = i
            self.nc.const_aps.register(BF16, v, self.tile_bf16[:, i:i + 1])
        return v

    def host_arrays(self):
        a32 = np.zeros((128, self.maxn), np.float32)
        for v, i in self.vals_f32.items():
            a32[:, i] = v
        a16 = np.zeros((128, self.maxn), dtype="bfloat16")
        for v, i in self.vals_bf16.items():
            a16[:, i] = v
        return a32, a16


class Arena:

    def __init__(self, nc, stack, size_bytes, name="arena"):
        self.nc = nc
        self.size = size_bytes
        self.handle = stack.enter_context(nc.sbuf_tensor(name, [128, size_bytes], U8))
        self.base = nc.lookup_mloc(self.handle).addr
        self._n = 0

    def at(self, off, shape, dtype, name=None):
        per_part = int(np.prod(shape[1:])) * mybir.dt.size(dtype)
        assert off % 32 == 0, (off, name)
        assert off + per_part <= self.size, (off, per_part, self.size, name)
        if name is None:
            name = f"ar{self._n}"
            self._n += 1
        h = self.nc.virtual_sbuf_tensor(name, shape, dtype)
        self.nc._tensor(h.name, shape, dtype, type="SB")
        mloc = self.nc.lookup_mloc(h)
        mloc.allocated = True
        mloc.addr = self.base + off
        return h



def emit_p1(nc, stack, prog, C, A, DR):
    xh_gate = DR.get("_xh_gate") or []
    x_slab = DR["x_slab"]
    w_in = DR["w_in"]
    wg = DR["wg"]
    w_out = DR["w_out"]
    bias_in = DR["bias_in"]
    bias_out = DR["bias_out"]
    bias_g = DR["bias_g"]
    arp = DR["arp"]
    aip = DR["aip"]
    identf = DR["identf"]
    cpool32 = DR["cpool32"]
    cpool16 = DR["cpool16"]
    mask8 = DR["mask8"]
    xT_out = DR["xT_out"]
    xn_out = DR["xn_out"]
    s0_out = DR["s0_out"]
    var_out = DR["var_out"]
    cc_in = DR["cc_in"]
    cc_out = DR["cc_out"]

    cur = [0]

    def al(shape, dtype, name=None):
        per = int(np.prod(shape[1:])) * mybir.dt.size(dtype)
        off = cur[0]
        cur[0] += (per + 31) // 32 * 32
        return A.at(off, shape, dtype, name)

    onesT = al([128, T], F32, "onesT")
    zeroT = al([128, T], F32, "zeroT")
    tscr = al([128, T], F32, "tscr")
    sq = al([128, T], F32, "sq")
    ident = al([128, 128], F32, "ident")
    binT = al([128, KT], F32, "binT")
    boutT = al([128, KT], F32, "boutT")
    bgT = al([128, 16], F32, "bgT")
    arpT = al([128, 8], F32, "arpT")
    aipT = al([128, 8], F32, "aipT")
    neg8sp = al([128, 8], F32, "neg8sp")
    neg16sp = al([128, 8], F32, "neg16sp")
    maskT = al([128, 8], F32, "maskT")
    sp_a = al([128, 8], F32, "sp_a")
    sp_b = al([128, 8], F32, "sp_b")
    ones_col = al([128, 1], F32, "ones_col")
    ones_row = al([1, 128], F32, "ones_row")
    sumb = al([128, 64], F32, "sumb")
    gat = al([128, 512], F32, "gat")
    Hst = al([128, 32], F32, "Hst")
    negHi = al([128, 16], F32, "negHi")
    w_a = al([128, 64], F32, "w_a")
    scr4 = al([128, 64], F32, "scr4")
    W0 = (cur[0] + 63) // 64 * 64

    XN = W0
    XH = W0 + 32 * 1024
    GA = W0 + 64 * 1024
    GX = GA + 16 * 1024
    BIG = W0 + 96 * 1024
    AUX = W0 + 160 * 1024
    WEND = W0 + 176 * 1024
    assert WEND <= 200 * 1024, WEND

    xn = [A.at(XN + k * 2048, [128, T], BF16, f"xn{k}") for k in range(KT)]
    xh = [A.at(XH + k * 2048, [128, T], BF16, f"xh{k}") for k in range(KT)]
    ga = [A.at(GA + i * 2048, [128, T], BF16, f"ga{i}") for i in range(8)]
    gx = [A.at(GX + i * 2048, [128, T], BF16, f"gx{i}") for i in range(8)]
    xT = [A.at(BIG + k * 4096, [128, T], F32, f"xT{k}") for k in range(KT)]
    x_tm = [A.at(AUX + i * 8192, [128, D], F32, f"x_tm{i}") for i in range(2)]
    stat_sb = A.at(GA, [1, 2 * T], F32, "stat_sb")
    mrow = A.at(GA + 8192, [1, T], F32, "mrow")
    varrow = A.at(GA + 12288, [1, T], F32, "varrow")
    rstdrow = A.at(GA + 16384, [1, T], F32, "rstdrow")
    nmr = A.at(GA + 20480, [1, T], F32, "nmr")
    Brstd = A.at(GA + 24576, [128, T], F32, "Brstd")
    Bneg = A.at(GA + 28672, [128, T], F32, "Bneg")
    win_sb = [A.at(BIG + k * 4096, [128, D], BF16, f"win{k}") for k in range(KT)]
    wg_sb = A.at(AUX, [128, 4096], BF16, "wg_sb")
    sc = [A.at(XN + i * 4096, [128, T], F32, f"sc{i}") for i in range(8)] + \
         [A.at(AUX + i * 4096, [128, T], F32, f"sc{8 + i}") for i in range(2)]
    sci = A.at(AUX + 8192, [128, T], I32, "sci")
    hr = [A.at(BIG + i * 2048, [128, T], BF16, f"hr{i}") for i in range(8)]
    hi = [A.at(BIG + 16384 + i * 2048, [128, T], BF16, f"hi{i}") for i in range(8)]
    Pr = [A.at(BIG + 32768 + i * 2048, [128, T], BF16, f"Pr{i}") for i in range(8)]
    Pi = [A.at(BIG + 49152 + i * 2048, [128, T], BF16, f"Pi{i}") for i in range(8)]
    wo_sb = [A.at((XN + k * 4096) if k < 8 else (GA + (k - 8) * 4096), [128, D], BF16, f"wo{k}")
             for k in range(KT)]
    s0 = [A.at(XH + k * 2048, [128, T], BF16, f"s0_{k}") for k in range(KT)]

    tok_cp = prog.dma("sync", C.tile_f32[:], cpool32, sem="d_pre")
    prog.dma("sync", C.tile_bf16[:], cpool16, sem="d_pre")
    prog.dma("sync", ident[:], identf, sem="d_pre")
    prog.dma("sync", binT[:], bias_in, sem="d_pre")
    prog.dma("sync", boutT[:], bias_out, sem="d_pre")
    prog.dma("sync", bgT[:], bias_g, sem="d_pre")
    prog.dma("sync", arpT[:], arp, sem="d_pre")
    prog.dma("sync", aipT[:], aip, sem="d_pre")
    tok_pre = prog.dma("sync", maskT[:], mask8, sem="d_pre")

    tok_ones = prog.op("vector", lambda e: e.memset(ones_col[:], 1.0), inc="v")
    prog.op("vector", lambda e: e.memset(ones_row[:], 1.0), inc="v")
    prog.op("vector", lambda e: e.memset(onesT[:], 1.0), inc="v")
    tok_zero = prog.op("vector", lambda e: e.memset(zeroT[:], 0.0), inc="v")

    u0 = prog.op("scalar", lambda e: e.activation(sp_a[:], arpT[:], AF.Abs), after=[tok_pre, tok_cp], inc="s")
    u1 = prog.op("scalar", lambda e: e.activation(sp_a[:], sp_a[:], AF.Exp, scale=C.f32(-1.0)),
                 after=[u0], inc="s", force_wait=True)
    t1 = prog.op("scalar", lambda e: e.activation(sp_a[:], sp_a[:], AF.Ln, bias=C.f32(1.0)),
                 after=[u1], inc="s", force_wait=True)
    t2 = prog.op("vector", lambda e: e.tensor_relu(sp_b[:], arpT[:]), after=[tok_pre], inc="v")
    t3 = prog.op("vector", lambda e: e.tensor_add(sp_a[:], sp_a[:], sp_b[:]), after=[t1, t2], inc="v",
                 force_wait=True)
    prog.op("vector", lambda e: e.tensor_scalar(neg8sp[:], sp_a[:], -8.0, None, ALU.mult), after=[t3], inc="v",
             force_wait=True)
    tok_sp = prog.op("vector", lambda e: e.tensor_scalar(neg16sp[:], sp_a[:], -16.0, None, ALU.mult),
                     after=[t3], inc="v", force_wait=True)

    es_pstr = ExitStack()
    ps_trs = [es_pstr.enter_context(nc.psum_tensor(f"ps_tr{i}", [128, 512], F32)) for i in range(2)]
    xtm_free = [None, None]
    ev_tok = [None, None]
    tok_tr_last = None
    for p in range(4):
        toks = [prog.dma("sync", x_tm[j][:], x_slab[128 * (2 * p + j):128 * (2 * p + j + 1), :],
                         after=[xtm_free[j]], sem=f"d_x{j}") for j in range(2)]
        for k in range(KT):
            par = (p * KT + k) % 2
            mm = None
            for j in range(2):
                def f(e, k=k, j=j, par=par):
                    return nc.tensor.transpose(ps_trs[par][:, 128 * j:128 * (j + 1)],
                                               x_tm[j][:, 128 * k:128 * (k + 1)], ident[:])
                aft = [toks[j], tok_pre, ev_tok[par]]
                mm = prog.op("tensor", f, after=aft, inc="t" if j == 1 else None)
            if k == KT - 1:
                xtm_free[0] = xtm_free[1] = mm
            tok_tr_last = mm
            def fe(e, k=k, p=p, par=par):
                return e.tensor_copy(xT[k][:, 256 * p:256 * (p + 1)], ps_trs[par][:, 0:256])
            ev_tok[par] = prog.op("vector", fe, after=[mm], inc="v")
    tok_xT_done = [ev_tok[0], ev_tok[1]]

    es_psst = ExitStack()
    ps_st = [es_psst.enter_context(nc.psum_tensor(f"ps_st{i}", [128, 512], F32)) for i in range(4)]
    last_sum = {}
    for c in range(2):
        mm = None
        for k in range(KT):
            def f(e, k=k, c=c):
                return nc.tensor.matmul(ps_st[c][0:1, :], ones_col[:], xT[k][:, 512 * c:512 * (c + 1)],
                                        start=(k == 0), stop=(k == KT - 1))
            mm = prog.op("tensor", f, after=tok_xT_done + [tok_ones] if k == 0 else None,
                         inc="t" if k == KT - 1 else None)
        last_sum[c] = mm
    sq_last_mm = None
    sq_final = {}
    for k in range(KT):
        def f(e, k=k):
            return e.activation(sq[:], xT[k][:], AF.Square)
        tk = prog.op("scalar", f, after=tok_xT_done + ([sq_last_mm] if sq_last_mm else []), inc="s")
        for c in range(2):
            def f2(e, k=k, c=c):
                return nc.tensor.matmul(ps_st[2 + c][0:1, :], ones_col[:], sq[:, 512 * c:512 * (c + 1)],
                                        start=(k == 0), stop=(k == KT - 1))
            sq_last_mm = prog.op("tensor", f2, after=[tk], inc="t")
            if k == KT - 1:
                sq_final[c] = sq_last_mm

    ev = []
    ev.append(prog.op("vector", lambda e: e.tensor_copy(stat_sb[0:1, 0:512], ps_st[0][0:1, :]),
                      after=[last_sum[0]], inc="v"))
    ev.append(prog.op("vector", lambda e: e.tensor_copy(stat_sb[0:1, 512:1024], ps_st[1][0:1, :]),
                      after=[last_sum[1]], inc="v"))
    ev.append(prog.op("vector", lambda e: e.tensor_copy(stat_sb[0:1, T:T + 512], ps_st[2][0:1, :]),
                      after=[sq_final[0]], inc="v"))
    ev.append(prog.op("vector", lambda e: e.tensor_copy(stat_sb[0:1, T + 512:2 * T], ps_st[3][0:1, :]),
                      after=[sq_final[1]], inc="v"))

    t1 = prog.op("vector", lambda e: e.tensor_scalar(mrow[:], stat_sb[0:1, 0:T], 1.0 / D, None, ALU.mult),
                 after=ev[0:2], inc="v")
    t2 = prog.op("vector", lambda e: e.tensor_mul(varrow[:], mrow[:], mrow[:]), after=[t1], inc="v")
    t3 = prog.op("vector", lambda e: e.scalar_tensor_tensor(varrow[:], stat_sb[0:1, T:2 * T], 1.0 / D,
                                                            varrow[:], ALU.mult, ALU.subtract),
                 after=[t2] + ev[2:4], inc="v")
    t4 = prog.op("scalar", lambda e: e.activation(rstdrow[:], varrow[:], AF.Ln, bias=C.f32(1e-5)),
                 after=[t3], inc="s")
    t5 = prog.op("scalar", lambda e: e.activation(rstdrow[:], rstdrow[:], AF.Exp, scale=C.f32(-0.5)),
                 after=[t4], inc="s")
    t6 = prog.op("vector", lambda e: e.tensor_mul(nmr[:], mrow[:], rstdrow[:]), after=[t5], inc="v")
    t7 = prog.op("vector", lambda e: e.tensor_scalar(nmr[:], nmr[:], -1.0, None, ALU.mult), after=[t6], inc="v")
    tok_var = prog.dma("sync", var_out, varrow[:], after=[t3], sem="d_st")

    bc_toks = []
    for i, (row, dst, tok) in enumerate(((rstdrow, Brstd, t5), (nmr, Bneg, t7))):
        for c in range(2):
            ps = ps_st[2 * i + c]
            def f(e, row=row, c=c, ps=ps):
                return nc.tensor.matmul(ps[:], ones_row[:], row[0:1, 512 * c:512 * (c + 1)],
                                        start=True, stop=True)
            mmtok = prog.op("tensor", f, after=[tok] + ev, inc="t")
            def fe(e, dst=dst, c=c, ps=ps):
                return e.tensor_copy(dst[:, 512 * c:512 * (c + 1)], ps[:])
            bc_toks.append(prog.op("vector", fe, after=[mmtok], inc="v"))

    tok_xn = [None] * KT
    for k in range(KT):
        a1 = prog.op("vector", lambda e, k=k: e.tensor_mul(tscr[:], xT[k][:], Brstd[:]),
                     after=bc_toks, inc="v")
        tok_xn[k] = prog.op("vector", lambda e, k=k: e.tensor_add(xn[k][:], tscr[:], Bneg[:]),
                            after=[a1], inc="v")
        prog.dma("sync", xT_out[:, T * k:T * (k + 1)], xT[k][:], after=[tok_xn[k]], sem="d_st")
        prog.dma("sync", xn_out[:, T * k:T * (k + 1)], xn[k][:], after=[tok_xn[k]], sem="d_st")

    es_psst.close()
    es_pstr.close()
    tok_xfree = prog.barrier_tok("d_st")

    tok_win = [prog.dma("gpsimd", win_sb[k][:], w_in[128 * k:128 * (k + 1), :],
                        after=[tok_xfree, tok_tr_last], sem="d_w") for k in range(KT)]
    es_psmm = ExitStack()
    ps_mm = [es_psmm.enter_context(nc.psum_tensor(f"ps_mm{i}", [128, 512], F32)) for i in range(8)]
    ps_free = [None] * 8

    for c in range(2):
        for grp in range(2):
            mm_tok = [None] * 8
            for k in range(KT):
                for j in range(8):
                    n = grp * 8 + j
                    def f(e, k=k, j=j, n=n, c=c):
                        return nc.tensor.matmul(ps_mm[j][:], win_sb[k][:, 128 * n:128 * (n + 1)],
                                                xn[k][:, 512 * c:512 * (c + 1)],
                                                start=(k == 0), stop=(k == KT - 1))
                    aft = [tok_win[KT - 1], tok_xn[KT - 1], ps_free[j]] if k == 0 else None
                    mm_tok[j] = prog.op("tensor", f, after=aft, inc="t" if k == KT - 1 else None)
            for j in range(8):
                n = grp * 8 + j
                def fe(e, j=j, n=n, c=c):
                    return e.activation(xh[n][:, 512 * c:512 * (c + 1)], ps_mm[j][:], AF.Identity,
                                        bias=binT[:, n:n + 1])
                ps_free[j] = prog.op("scalar", fe, after=[mm_tok[j]] + xh_gate, inc="s")
    tok_xh = dict((n, ps_free[n % 8]) for n in range(KT))

    tok_wg = None
    for g in range(2):
        for b8 in range(8):
            for sub in range(2):
                col = (g * 16 + b8 * 2 + sub) * 128
                tok_wg = prog.dma("gpsimd", wg_sb[:, col:col + 128],
                                  wg[g, b8, 128 * sub:128 * (sub + 1), :],
                                  after=[tok_xfree], sem="d_w")
    tok_ga = [None] * 8
    tok_gx = [None] * 8
    for c in range(2):
        for g in range(2):
            for b8 in range(8):
                pj = b8
                mm = None
                for sub in range(2):
                    col = (g * 16 + b8 * 2 + sub) * 128
                    def f(e, b8=b8, sub=sub, col=col, c=c, pj=pj):
                        return nc.tensor.matmul(ps_mm[pj][:], wg_sb[:, col:col + 128],
                                                xh[2 * b8 + sub][:, 512 * c:512 * (c + 1)],
                                                start=(sub == 0), stop=(sub == 1))
                    aft = [tok_wg, tok_xh[2 * b8 + 1], ps_free[pj]] if sub == 0 else None
                    mm = prog.op("tensor", f, after=aft, inc="t" if sub == 1 else None)
                dst, dtok = (gx, tok_gx) if g == 0 else (ga, tok_ga)
                def fe(e, b8=b8, g=g, c=c, dst=dst, pj=pj):
                    return e.activation(dst[b8][:, 512 * c:512 * (c + 1)], ps_mm[pj][:], AF.Sigmoid,
                                        bias=bgT[:, g * 8 + b8:g * 8 + b8 + 1])
                ps_free[pj] = prog.op("scalar", fe, after=[mm], inc="s")
                dtok[b8] = ps_free[pj]
    tok_lastwinmm = prog.barrier_tok("t")

    INV2PI = 1.0 / (2 * np.pi)
    scan_out_toks = []
    prev_tile_last_v = prog.barrier_tok("v")
    first_gate = [tok_lastwinmm]
    for i in range(8):
        mag, msq, phi, Phi, w1, w2, snt, cst, ur, ui = sc
        ta = prog.op("scalar", lambda e, i=i: e.activation(mag[:], ga[i][:], AF.Exp, scale=neg8sp[:, i:i + 1]),
                     after=[tok_ga[i], tok_sp, prev_tile_last_v] + first_gate, inc="s")
        tb = prog.op("scalar", lambda e, i=i: e.activation(msq[:], ga[i][:], AF.Exp, scale=neg16sp[:, i:i + 1]),
                     inc="s")
        tc = prog.op("scalar", lambda e: e.activation(w1[:], msq[:], AF.Ln, scale=C.f32(-1.0), bias=C.f32(1.0)),
                     after=[tb], inc="s")
        tnrm = prog.op("scalar", lambda e: e.activation(w1[:], w1[:], AF.Exp, scale=C.f32(0.5)), after=[tc], inc="s")
        tphi = prog.op("vector", lambda e, i=i: e.tensor_scalar(phi[:], ga[i][:], aipT[:, i:i + 1], None, ALU.mult),
                       after=[tok_ga[i], tok_pre, prev_tile_last_v] + first_gate, inc="v")
        tPhi0 = prog.op("vector", lambda e: e.tensor_tensor_scan(Phi[:, H0], onesT[:, H0], phi[:, H0], 0.0,
                                                                 ALU.mult, ALU.add), after=[tphi, tok_ones], inc="v")
        tPhi = prog.op("vector", lambda e: e.tensor_tensor_scan(Phi[:, H1], onesT[:, H1], phi[:, H1], 0.0,
                                                                ALU.mult, ALU.add), after=[tPhi0], inc="v")
        r1 = prog.op("vector", lambda e: e.tensor_scalar(sci[:], Phi[:], INV2PI, None, ALU.mult),
                     after=[tPhi], inc="v")
        r2 = prog.op("vector", lambda e: e.tensor_copy(w2[:], sci[:]), after=[r1], inc="v")
        r3 = prog.op("vector", lambda e: e.scalar_tensor_tensor(w2[:], w2[:], -2 * np.pi, Phi[:],
                                                                ALU.mult, ALU.add), after=[r2], inc="v")
        tsin = prog.op("scalar", lambda e: e.activation(snt[:], w2[:], AF.Sin), after=[r3], inc="s")
        r4 = prog.op("vector", lambda e: e.tensor_scalar(sci[:], Phi[:], INV2PI, 0.25, ALU.mult, ALU.add),
                     after=[r3], inc="v")
        r5 = prog.op("vector", lambda e: e.tensor_copy(w2[:], sci[:]), after=[r4, tsin], inc="v")
        r6 = prog.op("vector", lambda e: e.scalar_tensor_tensor(w2[:], w2[:], -2 * np.pi, Phi[:],
                                                                ALU.mult, ALU.add), after=[r5], inc="v")
        tcos = prog.op("scalar", lambda e: e.activation(cst[:], w2[:], AF.Sin, bias=C.f32(np.pi / 2)),
                       after=[r6], inc="s")
        tgxn = prog.op("vector", lambda e, i=i: e.tensor_mul(phi[:], gx[i][:], w1[:]),
                       after=[tnrm, tPhi, tok_gx[i]], inc="v")
        txr = prog.op("vector", lambda e, i=i: e.tensor_mul(w2[:], phi[:], xh[i][:]),
                      after=[tgxn, r6, tcos], inc="v")
        txi = prog.op("vector", lambda e, i=i: e.tensor_mul(w1[:], phi[:], xh[8 + i][:]),
                      after=[tgxn], inc="v")
        q1 = prog.op("vector", lambda e: e.tensor_mul(ur[:], cst[:], w2[:]), after=[txr, tcos], inc="v")
        q2 = prog.op("vector", lambda e: e.tensor_mul(phi[:], snt[:], w1[:]), after=[txi, tsin], inc="v")
        q3 = prog.op("vector", lambda e: e.tensor_add(ur[:], ur[:], phi[:]), after=[q1, q2], inc="v")
        q4 = prog.op("vector", lambda e: e.tensor_mul(ui[:], cst[:], w1[:]), after=[q2], inc="v")
        q5 = prog.op("vector", lambda e: e.tensor_mul(phi[:], snt[:], w2[:]), after=[q4, q1], inc="v")
        q6 = prog.op("vector", lambda e: e.tensor_sub(ui[:], ui[:], phi[:]), after=[q5], inc="v")
        s1 = prog.op("vector", lambda e: e.tensor_tensor_scan(w2[:, H0], mag[:, H0], ur[:, H0], 0.0,
                                                              ALU.mult, ALU.add), after=[q3, q6, ta], inc="v")
        s2 = prog.op("vector", lambda e: e.tensor_tensor_scan(w2[:, H1], mag[:, H1], ur[:, H1], 0.0,
                                                              ALU.mult, ALU.add), after=[s1], inc="v")
        s3 = prog.op("vector", lambda e: e.tensor_tensor_scan(w1[:, H0], mag[:, H0], ui[:, H0], 0.0,
                                                              ALU.mult, ALU.add), after=[s2], inc="v")
        s4 = prog.op("vector", lambda e: e.tensor_tensor_scan(w1[:, H1], mag[:, H1], ui[:, H1], 0.0,
                                                              ALU.mult, ALU.add), after=[s3], inc="v")
        s5 = prog.op("vector", lambda e: e.tensor_tensor_scan(phi[:, H0], mag[:, H0], zeroT[:, H0], 1.0,
                                                              ALU.mult, ALU.add), after=[s4, tok_zero], inc="v")
        s6 = prog.op("vector", lambda e: e.tensor_tensor_scan(phi[:, H1], mag[:, H1], zeroT[:, H1], 1.0,
                                                              ALU.mult, ALU.add), after=[s5], inc="v")
        z1 = prog.op("vector", lambda e: e.tensor_mul(ur[:], cst[:], w2[:]), after=[s4], inc="v")
        z2 = prog.op("vector", lambda e: e.tensor_mul(ui[:], snt[:], w1[:]), after=[s4], inc="v")
        z3 = prog.op("vector", lambda e, i=i: e.tensor_sub(hr[i][:], ur[:], ui[:]), after=[z1, z2], inc="v")
        z4 = prog.op("vector", lambda e: e.tensor_mul(ur[:], cst[:], w1[:]), after=[z2, z3], inc="v")
        z5 = prog.op("vector", lambda e: e.tensor_mul(ui[:], snt[:], w2[:]), after=[z1, z4], inc="v")
        z6 = prog.op("vector", lambda e, i=i: e.tensor_add(hi[i][:], ur[:], ui[:]), after=[z4, z5], inc="v")
        z7 = prog.op("vector", lambda e, i=i: e.tensor_mul(Pr[i][:], cst[:], phi[:]), after=[s6], inc="v")
        z8 = prog.op("vector", lambda e, i=i: e.tensor_mul(Pi[i][:], snt[:], phi[:]), after=[s6], inc="v")
        z9 = prog.op("vector", lambda e, i=i: e.tensor_copy(sumb[:, 2 * i:2 * i + 2],
                                                            hr[i][:, SC - 1::SC]), after=[z3], inc="v")
        z10 = prog.op("vector", lambda e, i=i: e.tensor_copy(sumb[:, 16 + 2 * i:16 + 2 * i + 2],
                                                             hi[i][:, SC - 1::SC]), after=[z6], inc="v")
        z11 = prog.op("vector", lambda e, i=i: e.tensor_copy(sumb[:, 32 + 2 * i:32 + 2 * i + 2],
                                                             Pr[i][:, SC - 1::SC]), after=[z7], inc="v")
        z12 = prog.op("vector", lambda e, i=i: e.tensor_copy(sumb[:, 48 + 2 * i:48 + 2 * i + 2],
                                                             Pi[i][:, SC - 1::SC]), after=[z8], inc="v")
        scan_out_toks += [z9, z10, z11, z12]
        prev_tile_last_v = z12
        first_gate = []

    tok_sum = prog.dma("sync", cc_in, sumb[:], after=scan_out_toks, sem="d_cc")
    tok_cc = prog.op("gpsimd", lambda e: e.collective_compute(
        "AllGather", ALU.bypass, replica_groups=[list(range(NCORES))],
        ins=[cc_in.opt()], outs=[cc_out.opt()]), after=[tok_sum], inc="cc")
    tok_gat = [prog.dma("sync", gat[:, 64 * r:64 * (r + 1)], cc_out[128 * r:128 * (r + 1), :],
                        after=[tok_cc], sem="d_cc") for r in range(NCORES)]

    tprev = prog.op("vector", lambda e: e.memset(Hst[:], 0.0), after=tok_gat + [tok_sum], inc="v")
    for r in range(NCORES):
        g0 = 64 * r
        m_ap = maskT[:, r:r + 1]
        ops = [
            lambda e, g0=g0, m_ap=m_ap: e.tensor_scalar(
                w_a[:, 0:16], gat[:, g0:g0 + 16], m_ap, None, ALU.mult),
            lambda e, g0=g0, m_ap=m_ap: e.tensor_scalar(
                w_a[:, 16:32], gat[:, g0 + 16:g0 + 32], m_ap, None, ALU.mult),
            lambda e, g0=g0, m_ap=m_ap: e.tensor_scalar(
                w_a[:, 32:48], gat[:, g0 + 32:g0 + 48], m_ap, 1.0, ALU.mult, ALU.add),
            lambda e, m_ap=m_ap: e.tensor_scalar(
                w_a[:, 32:48], w_a[:, 32:48], m_ap, None, ALU.subtract),
            lambda e, g0=g0, m_ap=m_ap: e.tensor_scalar(
                w_a[:, 48:64], gat[:, g0 + 48:g0 + 64], m_ap, None, ALU.mult),
            lambda e: e.tensor_mul(scr4[:, 0:16], w_a[:, 32:48], Hst[:, 0:16]),
            lambda e: e.tensor_mul(scr4[:, 16:32], w_a[:, 48:64], Hst[:, 16:32]),
            lambda e: e.tensor_mul(scr4[:, 32:48], w_a[:, 32:48], Hst[:, 16:32]),
            lambda e: e.tensor_mul(scr4[:, 48:64], w_a[:, 48:64], Hst[:, 0:16]),
            lambda e: e.tensor_sub(Hst[:, 0:16], scr4[:, 0:16], scr4[:, 16:32]),
            lambda e: e.tensor_add(Hst[:, 0:16], Hst[:, 0:16], w_a[:, 0:16]),
            lambda e: e.tensor_add(Hst[:, 16:32], scr4[:, 32:48], scr4[:, 48:64]),
            lambda e: e.tensor_add(Hst[:, 16:32], Hst[:, 16:32], w_a[:, 16:32]),
        ]
        for fn in ops:
            tprev = prog.op("vector", fn, after=[tprev], inc="v", force_wait=True)
    tok_H = tprev
    tok_nH = prog.op("vector", lambda e: e.tensor_scalar(negHi[:], Hst[:, 16:32], -1.0, None, ALU.mult),
                     after=[tok_H], inc="v", force_wait=True)

    tok_fix = None
    for i in range(8):
        for b in range(2):
            hb = [H0, H1][b]
            col = 2 * i + b
            prog.op("vector", lambda e, i=i, hb=hb, col=col: e.scalar_tensor_tensor(
                hr[i][:, hb], Pr[i][:, hb], Hst[:, col:col + 1], hr[i][:, hb], ALU.mult, ALU.add),
                after=[tok_nH], inc="v", force_wait=True)
            prog.op("vector", lambda e, i=i, hb=hb, col=col: e.scalar_tensor_tensor(
                hr[i][:, hb], Pi[i][:, hb], negHi[:, col:col + 1], hr[i][:, hb], ALU.mult, ALU.add), inc="v")
            prog.op("vector", lambda e, i=i, hb=hb, col=col: e.scalar_tensor_tensor(
                hi[i][:, hb], Pr[i][:, hb], Hst[:, 16 + col:16 + col + 1], hi[i][:, hb], ALU.mult, ALU.add),
                inc="v")
            tok_fix = prog.op("vector", lambda e, i=i, hb=hb, col=col: e.scalar_tensor_tensor(
                hi[i][:, hb], Pi[i][:, hb], Hst[:, col:col + 1], hi[i][:, hb], ALU.mult, ALU.add), inc="v")

    gate_wo = prog.barrier_tok("v")
    tok_wo = [prog.dma("gpsimd", wo_sb[k][:], w_out[128 * k:128 * (k + 1), :],
                       after=[tok_lastwinmm, gate_wo], sem="d_w") for k in range(KT)]
    hcat = hr + hi
    evac_tok = {}
    for c in range(2):
        for grp in range(2):
            mm_tok = [None] * 8
            for k in range(KT):
                for j in range(8):
                    n = grp * 8 + j
                    def f(e, k=k, j=j, n=n, c=c):
                        return nc.tensor.matmul(ps_mm[j][:], wo_sb[k][:, 128 * n:128 * (n + 1)],
                                                hcat[k][:, 512 * c:512 * (c + 1)],
                                                start=(k == 0), stop=(k == KT - 1))
                    aft = [tok_wo[KT - 1], tok_fix, ps_free[j]] if k == 0 else None
                    mm_tok[j] = prog.op("tensor", f, after=aft, inc="t" if k == KT - 1 else None)
            for j in range(8):
                n = grp * 8 + j
                def fe(e, j=j, n=n, c=c):
                    return e.activation(s0[n][:, 512 * c:512 * (c + 1)], ps_mm[j][:], AF.Identity,
                                        bias=boutT[:, n:n + 1])
                ps_free[j] = prog.op("scalar", fe, after=[mm_tok[j]], inc="s")
                evac_tok[(c, n)] = ps_free[j]
    for k in range(KT):
        prog.dma("sync", s0_out[:, T * k:T * (k + 1)], s0[k][:], after=[evac_tok[(1, k)]], sem="d_st")

    prog.op("sync", None, after=[prog.barrier_tok("d_st")])

    es_psmm.close()


def build_p1():
    nc = bass.Bass("TRN2", target_bir_lowering=False)
    stack = ExitStack()

    x_slab = nc.dram_tensor("x_slab", [T, D], F32, kind="ExternalInput").ap()
    w_in = nc.dram_tensor("w_in", [D, D], F32, kind="ExternalInput").ap()
    wg = nc.dram_tensor("wg", [2, 8, 256, 128], F32, kind="ExternalInput").ap()
    w_out = nc.dram_tensor("w_out", [D, D], F32, kind="ExternalInput").ap()
    bias_in = nc.dram_tensor("bias_in", [128, KT], F32, kind="ExternalInput").ap()
    bias_out = nc.dram_tensor("bias_out", [128, KT], F32, kind="ExternalInput").ap()
    bias_g = nc.dram_tensor("bias_g", [128, 16], F32, kind="ExternalInput").ap()
    arp = nc.dram_tensor("arp", [128, 8], F32, kind="ExternalInput").ap()
    aip = nc.dram_tensor("aip", [128, 8], F32, kind="ExternalInput").ap()
    identf = nc.dram_tensor("identf", [128, 128], F32, kind="ExternalInput").ap()
    cpool32 = nc.dram_tensor("cpool32", [128, 64], F32, kind="ExternalInput").ap()
    cpool16 = nc.dram_tensor("cpool16", [128, 64], BF16, kind="ExternalInput").ap()
    mask8 = nc.dram_tensor("mask8", [128, 8], F32, kind="ExternalInput").ap()

    xT_out = nc.dram_tensor("xT_out", [128, KT * T], F32, kind="ExternalOutput").ap()
    xn_out = nc.dram_tensor("xn_out", [128, KT * T], BF16, kind="ExternalOutput").ap()
    s0_out = nc.dram_tensor("s0_out", [128, KT * T], BF16, kind="ExternalOutput").ap()
    var_out = nc.dram_tensor("var_out", [1, T], F32, kind="ExternalOutput").ap()

    cc_in = nc.dram_tensor("cc_in", [128, 64], F32, kind="Internal").ap()
    cc_out = nc.dram_tensor("cc_out", [NCORES * 128, 64], F32, kind="Internal",
                            addr_space="Shared").ap()

    DR = {
        "x_slab": x_slab,
        "w_in": w_in,
        "wg": wg,
        "w_out": w_out,
        "bias_in": bias_in,
        "bias_out": bias_out,
        "bias_g": bias_g,
        "arp": arp,
        "aip": aip,
        "identf": identf,
        "cpool32": cpool32,
        "cpool16": cpool16,
        "mask8": mask8,
        "xT_out": xT_out,
        "xn_out": xn_out,
        "s0_out": s0_out,
        "var_out": var_out,
        "cc_in": cc_in,
        "cc_out": cc_out,
    }
    with stack:
        prog = Prog(nc, stack)
        C = Consts(nc, stack)
        A = Arena(nc, stack, 206 * 1024)
        emit_p1(nc, stack, prog, C, A, DR)
        block = stack.enter_context(nc.Block())
        prog.replay_into(block)
        consts = C.host_arrays()
    return nc, consts

def _slab(x_full, c):
    return np.ascontiguousarray(
        np.concatenate([x_full[0, c * SC:(c + 1) * SC], x_full[1, c * SC:(c + 1) * SC]], axis=0))


def _tilevec(v):
    v = np.asarray(v, np.float32)
    n = v.shape[0] // 128
    return np.ascontiguousarray(v.reshape(n, 128).T)


def p1_inputs(inputs, consts):
    x = np.asarray(inputs["L0_attn_x"], np.float32)
    ident = np.eye(128, dtype=np.float32)
    wgarr = np.ascontiguousarray(np.stack([np.asarray(inputs["L0_attn_Wg_x"], np.float32),
                                           np.asarray(inputs["L0_attn_Wg_a"], np.float32)], axis=0))
    bias_g = np.ascontiguousarray(np.concatenate(
        [_tilevec(inputs["L0_attn_bg_x"]), _tilevec(inputs["L0_attn_bg_a"])], axis=1))
    a32, a16 = consts
    base = {
        "w_in": np.ascontiguousarray(np.asarray(inputs["L0_attn_W_in"], np.float32)),
        "wg": wgarr,
        "w_out": np.ascontiguousarray(np.asarray(inputs["L0_attn_W_out"], np.float32)),
        "bias_in": _tilevec(inputs["L0_attn_b_in"]),
        "bias_out": _tilevec(inputs["L0_attn_b_out"]),
        "bias_g": bias_g,
        "arp": _tilevec(inputs["L0_attn_a_real_param"]),
        "aip": _tilevec(inputs["L0_attn_a_imag_param"]),
        "identf": ident,
        "cpool32": a32,
        "cpool16": a16,
    }
    per_core = []
    for c in range(NCORES):
        d = dict(base)
        d["x_slab"] = _slab(x, c)
        mask = np.zeros((128, 8), np.float32)
        mask[:, :c] = 1.0
        d["mask8"] = mask
        per_core.append(d)
    return per_core


def run_p1(inputs):
    nc, consts = build_p1()
    per_core = p1_inputs(inputs, consts)
    return run_bass_kernel_spmd(nc, per_core, core_ids=list(range(NCORES)))





def emit_p2(nc, stack, prog, C, A, DR):
    NT = INTER // 128
    NCH = 16
    KCH = 8
    xn_in = DR["xn_in"]
    xT_in = DR["xT_in"]
    s0_in = DR["s0_in"]
    var_in = DR["var_in"]
    w1t = DR["w1t"]
    w2t = DR["w2t"]
    ccw_out = DR["ccw_out"]
    ho_sw = DR["ho_sw"]
    tok_ccw = DR.get("_tok_ccw")
    identf = DR["identf"]
    cpool32 = DR["cpool32"]
    cpool16 = DR["cpool16"]
    x1_out = DR["x1_out"]
    hmid = DR["hmid"]
    cc2_in = DR["cc2_in"]
    cc2_out = DR["cc2_out"]
    cur = [0]

    def al(shape, dtype, name=None):
        per = int(np.prod(shape[1:])) * mybir.dt.size(dtype)
        off = cur[0]
        cur[0] += (per + 31) // 32 * 32
        return A.at(off, shape, dtype, name)

    ident = al([128, 128], F32, "ident")
    ones_col = al([128, 1], F32, "ones_col")
    ones_row = al([1, 128], F32, "ones_row")
    swsum = al([1, 16], F32, "swsum")
    swacc = al([128, 2], F32, "swacc")
    swrow = al([128, 4], F32, "swrow")
    redcol = al([128, 1], F32, "redcol")
    OFF = {}

    def mark(name):
        OFF[name] = cur[0]
    mark("varr")
    varr = al([1, T], F32, "varr")
    rms1 = al([1, T], F32, "rms1")
    mark("rowa")
    rowa = al([1, T], F32, "rowa")
    mark("rowb")
    rowb = al([1, T], F32, "rowb")
    mark("c1r")
    c1r = al([1, T], F32, "c1r")
    r1r = al([1, T], F32, "r1r")
    c2r = al([1, T], F32, "c2r")
    r2r = al([1, T], F32, "r2r")
    mabs_r = al([1, T], F32, "mabs_r")
    s2row = al([1, T], F32, "s2row")
    C1B = al([128, T], F32, "C1B")
    R1B = al([128, T], F32, "R1B")
    C2B = al([128, T], F32, "C2B")
    R2B = al([128, T], F32, "R2B")
    mark("macc")
    macc = al([128, T], F32, "macc")
    tsc1 = al([128, T], F32, "tsc1")
    tsc2 = al([128, T], F32, "tsc2")
    tsc3 = al([128, 512], F32, "tsc3")
    i8s = al([128, T], I8, "i8s")
    wclip = al([128, T], F32, "wclip")
    sldA = A.at(OFF["macc"], [128, T], F32, "sldA")
    sldB = A.at(OFF["rowa"], [128, T], BF16, "sldB")
    hst = [A.at(OFF["rowb"] + i * 1024, [128, 512], BF16, f"hst{i}") for i in range(2)]
    hld = [A.at(OFF["c1r"] + i * 2048, [128, T], BF16, f"hld{i}") for i in range(2)]
    W0 = (cur[0] + 63) // 64 * 64

    XN = W0
    XQ = W0 + 32 * 1024
    WB = W0 + 64 * 1024
    HB = W0 + 112 * 1024
    WEND = W0 + 128 * 1024
    assert WEND <= 206 * 1024, (WEND, W0)

    xn = [A.at(XN + k * 2048, [128, T], BF16, f"xn{k}") for k in range(KT)]
    xq2 = [A.at(XN + k * 2048, [128, T], BF16, f"xq2_{k}") for k in range(16)]
    gt = [A.at(WB + k * 2048, [128, T], BF16, f"gt{k}") for k in range(KT)]
    xq = [A.at(XQ + k * 2048, [128, T], BF16, f"xq{k}") for k in range(KT)]
    y2a = [A.at(XQ + k * 2048, [128, T], BF16, f"y2a{k}") for k in range(KT)]
    w1qb = [A.at(WB + b * 16 * 1024, [128, 8192], BF16, f"w1qb{b}") for b in range(2)]
    w2qb = [A.at(WB + b * 32 * 1024, [128, 16384], BF16, f"w2qb{b}") for b in range(2)]

    tok_cp = prog.dma("sync", C.tile_f32[:], cpool32, sem="d_pre")
    prog.dma("sync", C.tile_bf16[:], cpool16, sem="d_pre")
    prog.dma("sync", ident[:], identf, sem="d_pre")
    tok_var = prog.dma("sync", varr[:], var_in, sem="d_pre")
    tok_ones = prog.op("vector", lambda e: e.memset(ones_col[:], 1.0), inc="v")
    prog.op("vector", lambda e: e.memset(ones_row[:], 1.0), inc="v")

    es_ps = ExitStack()
    ps = [es_ps.enter_context(nc.psum_tensor(f"q2ps{i}", [128, 512], F32)) for i in range(8)]
    psfree = [None] * 8

    tok_swl = prog.dma("sync", swsum[:], ho_sw, sem="d_cc")
    tok_swb = tok_swl

    tok_xn = [prog.dma("sync", xn[k][:], xn_in[:, T * k:T * (k + 1)], sem="d_x") for k in range(KT)]
    pm = prog.op("vector", lambda e: e.tensor_tensor(macc[:], xn[0][:], xn[1][:], ALU.max),
                 after=[tok_xn[0], tok_xn[1]], inc="v")
    nm = prog.op("vector", lambda e: e.tensor_tensor(tsc2[:], xn[0][:], xn[1][:], ALU.min),
                 after=[tok_xn[0], tok_xn[1]], inc="v")
    for k in range(2, KT):
        pm = prog.op("vector", lambda e, k=k: e.tensor_tensor(macc[:], macc[:], xn[k][:], ALU.max),
                     after=[tok_xn[k], pm], inc="v", force_wait=True)
        nm = prog.op("vector", lambda e, k=k: e.tensor_tensor(tsc2[:], tsc2[:], xn[k][:], ALU.min),
                     after=[tok_xn[k], nm], inc="v", force_wait=True)
    nm2 = prog.op("vector", lambda e: e.tensor_scalar(tsc2[:], tsc2[:], -1.0, None, ALU.mult),
                  after=[nm], inc="v", force_wait=True)
    pm = prog.op("vector", lambda e: e.tensor_tensor(macc[:], macc[:], tsc2[:], ALU.max),
                 after=[pm, nm2], inc="v", force_wait=True)
    prevT = pm
    for ch in range(8):
        mmt = prog.op("tensor", lambda e, ch=ch: nc.tensor.transpose(
            ps[7][:, 0:128], macc[:, 128 * ch:128 * (ch + 1)], ident[:]),
            after=[prevT, tok_cp, psfree[7]], inc="t")
        rd = prog.op("vector", lambda e: e.reduce_max(redcol[:], ps[7][:, 0:128], axis=AX.X),
                     after=[mmt], inc="v")
        mmt2 = prog.op("tensor", lambda e: nc.tensor.transpose(ps[7][0:1, 128:256], redcol[:], ident[:]),
                       after=[rd], inc="t")
        prevT = prog.op("vector", lambda e, ch=ch: e.tensor_copy(mabs_r[0:1, 128 * ch:128 * (ch + 1)],
                                                                 ps[7][0:1, 128:256]), after=[mmt2], inc="v")
        psfree[7] = prevT
    tok_mabs = prevT
    v1 = prog.op("scalar", lambda e: e.activation(rowa[:], varr[:], AF.Ln, bias=C.f32(1e-5)),
                 after=[tok_var, tok_cp], inc="s")
    v2 = prog.op("scalar", lambda e: e.activation(rowb[:], varr[:], AF.Ln), after=[v1], inc="s",
                 force_wait=True)
    v3 = prog.op("vector", lambda e: e.tensor_sub(rms1[:], rowa[:], rowb[:]), after=[v2], inc="v")
    v4 = prog.op("scalar", lambda e: e.activation(rms1[:], rms1[:], AF.Exp, scale=C.f32(0.5)),
                 after=[v3], inc="s")
    q1 = prog.op("vector", lambda e: e.tensor_mul(rowa[:], rms1[:], mabs_r[:]), after=[v4, tok_mabs],
                 inc="v", force_wait=True)
    q2 = prog.op("vector", lambda e: e.tensor_scalar(rowa[:], rowa[:], 1e-5, None, ALU.max),
                 after=[q1], inc="v", force_wait=True)
    q3 = prog.op("vector", lambda e: e.reciprocal(rowb[:], rowa[:]), after=[q2], inc="v", force_wait=True)
    q4 = prog.op("vector", lambda e: e.tensor_mul(c1r[:], rms1[:], rowb[:]), after=[q3], inc="v",
                 force_wait=True)
    q5 = prog.op("vector", lambda e: e.tensor_scalar(c1r[:], c1r[:], 127.0, None, ALU.mult),
                 after=[q4], inc="v", force_wait=True)
    q6 = prog.op("vector", lambda e: e.tensor_scalar(r1r[:], rowa[:], 1.0 / 127.0, None, ALU.mult),
                 after=[q5], inc="v", force_wait=True)
    q7 = prog.op("vector", lambda e: e.tensor_scalar(r1r[:], r1r[:], swsum[0:1, 2:3], None, ALU.mult),
                 after=[q6, tok_swb], inc="v", force_wait=True)
    bc_toks = []
    for (row, dst, tok) in ((c1r, C1B, q5), (r1r, R1B, q7)):
        for cc_ in range(2):
            mmb = prog.op("tensor", lambda e, row=row, cc_=cc_: nc.tensor.matmul(
                ps[6][:], ones_row[:], row[0:1, 512 * cc_:512 * (cc_ + 1)], start=True, stop=True),
                after=[tok, psfree[6]], inc="t")
            tkb = prog.op("vector", lambda e, dst=dst, cc_=cc_: e.tensor_copy(
                dst[:, 512 * cc_:512 * (cc_ + 1)], ps[6][:]), after=[mmb], inc="v")
            psfree[6] = tkb
            bc_toks.append(tkb)
    tok_xq = [None] * KT
    for k in range(KT):
        qa = prog.op("vector", lambda e, k=k: e.tensor_tensor(i8s[:], xn[k][:], C1B[:], ALU.mult),
                     after=bc_toks + [tok_xn[k]], inc="v")
        tok_xq[k] = prog.op("vector", lambda e, k=k: e.tensor_copy(xq[k][:], i8s[:]),
                            after=[qa], inc="v", force_wait=True)

    tok_hacc0 = prog.op("vector", lambda e: e.memset(macc[:], 0.0), after=[tok_mabs], inc="v")
    hacc_tok = tok_hacc0
    sq_started = [False, False]
    sq_tok = {0: None, 1: None}
    w1buf_tok = [None, None]
    w1buf_mm = [None, None]
    last_mm_of_chunk = None
    tsc1_free = None
    tsc2_free = None
    hst_free = [None, None]
    hstore_all = []
    for ch in range(NCH):
        wb = ch % 2
        rnk = ch // 2
        half = ch % 2
        src_ap = ccw_out[128 * rnk:128 * (rnk + 1), 0:16384].rearrange(
            "p (kt x) -> p kt x", kt=KT)[:, :, 512 * half:512 * (half + 1)]
        w1buf_tok[wb] = prog.dma("gpsimd", w1qb[wb][:], src_ap,
                                 after=[w1buf_mm[wb], tok_ccw], sem=f"d_wf{wb}")
        for tc in range(2):
            mmt = [None] * 4
            for k in range(KT):
                for j in range(4):
                    def f(e, k=k, j=j, tc=tc, wb=wb):
                        return nc.tensor.matmul(ps[j][:], w1qb[wb][:, 512 * k + 128 * j:512 * k + 128 * (j + 1)],
                                                xq[k][:, 512 * tc:512 * (tc + 1)],
                                                start=(k == 0), stop=(k == KT - 1))
                    aft = [w1buf_tok[wb]]
                    if k == 0:
                        aft += [tok_xq[KT - 1], psfree[j]]
                    mmt[j] = prog.op("tensor", f, after=aft, inc="t" if k == KT - 1 else None)
            last_mm_of_chunk = mmt[3]
            w1buf_mm[wb] = mmt[3]
            for j in range(4):
                it = ch * 4 + j
                e1 = prog.op("vector", lambda e, j=j, tc=tc: e.tensor_tensor(
                    tsc1[:, 0:512], ps[j][:], R1B[:, 512 * tc:512 * (tc + 1)], ALU.mult),
                    after=[mmt[j], tsc1_free], inc="v")
                psfree[j] = e1
                e2 = prog.op("scalar", lambda e: e.activation(tsc2[:, 0:512], tsc1[:, 0:512], AF.Gelu),
                             after=[e1, tsc2_free], inc="s")
                tsc1_free = e2
                e3 = prog.op("scalar", lambda e: e.activation(tsc2[:, 0:512], tsc2[:, 0:512], AF.Gelu),
                             after=[e2], inc="s", force_wait=True)
                hb = it % 2
                e5 = prog.op("vector", lambda e, hb=hb: e.tensor_copy(hst[hb][:], tsc2[:, 0:512]),
                             after=[e3, hst_free[hb]], inc="v")
                tkst = prog.dma("sync", hmid[:, T * it + 512 * tc:T * it + 512 * (tc + 1)],
                                hst[hb][:], after=[e5], sem="d_h")
                hst_free[hb] = tkst
                hstore_all.append(tkst)
                e6 = prog.op("gpsimd", lambda e: e.tensor_mul(tsc3[:, 0:512], tsc2[:, 0:512],
                                                              tsc2[:, 0:512]), after=[e3], inc="p")
                mmq = prog.op("tensor", lambda e, tc=tc, last=(ch == NCH - 1 and j == 3): nc.tensor.matmul(
                    ps[4 + tc][0:1, :], ones_col[:], tsc3[:, 0:512],
                    start=(not sq_started[tc]), stop=last), after=[e6], inc="t")
                sq_started[tc] = True
                sq_tok[tc] = mmq
                hacc_tok = prog.op("vector", lambda e, tc=tc: e.tensor_tensor(
                    macc[:, 512 * tc:512 * (tc + 1)], macc[:, 512 * tc:512 * (tc + 1)],
                    tsc3[:, 0:512], ALU.max), after=[e3, hacc_tok, e6], inc="v", force_wait=True)
                tsc2_free = hacc_tok

    ev_s2 = []
    ev_s2.append(prog.op("vector", lambda e: e.tensor_copy(s2row[0:1, 0:512], ps[4][0:1, :]),
                         after=[sq_tok[0]], inc="v"))
    ev_s2.append(prog.op("vector", lambda e: e.tensor_copy(s2row[0:1, 512:1024], ps[5][0:1, :]),
                         after=[sq_tok[1]], inc="v"))
    psfree[4] = ev_s2[0]
    psfree[5] = ev_s2[1]
    prevT = hacc_tok
    for ch in range(8):
        mmt = prog.op("tensor", lambda e, ch=ch: nc.tensor.transpose(
            ps[7][:, 0:128], macc[:, 128 * ch:128 * (ch + 1)], ident[:]),
            after=[prevT, psfree[7]], inc="t")
        rd = prog.op("vector", lambda e: e.reduce_max(redcol[:], ps[7][:, 0:128], axis=AX.X),
                     after=[mmt], inc="v")
        mmt2 = prog.op("tensor", lambda e: nc.tensor.transpose(ps[7][0:1, 128:256], redcol[:], ident[:]),
                       after=[rd], inc="t")
        prevT = prog.op("vector", lambda e, ch=ch: e.tensor_copy(mabs_r[0:1, 128 * ch:128 * (ch + 1)],
                                                                 ps[7][0:1, 128:256]), after=[mmt2], inc="v")
        psfree[7] = prevT
    tok_m2a = prevT
    tok_m2b = prog.op("scalar", lambda e: e.activation(mabs_r[:], mabs_r[:], AF.Ln),
                      after=[tok_m2a], inc="s")
    tok_m2 = prog.op("scalar", lambda e: e.activation(mabs_r[:], mabs_r[:], AF.Exp, scale=C.f32(0.5)),
                     after=[tok_m2b], inc="s", force_wait=True)
    g1 = prog.op("scalar", lambda e: e.activation(rowa[:], s2row[:], AF.Ln), after=ev_s2 + [hstore_all[-1]], inc="s")
    g2 = prog.op("scalar", lambda e: e.activation(rowa[:], rowa[:], AF.Exp, scale=C.f32(0.5)),
                 after=[g1], inc="s", force_wait=True)
    g3 = prog.op("vector", lambda e: e.reciprocal(rowb[:], rowa[:]), after=[g2], inc="v")
    g4 = prog.op("vector", lambda e: e.tensor_scalar(rms1[:], rowb[:], float(np.sqrt(INTER)), None,
                                                     ALU.mult), after=[g3], inc="v", force_wait=True)
    q1 = prog.op("vector", lambda e: e.tensor_mul(rowa[:], rms1[:], mabs_r[:]), after=[g4, tok_m2],
                 inc="v", force_wait=True)
    q2 = prog.op("vector", lambda e: e.tensor_scalar(rowa[:], rowa[:], 1e-5, None, ALU.max),
                 after=[q1], inc="v", force_wait=True)
    q3 = prog.op("vector", lambda e: e.reciprocal(rowb[:], rowa[:]), after=[q2], inc="v", force_wait=True)
    q4 = prog.op("vector", lambda e: e.tensor_mul(c2r[:], rms1[:], rowb[:]), after=[q3], inc="v",
                 force_wait=True)
    q5 = prog.op("vector", lambda e: e.tensor_scalar(c2r[:], c2r[:], 127.0, None, ALU.mult),
                 after=[q4], inc="v", force_wait=True)
    q6 = prog.op("vector", lambda e: e.tensor_scalar(r2r[:], rowa[:], 1.0 / 127.0, None, ALU.mult),
                 after=[q5], inc="v", force_wait=True)
    q7 = prog.op("vector", lambda e: e.tensor_scalar(r2r[:], r2r[:], swsum[0:1, 3:4], None, ALU.mult),
                 after=[q6], inc="v", force_wait=True)
    bc2_toks = []
    for (row, dst, tok) in ((c2r, C2B, q5), (r2r, R2B, q7)):
        for cc_ in range(2):
            mmb = prog.op("tensor", lambda e, row=row, cc_=cc_: nc.tensor.matmul(
                ps[6][:], ones_row[:], row[0:1, 512 * cc_:512 * (cc_ + 1)], start=True, stop=True),
                after=[tok, psfree[6]], inc="t")
            tkb = prog.op("vector", lambda e, dst=dst, cc_=cc_: e.tensor_copy(
                dst[:, 512 * cc_:512 * (cc_ + 1)], ps[6][:]), after=[mmb], inc="v")
            psfree[6] = tkb
            bc2_toks.append(tkb)

    w2buf_tok = [None, None]
    w2buf_mm = [None, None]
    h_load = [None, None]
    hld_cons = [None, None]
    xq2_tok = [None] * 16
    xq2_mm = [None, None]
    y2a_tok = [None] * KT
    for kc in range(KCH):
        wb = kc % 2
        src_ap = ccw_out[128 * kc:128 * (kc + 1), 16384:].rearrange("p (kt x) -> p kt x", kt=8)
        w2buf_tok[wb] = prog.dma("gpsimd", w2qb[wb][:], src_ap,
                                 after=[w2buf_mm[wb], tok_ccw] +
                                       ([last_mm_of_chunk] if kc < 2 else []), sem=f"d_w2{wb}")
        for kk in range(8):
            kglob = kc * 8 + kk
            hb = kk % 2
            h_load[hb] = prog.dma("sync", hld[hb][:], hmid[:, T * kglob:T * (kglob + 1)],
                                  after=[h_load[hb], hld_cons[hb]] +
                                        ([hstore_all[-1]] if kc == 0 and kk == 0 else []) +
                                        ([xq2_mm[wb]] if kk < 2 else []),
                                  sem=f"d_h{hb}")
            qh = prog.op("vector", lambda e, hb=hb: e.tensor_tensor(i8s[:], hld[hb][:], C2B[:], ALU.mult),
                         after=[h_load[hb], xq2_mm[wb] if kk == 0 else None] + bc2_toks, inc="v")
            xq2_tok[wb * 8 + kk] = prog.op("vector", lambda e, wb=wb, kk=kk: e.tensor_copy(
                xq2[wb * 8 + kk][:], i8s[:]), after=[qh], inc="v", force_wait=True)
            hld_cons[hb] = xq2_tok[wb * 8 + kk]
        for tc in range(2):
            for ng in range(2):
                mmt = [None] * 8
                for kk in range(8):
                    for j in range(8):
                        n = ng * 8 + j
                        def f(e, kk=kk, j=j, n=n, tc=tc, wb=wb):
                            return nc.tensor.matmul(ps[j][:], w2qb[wb][:, 2048 * kk + 128 * n:2048 * kk + 128 * (n + 1)],
                                                    xq2[wb * 8 + kk][:, 512 * tc:512 * (tc + 1)],
                                                    start=(kk == 0), stop=(kk == 7))
                        aft = [w2buf_tok[wb], xq2_tok[wb * 8 + kk]]
                        if kk == 0:
                            aft += [psfree[j]]
                        mmt[j] = prog.op("tensor", f, after=aft, inc="t" if kk == 7 else None)
                for j in range(8):
                    n = ng * 8 + j
                    if kc == 0:
                        tk = prog.op("vector", lambda e, j=j, n=n, tc=tc: e.tensor_copy(
                            y2a[n][:, 512 * tc:512 * (tc + 1)], ps[j][:]),
                            after=[mmt[j], tok_xq[n]], inc="v")
                    else:
                        tk = prog.op("vector", lambda e, j=j, n=n, tc=tc: e.tensor_tensor(
                            y2a[n][:, 512 * tc:512 * (tc + 1)], y2a[n][:, 512 * tc:512 * (tc + 1)],
                            ps[j][:], ALU.add), after=[mmt[j], y2a_tok[n]], inc="v", force_wait=True)
                    psfree[j] = tk
                    y2a_tok[n] = tk
        w2buf_mm[wb] = mmt[7]
        xq2_mm[wb] = mmt[7]

    sA_tok = None
    sB_tok = None
    for k in range(KT):
        l1 = prog.dma("sync", sldA[:], xT_in[:, T * k:T * (k + 1)], after=[sA_tok, tok_m2], sem="d_rA")
        l2 = prog.dma("sync", sldB[:], s0_in[:, T * k:T * (k + 1)], after=[sB_tok] + bc2_toks, sem="d_rB")
        m1 = prog.op("vector", lambda e, k=k: e.tensor_tensor(tsc1[:], y2a[k][:], R2B[:], ALU.mult),
                     after=[y2a_tok[k], tsc1_free, w2buf_mm[0], w2buf_mm[1]] + bc2_toks, inc="v")
        tsc1_free = m1
        m2 = prog.op("scalar", lambda e, k=k: e.activation(gt[k][:], tsc1[:], AF.Silu),
                     after=[m1, tok_xq[k]], inc="s")
        m3 = prog.op("vector", lambda e, k=k: e.tensor_tensor(sldB[:], sldB[:], gt[k][:], ALU.mult),
                     after=[m2, l2], inc="v")
        sB_tok = m3
        m4 = prog.op("vector", lambda e: e.tensor_tensor(sldA[:], sldA[:], sldB[:], ALU.add),
                     after=[m3, l1], inc="v", force_wait=True)
        sA_tok = prog.dma("sync", x1_out[:, T * k:T * (k + 1)], sldA[:], after=[m4], sem="d_st")

    prog.op("sync", None, after=[prog.barrier_tok("d_st")])

    es_ps.close()


def build_p2():
    nc = bass.Bass("TRN2", target_bir_lowering=False)
    stack = ExitStack()
    NT = INTER // 128
    NCH = 16
    KCH = 8

    xn_in = nc.dram_tensor("xn_in", [128, KT * T], BF16, kind="ExternalInput").ap()
    xT_in = nc.dram_tensor("xT_in", [128, KT * T], F32, kind="ExternalInput").ap()
    s0_in = nc.dram_tensor("s0_in", [128, KT * T], BF16, kind="ExternalInput").ap()
    var_in = nc.dram_tensor("var_in", [1, T], F32, kind="ExternalInput").ap()
    w1t = nc.dram_tensor("w1t", [D, INTER], F32, kind="ExternalInput").ap()
    w2t = nc.dram_tensor("w2t", [INTER, D], F32, kind="ExternalInput").ap()
    wslice1 = nc.dram_tensor("wslice1", [D, 1024], F32, kind="ExternalInput").ap()
    wslice2 = nc.dram_tensor("wslice2", [1024, D], F32, kind="ExternalInput").ap()
    identf = nc.dram_tensor("identf", [128, 128], F32, kind="ExternalInput").ap()
    cpool32 = nc.dram_tensor("cpool32", [128, 64], F32, kind="ExternalInput").ap()
    cpool16 = nc.dram_tensor("cpool16", [128, 64], BF16, kind="ExternalInput").ap()

    x1_out = nc.dram_tensor("x1_out", [128, KT * T], F32, kind="ExternalOutput").ap()

    hmid = nc.dram_tensor("hmid", [128, NT * T], BF16, kind="Internal").ap()
    cc2_in = nc.dram_tensor("cc2_in", [1, 16], F32, kind="Internal").ap()
    cc2_out = nc.dram_tensor("cc2_out", [1, 16], F32, kind="Internal").ap()

    DR = {
        "xn_in": xn_in,
        "xT_in": xT_in,
        "s0_in": s0_in,
        "var_in": var_in,
        "w1t": w1t,
        "w2t": w2t,
        "wslice1": wslice1,
        "wslice2": wslice2,
        "identf": identf,
        "cpool32": cpool32,
        "cpool16": cpool16,
        "x1_out": x1_out,
        "hmid": hmid,
        "cc2_in": cc2_in,
        "cc2_out": cc2_out,
    }
    with stack:
        prog = Prog(nc, stack)
        C = Consts(nc, stack)
        A = Arena(nc, stack, 206 * 1024)
        emit_p2(nc, stack, prog, C, A, DR)
        block = stack.enter_context(nc.Block())
        prog.replay_into(block)
        consts = C.host_arrays()
    return nc, consts

def p2_inputs(inputs, consts, p1_results):
    a32, a16 = consts
    w1 = np.asarray(inputs["L0_ffn_w1"], np.float32)
    w2 = np.asarray(inputs["L0_ffn_w2"], np.float32)
    w1t_h = np.ascontiguousarray(w1.T)
    w2t_h = np.ascontiguousarray(w2.T)
    ident = np.eye(128, dtype=np.float32)
    per_core = []
    for c in range(NCORES):
        r = p1_results[c]
        per_core.append({
            "xn_in": np.asarray(r["xn_out"]),
            "xT_in": np.asarray(r["xT_out"]),
            "s0_in": np.asarray(r["s0_out"]),
            "var_in": np.asarray(r["var_out"]),
            "w1t": w1t_h,
            "w2t": w2t_h,
            "wslice1": np.ascontiguousarray(w1t_h[:, 1024 * c:1024 * (c + 1)]),
            "wslice2": np.ascontiguousarray(w2t_h[1024 * c:1024 * (c + 1), :]),
            "identf": ident,
            "cpool32": a32,
            "cpool16": a16,
        })
    return per_core


def run_p2(inputs, p1_results):
    nc, consts = build_p2()
    per_core = p2_inputs(inputs, consts, p1_results)
    return run_bass_kernel_spmd(nc, per_core, core_ids=list(range(NCORES)))


LAMBDA = np.exp(-np.array([(2.0 ** (-0.5)) ** (i + 1) for i in range(NHEAD)], dtype=np.float64))


def emit_p3(nc, stack, prog, C, A, DR):
    NCHK = 4
    x1_in = DR["x1_in"]
    wqkvt = DR["wqkvt"]
    woutt = DR["woutt"]
    bqk = DR["bqk"]
    bvrow_d = DR["bvrow_d"]
    bo = DR["bo"]
    identf = DR["identf"]
    identb_d = DR["identb_d"]
    dmaskt_d = DR["dmaskt_d"]
    dq1_d = DR["dq1_d"]
    lamc_d = DR["lamc_d"]
    dkcol_d = DR["dkcol_d"]
    dqfix_d = DR["dqfix_d"]
    wlam_d = DR["wlam_d"]
    cpool32 = DR["cpool32"]
    cpool16 = DR["cpool16"]
    xn2_out = DR["xn2_out"]
    s1_out = DR["s1_out"]
    cc3_in = DR["cc3_in"]
    cc3_out = DR["cc3_out"]
    cur = [0]

    def al(shape, dtype, name=None):
        per = int(np.prod(shape[1:])) * mybir.dt.size(dtype)
        off = cur[0]
        cur[0] += (per + 31) // 32 * 32
        return A.at(off, shape, dtype, name)

    ident = al([128, 128], F32, "ident")
    identb = al([128, 128], BF16, "identb")
    dmaskt = al([128, NHEAD * 128], BF16, "dmaskt")
    dq1 = al([128, NHEAD * 128], BF16, "dq1")
    lamc = al([128, NHEAD * 128], BF16, "lamc")
    dkcol = al([128, NHEAD], F32, "dkcol")
    bqkT = al([128, 32], F32, "bqkT")
    boT = al([128, KT], F32, "boT")
    Bv = al([128, D], BF16, "Bv")
    ones_col = al([128, 1], F32, "ones_col")
    ones_row = al([1, 128], F32, "ones_row")
    qsb = al([128, 128], BF16, "qsb")
    qsfb = al([128, 512], BF16, "qsfb")
    P_END = cur[0]
    assert P_END <= 22 * 1024, P_END

    X1R = 22 * 1024
    XN2R = X1R + 64 * 1024
    VBR = XN2R + 32 * 1024
    ATR = VBR + 32 * 1024
    SSR = ATR + 32 * 1024
    GKR = SSR + 16 * 1024
    assert GKR + 8 * 1024 <= 206 * 1024

    x1 = [A.at(X1R + k * 4096, [128, T], F32, f"x1_{k}") for k in range(KT)]
    qk = [A.at(X1R + n * 2048, [128, T], BF16, f"qk{n}") for n in range(32)]
    wout_sb = [A.at(X1R + k * 4096, [128, D], BF16, f"wout{k}") for k in range(KT)]
    xn2 = [A.at(XN2R + k * 2048, [128, T], BF16, f"xn2_{k}") for k in range(KT)]
    stmp = A.at(XN2R + 0, [128, 1024], F32, "stmp")
    ssb = A.at(XN2R + 4096, [128, NHEAD * 128], BF16, "ssb")
    wlam = A.at(XN2R, [128, NCORES * D], BF16, "wlam")
    s1 = [A.at(XN2R + k * 2048, [128, T], BF16, f"s1_{k}") for k in range(KT)]
    stat_sb = A.at(VBR, [1, 2 * T], F32, "stat_sb")
    mrow = A.at(VBR + 8192, [1, T], F32, "mrow")
    rstdrow = A.at(VBR + 12288, [1, T], F32, "rstdrow")
    nmr = A.at(VBR + 16384, [1, T], F32, "nmr")
    sq = A.at(VBR + 20480, [128, T], F32, "sq")
    bvrow = A.at(VBR + 24576, [1, D], F32, "bvrow")
    vb = [A.at(VBR + i * 4096, [128, D], BF16, f"vb{i}") for i in range(8)]
    dqfix = A.at(VBR, [128, NHEAD * SC], BF16, "dqfix")
    pacc = A.at(VBR + 16 * 1024, [128, D], F32, "pacc")
    ptmp = A.at(VBR + 24 * 1024, [128, 1024], F32, "ptmp")
    Brstd = A.at(ATR, [128, T], F32, "Brstd")
    Bneg = A.at(ATR + 4096, [128, T], F32, "Bneg")
    tscp = A.at(ATR + 8192, [128, T], F32, "tscp")
    attn = [A.at(ATR + h * 2048, [128, T], BF16, f"attn{h}") for h in range(NHEAD)]
    S_sb = [A.at(SSR + b * 4096, [128, D], BF16, f"S{b}") for b in range(2)]
    sin_sb = [A.at(SSR + 8192 + b * 4096, [128, D], BF16, f"sin{b}") for b in range(2)]
    wbuf = [A.at(SSR, [128, 8192], BF16, "wbuf0"), A.at(ATR, [128, 8192], BF16, "wbuf1")]
    kdb = A.at(GKR, [128, NHEAD * 128], BF16, "kdb")
    gat3 = A.at(GKR, [128, 2 * D], BF16, "gat3")

    tok_cp = prog.dma("sync", C.tile_f32[:], cpool32, sem="d_pre")
    prog.dma("sync", C.tile_bf16[:], cpool16, sem="d_pre")
    prog.dma("sync", ident[:], identf, sem="d_pre")
    prog.dma("sync", identb[:], identb_d, sem="d_pre")
    prog.dma("sync", dmaskt[:], dmaskt_d, sem="d_pre")
    prog.dma("sync", dq1[:], dq1_d, sem="d_pre")
    prog.dma("sync", lamc[:], lamc_d, sem="d_pre")
    prog.dma("sync", dkcol[:], dkcol_d, sem="d_pre")
    prog.dma("sync", bqkT[:], bqk, sem="d_pre")
    prog.dma("sync", boT[:], bo, sem="d_pre")
    tok_bv = prog.dma("sync", bvrow[:], bvrow_d, sem="d_pre")
    tok_pre = tok_bv
    tok_ones = prog.op("vector", lambda e: e.memset(ones_col[:], 1.0), inc="v")
    prog.op("vector", lambda e: e.memset(ones_row[:], 1.0), inc="v")

    es_ps = ExitStack()
    psA = [es_ps.enter_context(nc.psum_tensor(f"q3psA{i}", [128, 512], F32)) for i in range(4)]
    psB = es_ps.enter_context(nc.psum_tensor("q3psB", [128, 1024], F32))
    psC = es_ps.enter_context(nc.psum_tensor("q3psC", [128, 1024], F32))
    psfree = {"A0": None, "A1": None, "A2": None, "A3": None, "B": None, "C": None}

    tok_x1 = [prog.dma("sync", x1[k][:], x1_in[:, T * k:T * (k + 1)], sem="d_x") for k in range(KT)]
    last_sum = {}
    for c in range(2):
        mm = None
        for k in range(KT):
            def f(e, k=k, c=c):
                return nc.tensor.matmul(psA[c][0:1, :], ones_col[:], x1[k][:, 512 * c:512 * (c + 1)],
                                        start=(k == 0), stop=(k == KT - 1))
            mm = prog.op("tensor", f, after=[tok_x1[k], tok_ones], inc="t" if k == KT - 1 else None)
        last_sum[c] = mm
    sq_last = None
    sq_fin = {}
    for k in range(KT):
        tk = prog.op("scalar", lambda e, k=k: e.activation(sq[:], x1[k][:], AF.Square),
                     after=[tok_x1[k], sq_last], inc="s")
        for c in range(2):
            sq_last = prog.op("tensor", lambda e, k=k, c=c: nc.tensor.matmul(
                psA[2 + c][0:1, :], ones_col[:], sq[:, 512 * c:512 * (c + 1)],
                start=(k == 0), stop=(k == KT - 1)), after=[tk], inc="t")
            if k == KT - 1:
                sq_fin[c] = sq_last
    ev = []
    ev.append(prog.op("vector", lambda e: e.tensor_copy(stat_sb[0:1, 0:512], psA[0][0:1, :]),
                      after=[last_sum[0]], inc="v"))
    ev.append(prog.op("vector", lambda e: e.tensor_copy(stat_sb[0:1, 512:1024], psA[1][0:1, :]),
                      after=[last_sum[1]], inc="v"))
    ev.append(prog.op("vector", lambda e: e.tensor_copy(stat_sb[0:1, T:T + 512], psA[2][0:1, :]),
                      after=[sq_fin[0]], inc="v"))
    ev.append(prog.op("vector", lambda e: e.tensor_copy(stat_sb[0:1, T + 512:2 * T], psA[3][0:1, :]),
                      after=[sq_fin[1]], inc="v"))
    for i in range(4):
        psfree[f"A{i}"] = ev[i]
    t1 = prog.op("vector", lambda e: e.tensor_scalar(mrow[:], stat_sb[0:1, 0:T], 1.0 / D, None, ALU.mult),
                 after=ev[0:2], inc="v")
    t2 = prog.op("vector", lambda e: e.tensor_mul(rstdrow[:], mrow[:], mrow[:]), after=[t1], inc="v",
                 force_wait=True)
    t3 = prog.op("vector", lambda e: e.scalar_tensor_tensor(rstdrow[:], stat_sb[0:1, T:2 * T], 1.0 / D,
                                                            rstdrow[:], ALU.mult, ALU.subtract),
                 after=[t2] + ev[2:4], inc="v", force_wait=True)
    t4 = prog.op("scalar", lambda e: e.activation(rstdrow[:], rstdrow[:], AF.Ln, bias=C.f32(1e-5)),
                 after=[t3, tok_cp], inc="s")
    t5 = prog.op("scalar", lambda e: e.activation(rstdrow[:], rstdrow[:], AF.Exp, scale=C.f32(-0.5)),
                 after=[t4], inc="s", force_wait=True)
    t6 = prog.op("vector", lambda e: e.tensor_mul(nmr[:], mrow[:], rstdrow[:]), after=[t5], inc="v")
    t7 = prog.op("vector", lambda e: e.tensor_scalar(nmr[:], nmr[:], -1.0, None, ALU.mult), after=[t6],
                 inc="v", force_wait=True)
    bc_toks = []
    for i, (row, dst, tok) in enumerate(((rstdrow, Brstd, t5), (nmr, Bneg, t7))):
        for c in range(2):
            key = f"A{2 * i + c}"
            mmb = prog.op("tensor", lambda e, row=row, c=c, i=i: nc.tensor.matmul(
                psA[2 * i + c][:], ones_row[:], row[0:1, 512 * c:512 * (c + 1)], start=True, stop=True),
                after=[tok, psfree[key]], inc="t")
            tkb = prog.op("vector", lambda e, dst=dst, c=c, i=i: e.tensor_copy(
                dst[:, 512 * c:512 * (c + 1)], psA[2 * i + c][:]), after=[mmb], inc="v")
            psfree[key] = tkb
            bc_toks.append(tkb)
    tok_xn2 = [None] * KT
    for k in range(KT):
        a1 = prog.op("vector", lambda e, k=k: e.tensor_mul(tscp[:], x1[k][:], Brstd[:]),
                     after=bc_toks, inc="v")
        tok_xn2[k] = prog.op("vector", lambda e, k=k: e.tensor_add(xn2[k][:], tscp[:], Bneg[:]),
                             after=[a1], inc="v", force_wait=True)
        prog.dma("sync", xn2_out[:, T * k:T * (k + 1)], xn2[k][:], after=[tok_xn2[k]], sem="d_st")
    bv_toks = []
    for c4 in range(4):
        key = f"A{c4}"
        mmb = prog.op("tensor", lambda e, c4=c4: nc.tensor.matmul(
            psA[c4][:], ones_row[:], bvrow[0:1, 512 * c4:512 * (c4 + 1)], start=True, stop=True),
            after=[tok_bv, psfree[key]], inc="t")
        tkb = prog.op("vector", lambda e, c4=c4: e.tensor_copy(Bv[:, 512 * c4:512 * (c4 + 1)],
                                                               psA[c4][:]), after=[mmb], inc="v")
        psfree[key] = tkb
        bv_toks.append(tkb)
    tok_ln_done_v = prog.barrier_tok("v")
    tok_ln_done_s = prog.barrier_tok("s")
    tok_xn2_st = prog.barrier_tok("d_st")

    qk_tok = [None] * 32
    wb_tok = [None, None]
    wb_cons = [None, None]
    for grp in range(8):
        par = grp % 2
        n0 = grp * 4
        wb_tok[par] = prog.dma("gpsimd", wbuf[par][:],
                               wqkvt[:, 512 * grp:512 * (grp + 1)].rearrange("(kt p) n -> p kt n", p=128),
                               after=[wb_tok[par], wb_cons[par], tok_ln_done_v, tok_ln_done_s],
                               sem=f"d_wb{par}")
        for c in range(2):
            mmg = [None] * 4
            for k in range(KT):
                for j in range(4):
                    key = f"A{j}"
                    def f(e, k=k, j=j, c=c, par=par):
                        return nc.tensor.matmul(psA[j][:], wbuf[par][:, 512 * k + 128 * j:512 * k + 128 * (j + 1)],
                                                xn2[k][:, 512 * c:512 * (c + 1)],
                                                start=(k == 0), stop=(k == KT - 1))
                    aft = [wb_tok[par], tok_xn2[KT - 1], psfree[key]] if k == 0 else None
                    mmg[j] = prog.op("tensor", f, after=aft, inc="t" if k == KT - 1 else None)
            wb_cons[par] = mmg[3]
            for j in range(4):
                n = n0 + j
                key = f"A{j}"
                def fe(e, j=j, n=n, c=c):
                    return e.activation(qk[n][:, 512 * c:512 * (c + 1)], psA[j][:], AF.Identity,
                                        bias=bqkT[:, n:n + 1])
                psfree[key] = prog.op("scalar", fe, after=[mmg[j]], inc="s")
                qk_tok[n] = psfree[key]

    vb_tok = [None] * 8
    for nch in range(4):
        par = nch % 2
        wb_tok[par] = prog.dma("gpsimd", wbuf[par][:],
                               wqkvt[:, 2 * D + 512 * nch:2 * D + 512 * (nch + 1)].rearrange(
                                   "(kt p) n -> p kt n", p=128),
                               after=[wb_tok[par], wb_cons[par]], sem=f"d_wb{par}")
        for tg in range(2):
            mmg = [None] * 4
            for k in range(KT):
                for j in range(4):
                    ti = tg * 4 + j
                    key = f"A{j}"
                    def f(e, k=k, j=j, ti=ti, par=par):
                        return nc.tensor.matmul(psA[j][:], xn2[k][:, 128 * ti:128 * (ti + 1)],
                                                wbuf[par][:, 512 * k:512 * (k + 1)],
                                                start=(k == 0), stop=(k == KT - 1))
                    aft = [wb_tok[par], psfree[key]] if k == 0 else None
                    mmg[j] = prog.op("tensor", f, after=aft, inc="t" if k == KT - 1 else None)
            wb_cons[par] = mmg[3]
            for j in range(4):
                ti = tg * 4 + j
                key = f"A{j}"
                def fe(e, j=j, ti=ti, nch=nch):
                    return e.tensor_tensor(vb[ti][:, 512 * nch:512 * (nch + 1)], psA[j][:],
                                           Bv[:, 512 * nch:512 * (nch + 1)], ALU.add)
                psfree[key] = prog.op("vector", fe, after=[mmg[j]] + (bv_toks if nch == 0 else []), inc="v")
                vb_tok[ti] = psfree[key]

    zS = [prog.op("vector", lambda e, b=b: e.memset(S_sb[b][:], 0.0),
                  after=[wb_tok[0], wb_tok[1], vb_tok[7]], inc="v") for b in range(2)]
    S_tok = {0: zS[0], 1: zS[1]}
    kd_tok = [None] * NHEAD
    ev_sc = None
    attn_last = None
    for b in range(2):
        for u in range(NCHK):
            cols = slice(512 * b + 128 * u, 512 * b + 128 * (u + 1))
            for sg in range(2):
                sh0 = sg * 8
                sc_mm = None
                for h in range(sh0, sh0 + 8):
                    def f(e, h=h, cols=cols, sh0=sh0):
                        return nc.tensor.matmul(psC[:, 128 * (h - sh0):128 * (h - sh0 + 1)],
                                                qk[16 + h][:, cols], qk[h][:, cols],
                                                start=True, stop=True)
                    aft = [qk_tok[16 + h], qk_tok[h]]
                    if h == sh0:
                        aft += [psfree["C"]]
                    sc_mm = prog.op("tensor", f, after=aft, inc="t" if h == sh0 + 7 else None)
                ev_sc = prog.op("vector", lambda e, sg=sg: e.tensor_tensor(
                    ssb[:, 1024 * sg:1024 * (sg + 1)], psC[:], dmaskt[:, 1024 * sg:1024 * (sg + 1)],
                    ALU.mult), after=[sc_mm, tok_pre, ev_sc], inc="v")
                psfree["C"] = ev_sc
            kt_mm = None
            for h in range(NHEAD):
                def ft(e, h=h, cols=cols):
                    return nc.tensor.transpose(psC[:].bitcast(BF16)[:, 128 * h:128 * (h + 1)],
                                               qk[16 + h][:, cols], identb[:])
                aft = [ev_sc] + ([kd_tok[15]] if h == 0 else [])
                kt_mm = prog.op("tensor", ft, after=aft, inc="t" if h == NHEAD - 1 else None)
            for h in range(NHEAD):
                kd_tok[h] = prog.op("vector", lambda e, h=h: e.tensor_scalar(
                    kdb[:, 128 * h:128 * (h + 1)], psC[:].bitcast(BF16)[:, 128 * h:128 * (h + 1)],
                    dkcol[:, h:h + 1], None, ALU.mult), after=[kt_mm], inc="v")
            psfree["C"] = kd_tok[NHEAD - 1]
            for hg in range(2):
                h0 = hg * 8
                mm_o = None
                q_toks = []
                for h in range(h0, h0 + 8):
                    def fi(e, h=h, b=b, u=u, h0=h0):
                        return nc.tensor.matmul(psB[:, 128 * (h - h0):128 * (h - h0 + 1)],
                                                vb[b * 4 + u][:, 128 * h:128 * (h + 1)],
                                                ssb[:, 128 * h:128 * (h + 1)], start=True, stop=False)
                    aft = [ev_sc, vb_tok[b * 4 + u]]
                    if h == h0:
                        aft += [psfree["B"]]
                    prog.op("tensor", fi, after=aft)
                    tq = prog.op("vector", lambda e, h=h, cols=cols: e.tensor_mul(
                        qsb[:], qk[h][:, cols], dq1[:, 128 * h:128 * (h + 1)]),
                        after=[S_tok[b], q_toks[-1] if q_toks else None], inc="v")
                    def f2(e, h=h, b=b, h0=h0):
                        return nc.tensor.matmul(psB[:, 128 * (h - h0):128 * (h - h0 + 1)],
                                                S_sb[b][:, 128 * h:128 * (h + 1)], qsb[:],
                                                start=False, stop=True)
                    mm_o = prog.op("tensor", f2, after=[tq, S_tok[b]], inc="t")
                    q_toks.append(mm_o)
                ev_at = None
                for h in range(h0, h0 + 8):
                    def fe(e, h=h, cols=cols, h0=h0):
                        return e.tensor_copy(attn[h][:, cols], psB[:, 128 * (h - h0):128 * (h - h0 + 1)])
                    ev_at = prog.op("vector", fe, after=[mm_o], inc="v")
                attn_last = ev_at
                psfree["B"] = ev_at
                mm_kv = None
                for h in range(h0, h0 + 8):
                    def fk(e, h=h, b=b, u=u, h0=h0):
                        return nc.tensor.matmul(psB[:, 128 * (h - h0):128 * (h - h0 + 1)],
                                                kdb[:, 128 * h:128 * (h + 1)],
                                                vb[b * 4 + u][:, 128 * h:128 * (h + 1)],
                                                start=True, stop=True)
                    aft = [kd_tok[h]]
                    if h == h0:
                        aft += [psfree["B"]]
                    mm_kv = prog.op("tensor", fk, after=aft, inc="t" if h == h0 + 7 else None)
                su1 = prog.op("vector", lambda e, b=b, h0=h0: e.tensor_tensor(
                    stmp[:], S_sb[b][:, 128 * h0:128 * h0 + 1024],
                    lamc[:, 128 * h0:128 * h0 + 1024], ALU.mult),
                    after=[mm_kv, S_tok[b], tok_xn2_st], inc="v")
                S_tok[b] = prog.op("vector", lambda e, b=b, h0=h0: e.tensor_tensor(
                    S_sb[b][:, 128 * h0:128 * h0 + 1024], stmp[:], psB[:], ALU.add),
                    after=[su1], inc="v", force_wait=True)
                psfree["B"] = S_tok[b]

    tok_pub = [prog.dma("sync", cc3_in[:, D * b:D * (b + 1)], S_sb[b][:], after=[S_tok[b]], sem="d_cc")
               for b in range(2)]
    tok_cc = prog.op("gpsimd", lambda e: e.collective_compute(
        "AllGather", ALU.bypass, replica_groups=[list(range(NCORES))],
        ins=[cc3_in.opt()], outs=[cc3_out.opt()]), after=tok_pub, inc="cc")
    tok_wlam = prog.dma("sync", wlam[:], wlam_d,
                        after=[prog.barrier_tok("t"), prog.barrier_tok("v"),
                               prog.barrier_tok("d_st")], sem="d_w2")
    tok_dqfix = prog.dma("sync", dqfix[:], dqfix_d,
                         after=[prog.barrier_tok("v"), prog.barrier_tok("t")], sem="d_w2")
    sin_tok = {}
    gat_tok = None
    for b in range(2):
        acc_tok = prog.op("vector", lambda e: e.memset(pacc[:], 0.0),
                          after=[S_tok[0], S_tok[1], prog.barrier_tok("t"),
                                 prog.barrier_tok("v") if b == 0 else None], inc="v")
        for j in range(NCORES):
            gat_tok = prog.dma("sync", gat3[:], cc3_out[128 * j:128 * (j + 1), :],
                               after=[tok_cc, gat_tok, acc_tok], sem=f"d_g{b}")
            for half in range(2):
                m1 = prog.op("vector", lambda e, j=j, b=b, half=half: e.tensor_tensor(
                    ptmp[:], gat3[:, D * b + 1024 * half:D * b + 1024 * (half + 1)],
                    wlam[:, D * j + 1024 * half:D * j + 1024 * (half + 1)], ALU.mult),
                    after=[gat_tok, tok_wlam], inc="v")
                acc_tok = prog.op("vector", lambda e, half=half: e.tensor_tensor(
                    pacc[:, 1024 * half:1024 * (half + 1)], pacc[:, 1024 * half:1024 * (half + 1)],
                    ptmp[:], ALU.add), after=[m1], inc="v", force_wait=True)
        sin_tok[b] = prog.op("vector", lambda e, b=b: e.tensor_copy(sin_sb[b][:], pacc[:]),
                             after=[acc_tok], inc="v", force_wait=True)

    fix_last = None
    qs_tok = None
    mmf = None
    for b in range(2):
        for h in range(NHEAD):
            brange = slice(512 * b, 512 * (b + 1))
            qs_tok = prog.op("vector", lambda e, h=h, brange=brange: e.tensor_mul(
                qsfb[:], qk[h][:, brange], dqfix[:, SC * h:SC * (h + 1)]),
                after=[tok_dqfix, sin_tok[b], qs_tok, mmf], inc="v")
            mmf = prog.op("tensor", lambda e, h=h, b=b: nc.tensor.matmul(
                psA[0][:], sin_sb[b][:, 128 * h:128 * (h + 1)], qsfb[:], start=True, stop=True),
                after=[qs_tok, sin_tok[b], psfree["A0"]], inc="t")
            fix_last = prog.op("vector", lambda e, h=h, brange=brange: e.tensor_tensor(
                attn[h][:, brange], attn[h][:, brange], psA[0][:], ALU.add),
                after=[mmf, attn_last], inc="v")
            psfree["A0"] = fix_last

    wo_tok = [prog.dma("gpsimd", wout_sb[k][:], woutt[128 * k:128 * (k + 1), :],
                       after=[prog.barrier_tok("t"), fix_last], sem="d_w") for k in range(KT)]
    ev_s1 = {}
    for c in range(2):
        for grp in range(4):
            mmg = [None] * 4
            for k in range(KT):
                for j in range(4):
                    n = grp * 4 + j
                    key = f"A{j}"
                    def f(e, k=k, j=j, n=n, c=c):
                        return nc.tensor.matmul(psA[j][:], wout_sb[k][:, 128 * n:128 * (n + 1)],
                                                attn[k][:, 512 * c:512 * (c + 1)],
                                                start=(k == 0), stop=(k == KT - 1))
                    aft = [wo_tok[KT - 1], fix_last, psfree[key]] if k == 0 else None
                    mmg[j] = prog.op("tensor", f, after=aft, inc="t" if k == KT - 1 else None)
            for j in range(4):
                n = grp * 4 + j
                key = f"A{j}"
                def fe(e, j=j, n=n, c=c):
                    return e.activation(s1[n][:, 512 * c:512 * (c + 1)], psA[j][:], AF.Identity,
                                        bias=boT[:, n:n + 1])
                psfree[key] = prog.op("scalar", fe, after=[mmg[j], tok_wlam], inc="s")
                ev_s1[(c, n)] = psfree[key]
    for k in range(KT):
        prog.dma("sync", s1_out[:, T * k:T * (k + 1)], s1[k][:], after=[ev_s1[(1, k)]], sem="d_st")
    prog.op("sync", None, after=[prog.barrier_tok("d_st")])

    es_ps.close()


def build_p3():
    nc = bass.Bass("TRN2", target_bir_lowering=False)
    stack = ExitStack()
    NCHK = 4

    x1_in = nc.dram_tensor("x1_in", [128, KT * T], F32, kind="ExternalInput").ap()
    wqkvt = nc.dram_tensor("wqkvt", [D, 3 * D], F32, kind="ExternalInput").ap()
    woutt = nc.dram_tensor("woutt", [D, D], F32, kind="ExternalInput").ap()
    bqk = nc.dram_tensor("bqk", [128, 32], F32, kind="ExternalInput").ap()
    bvrow_d = nc.dram_tensor("bvrow_d", [1, D], F32, kind="ExternalInput").ap()
    bo = nc.dram_tensor("bo", [128, KT], F32, kind="ExternalInput").ap()
    identf = nc.dram_tensor("identf", [128, 128], F32, kind="ExternalInput").ap()
    identb_d = nc.dram_tensor("identb_d", [128, 128], BF16, kind="ExternalInput").ap()
    dmaskt_d = nc.dram_tensor("dmaskt_d", [128, NHEAD * 128], BF16, kind="ExternalInput").ap()
    dq1_d = nc.dram_tensor("dq1_d", [128, NHEAD * 128], BF16, kind="ExternalInput").ap()
    lamc_d = nc.dram_tensor("lamc_d", [128, NHEAD * 128], BF16, kind="ExternalInput").ap()
    dkcol_d = nc.dram_tensor("dkcol_d", [128, NHEAD], F32, kind="ExternalInput").ap()
    dqfix_d = nc.dram_tensor("dqfix_d", [128, NHEAD * SC], BF16, kind="ExternalInput").ap()
    wlam_d = nc.dram_tensor("wlam_d", [128, NCORES * D], BF16, kind="ExternalInput").ap()
    cpool32 = nc.dram_tensor("cpool32", [128, 64], F32, kind="ExternalInput").ap()
    cpool16 = nc.dram_tensor("cpool16", [128, 64], BF16, kind="ExternalInput").ap()

    xn2_out = nc.dram_tensor("xn2_out", [128, KT * T], BF16, kind="ExternalOutput").ap()
    s1_out = nc.dram_tensor("s1_out", [128, KT * T], BF16, kind="ExternalOutput").ap()

    cc3_in = nc.dram_tensor("cc3_in", [128, 2 * D], BF16, kind="Internal").ap()
    cc3_out = nc.dram_tensor("cc3_out", [NCORES * 128, 2 * D], BF16, kind="Internal",
                             addr_space="Shared").ap()

    DR = {
        "x1_in": x1_in,
        "wqkvt": wqkvt,
        "woutt": woutt,
        "bqk": bqk,
        "bvrow_d": bvrow_d,
        "bo": bo,
        "identf": identf,
        "identb_d": identb_d,
        "dmaskt_d": dmaskt_d,
        "dq1_d": dq1_d,
        "lamc_d": lamc_d,
        "dkcol_d": dkcol_d,
        "dqfix_d": dqfix_d,
        "wlam_d": wlam_d,
        "cpool32": cpool32,
        "cpool16": cpool16,
        "xn2_out": xn2_out,
        "s1_out": s1_out,
        "cc3_in": cc3_in,
        "cc3_out": cc3_out,
    }
    with stack:
        prog = Prog(nc, stack)
        C = Consts(nc, stack)
        A = Arena(nc, stack, 206 * 1024)
        emit_p3(nc, stack, prog, C, A, DR)
        block = stack.enter_context(nc.Block())
        prog.replay_into(block)
        consts = C.host_arrays()
    return nc, consts

def p3_consts(core):
    lam = LAMBDA
    dmaskt = np.zeros((128, NHEAD * 128), np.float32)
    dq1 = np.zeros((128, NHEAD * 128), np.float32)
    lamc = np.zeros((128, NHEAD * 128), np.float32)
    dkcol = np.zeros((128, NHEAD), np.float32)
    dqfix = np.zeros((128, NHEAD * SC), np.float32)
    pos = np.arange(128)
    for h in range(NHEAD):
        l_ = lam[h]
        diff = pos[None, :] - pos[:, None]
        dmaskt[:, 128 * h:128 * (h + 1)] = np.where(diff >= 0, l_ ** np.maximum(diff, 0), 0.0)
        dq1[:, 128 * h:128 * (h + 1)] = (l_ ** (pos + 1.0))[None, :]
        lamc[:, 128 * h:128 * (h + 1)] = l_ ** 128
        dkcol[:, h] = l_ ** (127.0 - pos)
        tpos = np.arange(SC)
        dqfix[:, SC * h:SC * (h + 1)] = (l_ ** (tpos + 1.0))[None, :]
    wlam = np.zeros((128, NCORES * D), np.float32)
    for j in range(NCORES):
        if j < core:
            for h in range(NHEAD):
                wlam[:, D * j + 128 * h:D * j + 128 * (h + 1)] = lam[h] ** (512.0 * (core - 1 - j))
    bf = lambda a: np.ascontiguousarray(a.astype("bfloat16"))
    return {"dmaskt_d": bf(dmaskt), "dq1_d": bf(dq1), "lamc_d": bf(lamc),
            "dkcol_d": np.ascontiguousarray(dkcol), "dqfix_d": bf(dqfix), "wlam_d": bf(wlam)}


def p3_inputs(inputs, consts, p2_results):
    a32, a16 = consts
    wqkv = np.asarray(inputs["L1_attn_w_qkv"], np.float32)
    wout = np.asarray(inputs["L1_attn_w_out"], np.float32)
    bqkv = np.asarray(inputs["L1_attn_b_qkv"], np.float32)
    bout = np.asarray(inputs["L1_attn_b_out"], np.float32)
    ident = np.eye(128, dtype=np.float32)
    base = {
        "wqkvt": np.ascontiguousarray(wqkv.T),
        "woutt": np.ascontiguousarray(wout.T),
        "bqk": np.ascontiguousarray(bqkv[:2 * D].reshape(32, 128).T),
        "bvrow_d": np.ascontiguousarray(bqkv[2 * D:].reshape(1, D)),
        "bo": np.ascontiguousarray(bout.reshape(KT, 128).T),
        "identf": ident,
        "identb_d": np.ascontiguousarray(ident.astype("bfloat16")),
        "cpool32": a32,
        "cpool16": a16,
    }
    per_core = []
    for c in range(NCORES):
        d = dict(base)
        d.update(p3_consts(c))
        d["x1_in"] = np.asarray(p2_results[c]["x1_out"])
        per_core.append(d)
    return per_core


def run_p3(inputs, p2_results):
    nc, consts = build_p3()
    per_core = p3_inputs(inputs, consts, p2_results)
    return run_bass_kernel_spmd(nc, per_core, core_ids=list(range(NCORES)))


def _hadamard(n):
    h = np.array([[1.0]])
    while h.shape[0] < n:
        h = np.block([[h, h], [h, -h]])
    return h


def fwht_perm1():
    p = np.empty(D, np.int64)
    for j in range(16):
        for lo3 in range(8):
            for t4 in range(16):
                p[j * 128 + lo3 * 16 + t4] = t4 * 128 + j * 8 + lo3
    return p


def fwht_perm2():
    p = np.empty(INTER, np.int64)
    for j2 in range(64):
        for lo1 in range(2):
            for t6 in range(64):
                p[j2 * 128 + lo1 * 64 + t6] = t6 * 128 + j2 * 2 + lo1
    return p


def emit_p4(nc, stack, prog, C, A, DR):
    NT = INTER // 128
    x1_in = DR["x1_in"]
    xn2_in = DR["xn2_in"]
    s1_in = DR["s1_in"]
    wupt = DR["wupt"]
    wgt = DR["wgt"]
    wdnt = DR["wdnt"]
    suug_d = DR["suug_d"]
    sudp_d = DR["sudp_d"]
    wscup_d = DR["wscup_d"]
    wscg_d = DR["wscg_d"]
    wscdp_d = DR["wscdp_d"]
    h128_d = DR["h128_d"]
    sb1_d = DR["sb1_d"]
    sb2_d = DR["sb2_d"]
    identf = DR["identf"]
    cpool32 = DR["cpool32"]
    cpool16 = DR["cpool16"]
    out_slab = DR["out_slab"]
    za_dram = DR["za_dram"]
    za2_dram = DR["za2_dram"]
    cur = [0]

    def al(shape, dtype, name=None):
        per = int(np.prod(shape[1:])) * mybir.dt.size(dtype)
        off = cur[0]
        cur[0] += (per + 31) // 32 * 32
        return A.at(off, shape, dtype, name)

    ident = al([128, 128], F32, "ident")
    h128b = al([128, 128], BF16, "h128b")
    sb1 = al([128, 128], BF16, "sb1")
    sb2 = al([128, 128], BF16, "sb2")
    suug = al([128, KT], F32, "suug")
    sudp = al([128, NT], F32, "sudp")
    v_up = al([128, NT], F32, "v_up")
    v_g = al([128, NT], F32, "v_g")
    v_dn = al([128, KT], F32, "v_dn")
    zst = [al([128, 512], BF16, f"zst{i}") for i in range(2)]
    mtst = [al([128, T], BF16, f"mtst{i}") for i in range(2)]
    gss = [al([128, 512], BF16, f"gss{i}") for i in range(4)]
    mst = [al([128, 512], BF16, f"mst{i}") for i in range(4)]
    s1st = [al([128, T], BF16, f"s1st{i}") for i in range(2)]
    gst = al([128, T], F32, "gst")
    gbf = al([128, T], BF16, "gbf")
    otm = [al([128, D], F32, f"otm{i}") for i in range(2)]
    P_END = cur[0]

    W0 = (P_END + 63) // 64 * 64
    ZC = W0
    XN2R = W0 + 32 * 1024
    WBR = W0 + 64 * 1024
    MHR = W0 + 96 * 1024
    WBR2 = W0 + 112 * 1024
    WEND = W0 + 144 * 1024
    assert WEND <= 206 * 1024, (WEND, W0)

    zc = [A.at(ZC + k * 2048, [128, T], BF16, f"zc{k}") for k in range(KT)]
    xn2 = [A.at(XN2R + k * 2048, [128, T], BF16, f"xn2_{k}") for k in range(KT)]
    y3a = [A.at(XN2R + k * 2048, [128, T], BF16, f"y3a{k}") for k in range(KT)]
    wbU = [A.at(r, [128, 8192], BF16, f"wbU{i}") for i, r in enumerate((WBR, WBR2))]
    wbG = [A.at(r + 16 * 1024, [128, 8192], BF16, f"wbG{i}") for i, r in enumerate((WBR, WBR2))]
    wbD = [A.at(r, [128, 8 * 2048], BF16, f"wbD{i}") for i, r in enumerate((WBR, WBR2))]
    mh = [A.at(MHR + i * 2048, [128, T], BF16, f"mh{i}") for i in range(8)]
    xo = [A.at((ZC + k * 4096) if k < 8 else (WBR + (k - 8) * 4096), [128, T], F32, f"xo{k}")
          for k in range(KT)]

    tok_cp = prog.dma("sync", C.tile_f32[:], cpool32, sem="d_pre")
    prog.dma("sync", C.tile_bf16[:], cpool16, sem="d_pre")
    prog.dma("sync", ident[:], identf, sem="d_pre")
    prog.dma("sync", h128b[:], h128_d, sem="d_pre")
    prog.dma("sync", sb1[:], sb1_d, sem="d_pre")
    prog.dma("sync", sb2[:], sb2_d, sem="d_pre")
    prog.dma("sync", suug[:], suug_d, sem="d_pre")
    prog.dma("sync", sudp[:], sudp_d, sem="d_pre")
    prog.dma("sync", v_up[:], wscup_d, sem="d_pre")
    prog.dma("sync", v_g[:], wscg_d, sem="d_pre")
    tok_pre = prog.dma("sync", v_dn[:], wscdp_d, sem="d_pre")
    f1 = prog.op("vector", lambda e: e.tensor_scalar(v_g[:], v_g[:], float(1 / np.sqrt(D)), None, ALU.mult),
                 after=[tok_pre], inc="v", force_wait=True)
    f2 = prog.op("vector", lambda e: e.tensor_mul(v_up[:], v_up[:], sudp[:]), after=[f1], inc="v",
                 force_wait=True)
    f3 = prog.op("vector", lambda e: e.tensor_scalar(v_up[:], v_up[:], float(1 / np.sqrt(D)), None, ALU.mult),
                 after=[f2], inc="v", force_wait=True)
    tok_fold = prog.op("vector", lambda e: e.tensor_scalar(v_dn[:], v_dn[:], float(1 / np.sqrt(INTER)),
                                                           None, ALU.mult), after=[f3], inc="v",
                       force_wait=True)

    es_ps = ExitStack()
    ps = [es_ps.enter_context(nc.psum_tensor(f"q4ps{i}", [128, 512], F32)) for i in range(8)]
    psfree = [None] * 8

    tok_xn2 = [prog.dma("sync", xn2[k][:], xn2_in[:, T * k:T * (k + 1)], sem="d_x") for k in range(KT)]
    z_cons = [None, None]
    zst_cons = [None, None]
    it = 0
    for k in range(KT):
        par = k % 2
        zb = mtst[par]
        tz = prog.op("vector", lambda e, k=k, zb=zb: e.tensor_scalar(zb[:], xn2[k][:], suug[:, k:k + 1],
                                                                     None, ALU.mult),
                     after=[tok_xn2[k], tok_pre, z_cons[par]], inc="v")
        for c in range(2):
            pj = it % 4
            sp = it % 2
            mm = prog.op("tensor", lambda e, zb=zb, c=c, pj=pj: nc.tensor.matmul(
                ps[pj][:], h128b[:], zb[:, 512 * c:512 * (c + 1)], start=True, stop=True),
                after=[tz, psfree[pj]], inc="t")
            if c == 1:
                z_cons[par] = mm
            ev = prog.op("vector", lambda e, pj=pj, sp=sp: e.tensor_copy(zst[sp][:], ps[pj][:]),
                         after=[mm, zst_cons[sp]], inc="v")
            psfree[pj] = ev
            zst_cons[sp] = prog.dma("sync", za_dram[:, T * k + 512 * c:T * k + 512 * (c + 1)],
                                    zst[sp][:], after=[ev], sem="d_za")
            it += 1
    tok_zaw = prog.barrier_tok("d_za")

    za_flat = za_dram.rearrange("p (s t) -> (p s) t", s=KT)
    mt_cons = [None, None]
    zc_tok = [None] * KT
    for j in range(KT):
        par = j % 2
        ld = prog.dma("sync", mtst[par][:], za_flat[128 * j:128 * (j + 1), :],
                      after=[tok_zaw, mt_cons[par], z_cons[par]], sem=f"d_mt{par}")
        mm_last = None
        for c in range(2):
            pj = 4 + (2 * j + c) % 4
            mm = prog.op("tensor", lambda e, par=par, c=c, pj=pj: nc.tensor.matmul(
                ps[pj][:], sb1[:], mtst[par][:, 512 * c:512 * (c + 1)], start=True, stop=True),
                after=[ld, psfree[pj]], inc="t")
            mm_last = mm
            psfree[pj] = prog.op("vector", lambda e, j=j, c=c, pj=pj: e.tensor_copy(
                zc[j][:, 512 * c:512 * (c + 1)], ps[pj][:]), after=[mm], inc="v")
        mt_cons[par] = mm_last
        zc_tok[j] = psfree[4 + (2 * j + 1) % 4]

    wu_tok = [None, None]
    wg_tok2 = [None, None]
    wU_cons = [None, None]
    wG_cons = [None, None]
    za2_st = [None, None]
    for ch in range(KT):
        par = ch % 2
        gate_w = [prog.barrier_tok("t")]
        wu_tok[par] = prog.dma("gpsimd", wbU[par][:],
                               wupt[:, 512 * ch:512 * (ch + 1)].rearrange("(kt p) n -> p kt n", p=128),
                               after=[wu_tok[par], wU_cons[par]] + gate_w, sem=f"d_wu{par}")
        wg_tok2[par] = prog.dma("gpsimd", wbG[par][:],
                                wgt[:, 512 * ch:512 * (ch + 1)].rearrange("(kt p) n -> p kt n", p=128),
                                after=[wg_tok2[par], wG_cons[par]] + gate_w, sem=f"d_wg{par}")
        for tc in range(2):
            mmU = [None] * 4
            mmG = [None] * 4
            for k in range(KT):
                for j in range(4):
                    def fU(e, k=k, j=j, tc=tc, par=par):
                        return nc.tensor.matmul(ps[j][:], wbU[par][:, 512 * k + 128 * j:512 * k + 128 * (j + 1)],
                                                zc[k][:, 512 * tc:512 * (tc + 1)],
                                                start=(k == 0), stop=(k == KT - 1))
                    aft = [wu_tok[par], zc_tok[k], psfree[j]] if k == 0 else None
                    mmU[j] = prog.op("tensor", fU, after=aft, inc="t" if k == KT - 1 else None)
                for j in range(4):
                    def fG(e, k=k, j=j, tc=tc, par=par):
                        return nc.tensor.matmul(ps[4 + j][:], wbG[par][:, 512 * k + 128 * j:512 * k + 128 * (j + 1)],
                                                zc[k][:, 512 * tc:512 * (tc + 1)],
                                                start=(k == 0), stop=(k == KT - 1))
                    aft = [wg_tok2[par], zc_tok[k], psfree[4 + j]] if k == 0 else None
                    mmG[j] = prog.op("tensor", fG, after=aft, inc="t" if k == KT - 1 else None)
            wU_cons[par] = mmU[3]
            wG_cons[par] = mmG[3]
            for j in range(4):
                n = 4 * ch + j
                e_up = prog.op("vector", lambda e, j=j, n=n: e.tensor_scalar(
                    mst[j][:], ps[j][:], v_up[:, n:n + 1], None, ALU.mult),
                    after=[mmU[j], tok_fold], inc="v")
                e_g = prog.op("scalar", lambda e, j=j, n=n: e.activation(
                    gss[j][:], ps[4 + j][:], AF.Silu, scale=v_g[:, n:n + 1]),
                    after=[mmG[j], tok_fold, tok_cp], inc="s")
                e_m = prog.op("vector", lambda e, j=j: e.tensor_mul(mst[j][:], mst[j][:], gss[j][:]),
                              after=[e_up, e_g], inc="v", force_wait=True)
                psfree[j] = e_m
                mm2 = prog.op("tensor", lambda e, j=j: nc.tensor.matmul(
                    ps[j][:], h128b[:], mst[j][:], start=True, stop=True),
                    after=[e_m], inc="t")
                psfree[4 + j] = mm2
                sp = (2 * (4 * ch + j) + tc) % 2
                ev2 = prog.op("vector", lambda e, j=j, sp=sp: e.tensor_copy(zst[sp][:], ps[j][:]),
                              after=[mm2, za2_st[sp]], inc="v")
                psfree[j] = ev2
                za2_st[sp] = prog.dma("sync", za2_dram[:, T * n + 512 * tc:T * n + 512 * (tc + 1)],
                                      zst[sp][:], after=[ev2], sem="d_za")
    tok_za2 = prog.barrier_tok("d_za")

    za2_flat = za2_dram.rearrange("p (s t) -> (p s) t", s=NT)
    wd_tok = [None, None]
    wD_cons = [wU_cons[0], wU_cons[1]]
    y3_tok = [None] * KT
    mh_tok = [None] * 8
    mt_cons = [None, None]
    for kc in range(8):
        wpar = kc % 2
        wd_tok[wpar] = prog.dma("gpsimd", wbD[wpar][:],
                                wdnt[1024 * kc:1024 * (kc + 1), :].rearrange("(kt p) n -> p kt n", p=128),
                                after=[wd_tok[wpar], wD_cons[wpar],
                                       wG_cons[wpar] if kc < 2 else None], sem=f"d_wd{wpar}")
        for kk in range(8):
            j2 = 8 * kc + kk
            par = kk % 2
            ld = prog.dma("sync", mtst[par][:], za2_flat[128 * j2:128 * (j2 + 1), :],
                          after=[tok_za2, mt_cons[par]], sem=f"d_mt{par}")
            mm_last = None
            for c in range(2):
                pj = (2 * kk + c) % 4
                mm = prog.op("tensor", lambda e, par=par, c=c, pj=pj: nc.tensor.matmul(
                    ps[pj][:], sb2[:], mtst[par][:, 512 * c:512 * (c + 1)], start=True, stop=True),
                    after=[ld, psfree[pj]], inc="t")
                mm_last = mm
                psfree[pj] = prog.op("vector", lambda e, kk=kk, c=c, pj=pj: e.tensor_copy(
                    mh[kk][:, 512 * c:512 * (c + 1)], ps[pj][:]), after=[mm, mh_tok[kk]], inc="v")
            mt_cons[par] = mm_last
            mh_tok[kk] = psfree[(2 * kk + 1) % 4]
        for tc in range(2):
            for ng in range(2):
                mmt = [None] * 8
                for kk in range(8):
                    for j in range(8):
                        n = ng * 8 + j
                        def f(e, kk=kk, j=j, n=n, tc=tc, wpar=wpar):
                            return nc.tensor.matmul(ps[j][:], wbD[wpar][:, 2048 * kk + 128 * n:2048 * kk + 128 * (n + 1)],
                                                    mh[kk][:, 512 * tc:512 * (tc + 1)],
                                                    start=(kk == 0), stop=(kk == 7))
                        aft = [wd_tok[wpar], mh_tok[kk]]
                        if kk == 0:
                            aft += [psfree[j]]
                        mmt[j] = prog.op("tensor", f, after=aft, inc="t" if kk == 7 else None)
                wD_cons[wpar] = mmt[7]
                for j in range(8):
                    n = ng * 8 + j
                    if kc == 0:
                        tk = prog.op("vector", lambda e, j=j, n=n, tc=tc: e.tensor_copy(
                            y3a[n][:, 512 * tc:512 * (tc + 1)], ps[j][:]),
                            after=[mmt[j], tok_xn2[n]], inc="v")
                    else:
                        tk = prog.op("vector", lambda e, j=j, n=n, tc=tc: e.tensor_tensor(
                            y3a[n][:, 512 * tc:512 * (tc + 1)], y3a[n][:, 512 * tc:512 * (tc + 1)],
                            ps[j][:], ALU.add), after=[mmt[j], y3_tok[n]], inc="v", force_wait=True)
                    psfree[j] = tk
                    y3_tok[n] = tk
        for kk in range(8):
            mh_tok[kk] = prog.barrier_tok("t")

    tok_wdn_done = prog.barrier_tok("t")
    x1_tok = [None] * KT
    s1_tok2 = [None] * KT
    xo_tok = [None] * KT
    sA = [None, None]
    for n in range(KT):
        par = n % 2
        l1 = prog.dma("sync", xo[n][:], x1_in[:, T * n:T * (n + 1)],
                      after=[wG_cons[1] if n < 8 else wD_cons[0]], sem="d_r1")
        l2 = prog.dma("sync", s1st[par][:], s1_in[:, T * n:T * (n + 1)],
                      after=[s1_tok2[n - 2] if n >= 2 else None], sem=f"d_r2{par}")
        m1 = prog.op("vector", lambda e, n=n: e.tensor_scalar(gst[:], y3a[n][:], v_dn[:, n:n + 1],
                                                              None, ALU.mult),
                     after=[y3_tok[n], xo_tok[n - 1] if n >= 1 else None], inc="v")
        m2 = prog.op("scalar", lambda e: e.activation(gbf[:], gst[:], AF.Silu), after=[m1], inc="s")
        m3 = prog.op("vector", lambda e, par=par: e.tensor_mul(s1st[par][:], s1st[par][:], gbf[:]),
                     after=[m2, l2], inc="v")
        s1_tok2[n] = m3
        xo_tok[n] = prog.op("vector", lambda e, n=n, par=par: e.tensor_tensor(
            xo[n][:], xo[n][:], s1st[par][:], ALU.add), after=[m3, l1], inc="v", force_wait=True)
    ot_cons = [None, None]
    for ti in range(8):
        par = ti % 2
        ev_last = None
        for n in range(KT):
            pj = n % 4
            mm = prog.op("tensor", lambda e, n=n, ti=ti, pj=pj: nc.tensor.transpose(
                ps[pj][:, 0:128], xo[n][:, 128 * ti:128 * (ti + 1)], ident[:]),
                after=[xo_tok[n], psfree[pj], tok_cp], inc="t")
            ev_last = prog.op("vector", lambda e, n=n, par=par, pj=pj: e.tensor_copy(
                otm[par][:, 128 * n:128 * (n + 1)], ps[pj][:, 0:128]),
                after=[mm, ot_cons[par] if n == 0 else None], inc="v")
            psfree[pj] = ev_last
        ot_cons[par] = prog.dma("sync", out_slab[128 * ti:128 * (ti + 1), :], otm[par][:],
                                after=[ev_last], sem="d_st")
    prog.op("sync", None, after=[prog.barrier_tok("d_st")])

    es_ps.close()


def build_p4():
    nc = bass.Bass("TRN2", target_bir_lowering=False)
    stack = ExitStack()
    NT = INTER // 128

    x1_in = nc.dram_tensor("x1_in", [128, KT * T], F32, kind="ExternalInput").ap()
    xn2_in = nc.dram_tensor("xn2_in", [128, KT * T], BF16, kind="ExternalInput").ap()
    s1_in = nc.dram_tensor("s1_in", [128, KT * T], BF16, kind="ExternalInput").ap()
    wupt = nc.dram_tensor("wupt", [D, INTER], F32, kind="ExternalInput").ap()
    wgt = nc.dram_tensor("wgt", [D, INTER], F32, kind="ExternalInput").ap()
    wdnt = nc.dram_tensor("wdnt", [INTER, D], F32, kind="ExternalInput").ap()
    suug_d = nc.dram_tensor("suug_d", [128, KT], F32, kind="ExternalInput").ap()
    sudp_d = nc.dram_tensor("sudp_d", [128, NT], F32, kind="ExternalInput").ap()
    wscup_d = nc.dram_tensor("wscup_d", [128, NT], F32, kind="ExternalInput").ap()
    wscg_d = nc.dram_tensor("wscg_d", [128, NT], F32, kind="ExternalInput").ap()
    wscdp_d = nc.dram_tensor("wscdp_d", [128, KT], F32, kind="ExternalInput").ap()
    h128_d = nc.dram_tensor("h128_d", [128, 128], BF16, kind="ExternalInput").ap()
    sb1_d = nc.dram_tensor("sb1_d", [128, 128], BF16, kind="ExternalInput").ap()
    sb2_d = nc.dram_tensor("sb2_d", [128, 128], BF16, kind="ExternalInput").ap()
    identf = nc.dram_tensor("identf", [128, 128], F32, kind="ExternalInput").ap()
    cpool32 = nc.dram_tensor("cpool32", [128, 64], F32, kind="ExternalInput").ap()
    cpool16 = nc.dram_tensor("cpool16", [128, 64], BF16, kind="ExternalInput").ap()

    out_slab = nc.dram_tensor("out_slab", [T, D], F32, kind="ExternalOutput").ap()

    za_dram = nc.dram_tensor("za_dram", [128, KT * T], BF16, kind="Internal").ap()
    za2_dram = nc.dram_tensor("za2_dram", [128, NT * T], BF16, kind="Internal").ap()

    DR = {
        "x1_in": x1_in,
        "xn2_in": xn2_in,
        "s1_in": s1_in,
        "wupt": wupt,
        "wgt": wgt,
        "wdnt": wdnt,
        "suug_d": suug_d,
        "sudp_d": sudp_d,
        "wscup_d": wscup_d,
        "wscg_d": wscg_d,
        "wscdp_d": wscdp_d,
        "h128_d": h128_d,
        "sb1_d": sb1_d,
        "sb2_d": sb2_d,
        "identf": identf,
        "cpool32": cpool32,
        "cpool16": cpool16,
        "out_slab": out_slab,
        "za_dram": za_dram,
        "za2_dram": za2_dram,
    }
    with stack:
        prog = Prog(nc, stack)
        C = Consts(nc, stack)
        A = Arena(nc, stack, 206 * 1024)
        emit_p4(nc, stack, prog, C, A, DR)
        block = stack.enter_context(nc.Block())
        prog.replay_into(block)
        consts = C.host_arrays()
    return nc, consts

def p4_inputs(inputs, consts, p2_results, p3_results):
    a32, a16 = consts
    p1_ = fwht_perm1()
    p2_ = fwht_perm2()
    wup = np.asarray(inputs["L1_ffn_W_up"], np.float32)
    wgate = np.asarray(inputs["L1_ffn_W_gate"], np.float32)
    wdn = np.asarray(inputs["L1_ffn_W_down"], np.float32)
    wsc = np.asarray(inputs["L1_ffn_Wscale_ug"], np.float32)
    base = {
        "wupt": np.ascontiguousarray(wup.T[p1_, :]),
        "wgt": np.ascontiguousarray(wgate.T[p1_, :]),
        "wdnt": np.ascontiguousarray(wdn.T[p2_, :]),
        "suug_d": _tilevec(inputs["L1_ffn_SU_ug"]),
        "sudp_d": _tilevec(inputs["L1_ffn_SU_dp"]),
        "wscup_d": _tilevec(wsc[:INTER]),
        "wscg_d": _tilevec(wsc[INTER:]),
        "wscdp_d": _tilevec(inputs["L1_ffn_Wscale_dp"]),
        "h128_d": np.ascontiguousarray(_hadamard(128).astype("bfloat16")),
        "sb1_d": np.ascontiguousarray(np.kron(np.eye(8), _hadamard(16)).astype("bfloat16")),
        "sb2_d": np.ascontiguousarray(np.kron(np.eye(2), _hadamard(64)).astype("bfloat16")),
        "identf": np.eye(128, dtype=np.float32),
        "cpool32": a32,
        "cpool16": a16,
    }
    per_core = []
    for c in range(NCORES):
        d = dict(base)
        d["x1_in"] = np.asarray(p2_results[c]["x1_out"])
        d["xn2_in"] = np.asarray(p3_results[c]["xn2_out"])
        d["s1_in"] = np.asarray(p3_results[c]["s1_out"])
        per_core.append(d)
    return per_core


def run_p4(inputs, p2_results, p3_results):
    nc, consts = build_p4()
    per_core = p4_inputs(inputs, consts, p2_results, p3_results)
    return run_bass_kernel_spmd(nc, per_core, core_ids=list(range(NCORES)))





def build_fused(dbg=False):
    nc = bass.Bass("TRN2", target_bir_lowering=False)
    stack = ExitStack()
    NT = INTER // 128

    def ein(name, shape, dtype):
        return nc.dram_tensor(name, shape, dtype, kind="ExternalInput").ap()

    ins = {}
    for nm, shp, dt_ in (
        ("x_slab", [T, D], F32), ("w_in", [D, D], F32), ("wg", [2, 8, 256, 128], F32),
        ("w_out", [D, D], F32), ("bias_in", [128, KT], F32), ("bias_out", [128, KT], F32),
        ("bias_g", [128, 16], F32), ("arp", [128, 8], F32), ("aip", [128, 8], F32),
        ("identf", [128, 128], F32), ("cpool32", [128, 64], F32), ("cpool16", [128, 64], BF16),
        ("mask8", [128, 8], F32),
        ("w1t", [D, INTER], F32), ("w2t", [INTER, D], F32),
        ("wslice1", [D, 1024], F32), ("wslice2", [1024, D], F32),
        ("wqkvt", [D, 3 * D], F32), ("woutt", [D, D], F32), ("bqk", [128, 32], F32),
        ("bvrow_d", [1, D], F32), ("bo", [128, KT], F32), ("identb_d", [128, 128], BF16),
        ("dmaskt_d", [128, NHEAD * 128], BF16), ("dq1_d", [128, NHEAD * 128], BF16),
        ("lamc_d", [128, NHEAD * 128], BF16), ("dkcol_d", [128, NHEAD], F32),
        ("dqfix_d", [128, NHEAD * SC], BF16), ("wlam_d", [128, NCORES * D], BF16),
        ("wupt", [D, INTER], F32), ("wgt", [D, INTER], F32), ("wdnt", [INTER, D], F32),
        ("suug_d", [128, KT], F32), ("sudp_d", [128, NT], F32), ("wscup_d", [128, NT], F32),
        ("wscg_d", [128, NT], F32), ("wscdp_d", [128, KT], F32),
        ("h128_d", [128, 128], BF16), ("sb1_d", [128, 128], BF16), ("sb2_d", [128, 128], BF16),
    ):
        ins[nm] = ein(nm, shp, dt_)

    out_slab = nc.dram_tensor("out_slab", [T, D], F32, kind="ExternalOutput").ap()

    ho = {}
    for nm, shp, dt_ in (
        ("ho_xT", [128, KT * T], F32), ("ho_xn", [128, KT * T], BF16),
        ("ho_s0", [128, KT * T], BF16), ("ho_var", [1, T], F32),
        ("ho_x1", [128, KT * T], F32), ("ho_xn2", [128, KT * T], BF16),
        ("ho_s1", [128, KT * T], BF16),
        ("hmid", [128, NT * T], BF16), ("za_dram", [128, KT * T], BF16),
        ("za2_dram", [128, NT * T], BF16),
        ("cc_in", [128, 64], F32), ("cc2_in", [1, 16], F32), ("cc2_out", [1, 16], F32),
        ("cc3_in", [128, 2 * D], BF16),
    ):
        ho[nm] = nc.dram_tensor(nm, shp, dt_, kind="Internal").ap()
    ho["cc_out"] = nc.dram_tensor("cc_out", [NCORES * 128, 64], F32, kind="Internal",
                                  addr_space="Shared").ap()
    ho["cc3_out"] = nc.dram_tensor("cc3_out", [NCORES * 128, 2 * D], BF16, kind="Internal",
                                   addr_space="Shared").ap()
    ho["ccw_in"] = nc.dram_tensor("ccw_in", [128, 32768], BF16, kind="Internal").ap()
    ho["ccw_out"] = nc.dram_tensor("ccw_out", [NCORES * 128, 32768], BF16, kind="Internal",
                                   addr_space="Shared").ap()
    ho["ho_sw"] = nc.dram_tensor("ho_sw", [1, 16], F32, kind="Internal").ap()
    ho["p0scr"] = nc.dram_tensor("p0scr", [1, 256], F32, kind="Internal").ap()

    with stack:
        prog = Prog(nc, stack)
        C = Consts(nc, stack)
        A = Arena(nc, stack, 206 * 1024)

        dr0 = dict(ins)
        dr0.update({"ccw_in": ho["ccw_in"], "ccw_out": ho["ccw_out"], "ho_sw": ho["ho_sw"],
                    "p0scr": ho["p0scr"], "cc2_in": ho["cc2_in"], "cc2_out": ho["cc2_out"]})
        p0_toks = emit_p0(nc, stack, prog, C, A, dr0)

        dr1 = dict(ins)
        dr1.update({"xT_out": ho["ho_xT"], "xn_out": ho["ho_xn"], "s0_out": ho["ho_s0"],
                    "var_out": ho["ho_var"], "cc_in": ho["cc_in"], "cc_out": ho["cc_out"],
                    "_xh_gate": p0_toks["xh_gate"]})
        emit_p1(nc, stack, prog, C, A, dr1)
        prog.set_phase_gate()

        dr2 = dict(ins)
        dr2.update({"xn_in": ho["ho_xn"], "xT_in": ho["ho_xT"], "s0_in": ho["ho_s0"],
                    "var_in": ho["ho_var"], "x1_out": ho["ho_x1"], "hmid": ho["hmid"],
                    "ccw_out": ho["ccw_out"], "ho_sw": ho["ho_sw"],
                    "cc2_in": ho["cc2_in"], "cc2_out": ho["cc2_out"],
                    "_tok_ccw": p0_toks["ccw"]})
        emit_p2(nc, stack, prog, C, A, dr2)
        prog.set_phase_gate()

        if dbg:
            o_x1 = nc.dram_tensor("o_x1", [128, KT * T], F32, kind="ExternalOutput").ap()
            o_s0 = nc.dram_tensor("o_s0", [128, KT * T], BF16, kind="ExternalOutput").ap()
            prog.dma("sync", o_x1, ho["ho_x1"], after=prog.gate, sem="d_dbg")
            prog.dma("sync", o_s0, ho["ho_s0"], after=prog.gate, sem="d_dbg")
        dr3 = dict(ins)
        dr3.update({"x1_in": ho["ho_x1"], "xn2_out": ho["ho_xn2"], "s1_out": ho["ho_s1"],
                    "cc3_in": ho["cc3_in"], "cc3_out": ho["cc3_out"]})
        emit_p3(nc, stack, prog, C, A, dr3)
        prog.set_phase_gate()

        dr4 = dict(ins)
        dr4.update({"x1_in": ho["ho_x1"], "xn2_in": ho["ho_xn2"], "s1_in": ho["ho_s1"],
                    "out_slab": out_slab, "za_dram": ho["za_dram"], "za2_dram": ho["za2_dram"]})
        if dbg:
            o_s1 = nc.dram_tensor("o_s1", [128, KT * T], BF16, kind="ExternalOutput").ap()
            prog.dma("sync", o_s1, ho["ho_s1"], after=prog.gate, sem="d_dbg")
        emit_p4(nc, stack, prog, C, A, dr4)
        if dbg:
            prog.op("sync", None, after=[prog.barrier_tok("d_dbg")])

        block = stack.enter_context(nc.Block())
        prog.replay_into(block)
        consts = C.host_arrays()
    return nc, consts


def fused_inputs(inputs, consts):
    pc1 = p1_inputs(inputs, consts)
    dummy1 = [{"xn_out": None, "xT_out": None, "s0_out": None, "var_out": None}] * NCORES
    per_core = []
    w1 = np.asarray(inputs["L0_ffn_w1"], np.float32)
    w2 = np.asarray(inputs["L0_ffn_w2"], np.float32)
    w1t_h = np.ascontiguousarray(w1.T)
    w2t_h = np.ascontiguousarray(w2.T)
    wqkv = np.asarray(inputs["L1_attn_w_qkv"], np.float32)
    wout = np.asarray(inputs["L1_attn_w_out"], np.float32)
    bqkv = np.asarray(inputs["L1_attn_b_qkv"], np.float32)
    bout = np.asarray(inputs["L1_attn_b_out"], np.float32)
    p1_ = fwht_perm1()
    p2_ = fwht_perm2()
    wup = np.asarray(inputs["L1_ffn_W_up"], np.float32)
    wgate = np.asarray(inputs["L1_ffn_W_gate"], np.float32)
    wdn = np.asarray(inputs["L1_ffn_W_down"], np.float32)
    wsc = np.asarray(inputs["L1_ffn_Wscale_ug"], np.float32)
    ident = np.eye(128, dtype=np.float32)
    shared = {
        "w1t": w1t_h, "w2t": w2t_h,
        "wqkvt": np.ascontiguousarray(wqkv.T),
        "woutt": np.ascontiguousarray(wout.T),
        "bqk": np.ascontiguousarray(bqkv[:2 * D].reshape(32, 128).T),
        "bvrow_d": np.ascontiguousarray(bqkv[2 * D:].reshape(1, D)),
        "bo": np.ascontiguousarray(bout.reshape(KT, 128).T),
        "identb_d": np.ascontiguousarray(ident.astype("bfloat16")),
        "wupt": np.ascontiguousarray(wup.T[p1_, :]),
        "wgt": np.ascontiguousarray(wgate.T[p1_, :]),
        "wdnt": np.ascontiguousarray(wdn.T[p2_, :]),
        "suug_d": _tilevec(inputs["L1_ffn_SU_ug"]),
        "sudp_d": _tilevec(inputs["L1_ffn_SU_dp"]),
        "wscup_d": _tilevec(wsc[:INTER]),
        "wscg_d": _tilevec(wsc[INTER:]),
        "wscdp_d": _tilevec(inputs["L1_ffn_Wscale_dp"]),
        "h128_d": np.ascontiguousarray(_hadamard(128).astype("bfloat16")),
        "sb1_d": np.ascontiguousarray(np.kron(np.eye(8), _hadamard(16)).astype("bfloat16")),
        "sb2_d": np.ascontiguousarray(np.kron(np.eye(2), _hadamard(64)).astype("bfloat16")),
    }
    for c in range(NCORES):
        d = dict(pc1[c])
        d.update(shared)
        d.update(p3_consts(c))
        d["wslice1"] = np.ascontiguousarray(w1t_h[:, 1024 * c:1024 * (c + 1)])
        d["wslice2"] = np.ascontiguousarray(w2t_h[1024 * c:1024 * (c + 1), :])
        per_core.append(d)
    return per_core


def kernel(**inputs):
    nc, consts = build_fused()
    per_core = fused_inputs(inputs, consts)
    res = run_bass_kernel_spmd(nc, per_core, core_ids=list(range(NCORES)))
    out = np.zeros((B, S, D), np.float32)
    for c in range(NCORES):
        slab = np.asarray(res.results[c]["out_slab"], np.float32)
        out[0, c * SC:(c + 1) * SC] = slab[:SC]
        out[1, c * SC:(c + 1) * SC] = slab[SC:]
    return out


def emit_p0(nc, stack, prog, C, A, DR):
    wslice1 = DR["wslice1"]
    wslice2 = DR["wslice2"]
    ccw_in = DR["ccw_in"]
    ccw_out = DR["ccw_out"]
    cc2_in = DR["cc2_in"]
    cc2_out = DR["cc2_out"]
    ho_sw = DR["ho_sw"]
    scr_d = DR["p0scr"]

    B0 = 54 * 1024
    wf = [A.at(B0 + i * 8192, [128, D], F32, f"p0wf{i}") for i in range(2)]
    wq8 = A.at(B0 + 16 * 1024, [128, D], I8, "p0wq8")
    wcl = A.at(B0 + 20 * 1024, [128, 1024], F32, "p0wcl")
    swacc = A.at(B0 + 24 * 1024, [128, 2], F32, "p0swacc")
    redc = A.at(B0 + 24 * 1024 + 64, [128, 1], F32, "p0redc")
    swrow = A.at(B0 + 24 * 1024 + 128, [1, 256], F32, "p0swrow")
    swb = A.at(B0 + 25 * 1024, [128, 2], F32, "p0swb")

    tok_z = prog.op("vector", lambda e: e.memset(swacc[:], 0.0), inc="v")
    ld = [None, None]
    prevred = tok_z
    for k in range(KT):
        b = k % 2
        ld[b] = prog.dma("sync", wf[b][:, 0:1024], wslice1[128 * k:128 * (k + 1), :],
                         after=[ld[b]], sem=f"d_p0{b}")
        rk = prog.op("vector", lambda e, b=b: e.reduce_sum(redc[:], wf[b][:, 0:1024], axis=AX.X,
                                                           apply_absolute_value=True),
                     after=[ld[b], prevred], inc="v", force_wait=True)
        prevred = prog.op("vector", lambda e: e.tensor_add(swacc[:, 0:1], swacc[:, 0:1], redc[:]),
                          after=[rk], inc="v", force_wait=True)
        ld[b] = prevred
    for k in range(8):
        b = k % 2
        ld[b] = prog.dma("sync", wf[b][:], wslice2[128 * k:128 * (k + 1), :],
                         after=[ld[b]], sem=f"d_p0{b}")
        rk = prog.op("vector", lambda e, b=b: e.reduce_sum(redc[:], wf[b][:], axis=AX.X,
                                                           apply_absolute_value=True),
                     after=[ld[b], prevred], inc="v", force_wait=True)
        prevred = prog.op("vector", lambda e: e.tensor_add(swacc[:, 1:2], swacc[:, 1:2], redc[:]),
                          after=[rk], inc="v", force_wait=True)
        ld[b] = prevred
    t1 = prog.dma("sync", scr_d[0:1, 0:256], swacc[:], after=[prevred], sem="d_p0s")
    t2 = prog.dma("sync", swrow[:], scr_d[0:1, 0:256], after=[t1], sem="d_p0s")
    r1 = prog.op("vector", lambda e: e.reduce_sum(swrow[0:1, 0:1], swrow[0:1, 0:256:2], axis=AX.X),
                 after=[t2], inc="v", force_wait=True)
    r2 = prog.op("vector", lambda e: e.reduce_sum(swrow[0:1, 1:2], swrow[0:1, 1:256:2], axis=AX.X),
                 after=[r1], inc="v", force_wait=True)
    t3 = prog.dma("sync", cc2_in[0:1, 0:16], swrow[0:1, 0:16], after=[r2], sem="d_p0s")
    tok_cc = prog.op("gpsimd", lambda e: e.collective_compute(
        "AllReduce", ALU.add, replica_groups=[list(range(NCORES))],
        ins=[cc2_in.opt()], outs=[cc2_out.opt()]), after=[t3], inc="cc")
    t4 = prog.dma("sync", swrow[0:1, 0:16], cc2_out, after=[tok_cc], sem="d_p0s")
    u1 = prog.op("vector", lambda e: e.tensor_scalar(swrow[0:1, 2:4], swrow[0:1, 0:2],
                                                     1.0 / (D * INTER), None, ALU.mult),
                 after=[t4], inc="v", force_wait=True)
    u2 = prog.op("vector", lambda e: e.tensor_scalar(swrow[0:1, 2:4], swrow[0:1, 2:4], 1e-5, None,
                                                     ALU.max), after=[u1], inc="v", force_wait=True)
    u3 = prog.op("vector", lambda e: e.reciprocal(swrow[0:1, 4:6], swrow[0:1, 2:4]),
                 after=[u2], inc="v", force_wait=True)
    tok_swst = prog.dma("sync", ho_sw, swrow[0:1, 0:16], after=[u3], sem="d_p0s")
    tok_swb = prog.dma("sync", swb[:], ho_sw[0:1, 4:6].broadcast_to((128, 2)),
                       after=[tok_swst], sem="d_p0s")
    ld = [None, None]
    st_tok = None
    for k in range(KT):
        b = k % 2
        ld[b] = prog.dma("sync", wf[b][:, 0:1024], wslice1[128 * k:128 * (k + 1), :],
                         after=[ld[b]], sem=f"d_p0{b}")
        qa = prog.op("vector", lambda e, b=b: e.tensor_scalar(wcl[:], wf[b][:, 0:1024], swb[:, 0:1],
                                                              1.49, ALU.mult, ALU.min),
                     after=[ld[b], tok_swb, st_tok], inc="v")
        qb = prog.op("vector", lambda e: e.tensor_scalar(wq8[:, 0:1024], wcl[:], -1.49, None, ALU.max),
                     after=[qa], inc="v", force_wait=True)
        st_tok = prog.dma("gpsimd", ccw_in[:, 1024 * k:1024 * (k + 1)], wq8[:, 0:1024],
                          after=[qb], sem="d_p0q")
        ld[b] = st_tok
    for k in range(8):
        b = k % 2
        ld[b] = prog.dma("sync", wf[b][:], wslice2[128 * k:128 * (k + 1), :],
                         after=[ld[b]], sem=f"d_p0{b}")
        qa = prog.op("vector", lambda e, b=b: e.tensor_scalar(wcl[:], wf[b][:, 0:1024], swb[:, 1:2],
                                                              1.49, ALU.mult, ALU.min),
                     after=[ld[b], tok_swb, st_tok], inc="v")
        qb = prog.op("vector", lambda e: e.tensor_scalar(wq8[:, 0:1024], wcl[:], -1.49, None, ALU.max),
                     after=[qa], inc="v", force_wait=True)
        qa2 = prog.op("vector", lambda e, b=b: e.tensor_scalar(wcl[:], wf[b][:, 1024:2048], swb[:, 1:2],
                                                               1.49, ALU.mult, ALU.min),
                      after=[qb], inc="v", force_wait=True)
        qb2 = prog.op("vector", lambda e: e.tensor_scalar(wq8[:, 1024:2048], wcl[:], -1.49, None, ALU.max),
                      after=[qa2], inc="v", force_wait=True)
        st_tok = prog.dma("gpsimd", ccw_in[:, 16384 + 2048 * k:16384 + 2048 * (k + 1)], wq8[:],
                          after=[qb2], sem="d_p0q")
        ld[b] = st_tok
    tok_ccw = prog.op("gpsimd", lambda e: e.collective_compute(
        "AllGather", ALU.bypass, replica_groups=[list(range(NCORES))],
        ins=[ccw_in.opt()], outs=[ccw_out.opt()]),
        after=[st_tok, prog.barrier_tok("d_p0q")], inc="cc")
    return {"ccw": tok_ccw,
            "xh_gate": [prog.barrier_tok("d_p0q"), prog.barrier_tok("v")]}
